# version 20
# baseline (speedup 1.0000x reference)
import numpy as np

import concourse.bass as bass
import concourse.mybir as mybir
from concourse.bass_utils import run_bass_kernel_spmd

F32 = mybir.dt.float32
BF16 = mybir.dt.bfloat16
AF = mybir.ActivationFunctionType
OP = mybir.AluOpType
AX = mybir.AxisListType.X

NCORES = 8
D = 1024
KD = 8
MC = 8
INTER = 4096
ICN = 32
LN_EPS = 1e-5


def _pstep(t):
    return t.ap().ap[0][0]


class Builder:

    ENGINES = ("sync", "tensor", "scalar", "vector", "gpsimd")

    def __init__(self, nc):
        self.nc = nc
        self.thunks = {e: [] for e in self.ENGINES}
        self.sems = {}
        self.counts = {}

    def sem(self, name):
        handle = self.nc.semaphore(name).__enter__()
        self.sems[name] = handle
        self.counts[name] = 0
        return handle

    def emit(self, eng, fn, inc=None):
        sig = None
        if inc is not None:
            semname, k = inc
            self.counts[semname] += k
            sig = (semname, self.counts[semname])
            handle = self.sems[semname]

            def wrapped(e, fn=fn, handle=handle, k=k):
                fn(e).then_inc(handle, k)
            self.thunks[eng].append(wrapped)
        else:
            self.thunks[eng].append(fn)
        return sig

    def wait(self, eng, sig):
        if sig is None:
            return
        semname, val = sig
        handle = self.sems[semname]
        self.thunks[eng].append(lambda e, h=handle, v=val: e.wait_ge(h, v))

    def run(self):
        nc = self.nc
        with nc.Block() as block:
            @block.sync
            def _(eng):
                for t in self.thunks["sync"]:
                    t(eng)

            @block.tensor
            def _(eng):
                for t in self.thunks["tensor"]:
                    t(eng)

            @block.scalar
            def _(eng):
                for t in self.thunks["scalar"]:
                    t(eng)

            @block.vector
            def _(eng):
                for t in self.thunks["vector"]:
                    t(eng)

            @block.gpsimd
            def _(eng):
                for t in self.thunks["gpsimd"]:
                    t(eng)


class BankPool:

    def __init__(self, B, banks):
        self.B = B
        self.banks = banks
        self.free_at = [None] * len(banks)
        self.rr = 0

    def acquire(self):
        i = self.rr % len(self.banks)
        self.rr += 1
        if self.free_at[i] is not None:
            self.B.wait("tensor", self.free_at[i])
            self.free_at[i] = None
        return i, self.banks[i]

    def release(self, i, sig):
        self.free_at[i] = sig


def build_nc(T_CORE=2048):
    TC = T_CORE // 128
    HALF = T_CORE // 2
    TBH = HALF // 512
    NU = 2 * MC

    nc = bass.Bass("TRN2", target_bir_lowering=False)

    x_d = nc.dram_tensor("x", [T_CORE, D], F32, kind="ExternalInput").ap()
    proj_d = {}
    for P in ("A", "dt", "Bp", "C"):
        proj_d[(P, "wmuT")] = nc.dram_tensor(f"{P}_wmuT_my", [128, D], F32,
                                             kind="ExternalInput").ap()
        proj_d[(P, "wlvT")] = nc.dram_tensor(f"{P}_wlvT_my", [128, D], F32,
                                             kind="ExternalInput").ap()
        proj_d[(P, "epsT")] = nc.dram_tensor(f"{P}_epsT_my", [128, D], BF16,
                                             kind="ExternalInput").ap()
        for part in ("bmu8", "blv8", "epsb8"):
            proj_d[(P, part)] = nc.dram_tensor(f"{P}_{part}", [128, 8], F32,
                                               kind="ExternalInput").ap()
    wag_in = nc.dram_tensor("wag_in", [128, 3072], BF16)
    wag_out = nc.dram_tensor("wag_out", [8, 128, 3072], BF16, addr_space="Shared")
    w1_d = nc.dram_tensor("W1", [D, INTER], F32, kind="ExternalInput").ap()
    b1_d = nc.dram_tensor("b1_32", [128, ICN], F32, kind="ExternalInput").ap()
    w2_d = nc.dram_tensor("W2", [INTER, D], F32, kind="ExternalInput").ap()
    b2_d = nc.dram_tensor("b2row", [1, D], F32, kind="ExternalInput").ap()
    eye_d = nc.dram_tensor("eye", [128, 128], F32, kind="ExternalInput").ap()
    msend_d = nc.dram_tensor("msend", [128, 8], F32, kind="ExternalInput").ap()
    rread_d = nc.dram_tensor("rread", [128, 8], F32, kind="ExternalInput").ap()
    out_d = nc.dram_tensor("out", [T_CORE, D], F32, kind="ExternalOutput").ap()

    hsp = nc.dram_tensor("hsp", [ICN, 128, T_CORE], BF16)
    x1sp = nc.dram_tensor("x1sp", [T_CORE, D], F32)
    cc_in = [nc.dram_tensor(f"cc_in{m}", [128, 8], F32) for m in range(MC)]
    cc_out = [nc.dram_tensor(f"cc_out{m}", [128, 8], F32) for m in range(MC)]

    from contextlib import ExitStack
    stack = ExitStack()
    en = stack.enter_context

    B = Builder(nc)
    for s in ("SQ", "SQB", "SQC", "GQ", "GQA", "SCQ", "TE", "SC", "VE", "CC",
              "FQ", "FQB", "FQC", "FQD", "W2Q", "W2QB", "RQ", "RQB"):
        B.sem(s)

    def fence(eng, semname):
        B.wait(eng, (semname, B.counts[semname]))

    XB = en(nc.sbuf_tensor("XB", [128, KD * T_CORE], BF16))
    EYE = en(nc.sbuf_tensor("EYE", [128, 128], F32))
    SM = en(nc.sbuf_tensor("SM", [128, 8], F32))
    SR = en(nc.sbuf_tensor("SR", [128, 8], F32))
    BRAW = en(nc.sbuf_tensor("BRAW", [128, 8 * 12], F32))
    BS = en(nc.sbuf_tensor("BS", [128, 8], F32))
    BB = en(nc.sbuf_tensor("BB", [128, 8], F32))
    BC4 = en(nc.sbuf_tensor("BC4", [128, 8], F32))
    BT = en(nc.sbuf_tensor("BT", [128, 8 * 3], F32))
    B1T = en(nc.sbuf_tensor("B1T", [128, ICN], F32))
    CTB = en(nc.sbuf_tensor("CTB", [128, 8 * MC], F32))
    ARB = en(nc.sbuf_tensor("ARB", [128, 8 * MC], F32))
    INI = en(nc.sbuf_tensor("INI", [128, MC], F32))
    ZZ = en(nc.sbuf_tensor("ZZ", [128, 8], F32))
    HCC = en(nc.sbuf_tensor("HCC", [128, 1], F32))
    LN4C = en(nc.sbuf_tensor("LN4C", [128, 1], F32))
    SUMS = en(nc.sbuf_tensor("SUMS", [128, TC], F32))
    SQS = en(nc.sbuf_tensor("SQS", [128, TC], F32))
    NEGM = en(nc.sbuf_tensor("NEGM", [128, TC], F32))
    VARP = en(nc.sbuf_tensor("VARP", [128, TC], F32))
    RSTD = en(nc.sbuf_tensor("RSTD", [128, TC], F32))
    NMRS = en(nc.sbuf_tensor("NMRS", [128, TC], F32))

    banks = [en(nc.psum_tensor(f"pb{i}", [128, 512], F32)) for i in range(8)]
    pool = BankPool(B, banks)

    stackAB = ExitStack()
    enB = stackAB.enter_context
    WS = enB(nc.sbuf_tensor("WS", [128, KD * 1024], BF16))
    WB = enB(nc.sbuf_tensor("WB", [128, KD * 1024], BF16))
    WC = enB(nc.sbuf_tensor("WC", [128, KD * 1024], BF16))
    RT = enB(nc.sbuf_tensor("RT", [128, TC * 1024], F32))
    EBUF = enB(nc.sbuf_tensor("EBUF", [128, 3 * HALF], F32))
    BXBUF = enB(nc.sbuf_tensor("BXBUF", [128, 4 * HALF], F32))
    HBUF = enB(nc.sbuf_tensor("HBUF", [128, 3 * HALF], F32))
    XCOL = enB(nc.sbuf_tensor("XCOL", [128, 1024], F32))
    WST = enB(nc.sbuf_tensor("WST", [128, 2 * 1024], F32))
    WSTE = enB(nc.sbuf_tensor("WSTE", [128, 1024], BF16))
    UEX = enB(nc.sbuf_tensor("UEX", [128, 1024], F32))

    pR = _pstep(RT)
    pH = _pstep(HBUF)

    x_sigs = []

    def emit_x_loads():
        for tc in range(TC):
            sig = B.emit("gpsimd", lambda e, tc=tc: e.dma_start(
                out=RT[:, tc * 1024:(tc + 1) * 1024],
                in_=x_d[tc * 128:(tc + 1) * 128, :]), inc=("GQ", 16))
            x_sigs.append(sig)
            if tc % 4 == 3:
                fence("gpsimd", "GQ")

    tiny = [(EYE[:, :], eye_d), (SM[:, :], msend_d), (SR[:, :], rread_d),
            (B1T[:, :], b1_d)]
    for i, (P, part) in enumerate([(P, p) for P in ("A", "dt", "Bp", "C")
                                   for p in ("bmu8", "blv8", "epsb8")]):
        tiny.append((BRAW[:, i * 8:(i + 1) * 8], proj_d[(P, part)]))
    tiny_sig = None
    for dst, src in tiny:
        tiny_sig = B.emit("gpsimd", lambda e, dst=dst, src=src: e.dma_start(
            out=dst, in_=src[:, :]), inc=("GQ", 16))
    fence("gpsimd", "GQ")


    B.wait("tensor", tiny_sig)
    for w in range(36):
        bi, pb = pool.acquire()
        for j in range(4):
            fn = lambda e, pb=pb, j=j: e.transpose(
                pb[:, j * 128:(j + 1) * 128], EYE[:, :], EYE[:, :])
            if j == 3:
                wsig = B.emit("tensor", fn, inc=("TE", 1))
            else:
                B.emit("tensor", fn)
        pool.release(bi, wsig)

    ln4_sig = B.emit("vector", lambda e: e.memset(LN4C[:, :], float(np.log(4.0))),
                     inc=("VE", 1))

    prep_dma = {}
    last_prep = [None]
    for P in ("A", "dt", "Bp", "C"):
        B.wait("sync", last_prep[0])
        for j, part in enumerate(("wmuT", "wlvT")):
            B.emit("sync", lambda e, P=P, part=part, j=j: e.dma_start(
                out=WST[:, j * 1024:(j + 1) * 1024], in_=proj_d[(P, part)][:, :]),
                inc=("SQ", 16))
        sig = B.emit("sync", lambda e, P=P: e.dma_start(
            out=WSTE[:, :], in_=proj_d[(P, "epsT")][:, :]), inc=("SQ", 16))
        prep_dma[P] = sig
        fence("sync", "SQ")

        mode = {"A": "hold", "dt": "ws", "Bp": "wb", "C": "wc4"}[P]
        B.wait("scalar", prep_dma[P])
        B.wait("scalar", last_prep[0])
        if P == "C":
            B.wait("scalar", ln4_sig)
        for half in range(2):
            fn = lambda e, half=half, P=P: e.activation(
                out=UEX[:, half * 512:(half + 1) * 512],
                in_=WST[:, 1024 + half * 512: 1024 + (half + 1) * 512],
                func=AF.Exp, scale=0.5,
                **({"bias": LN4C[:, 0:1]} if P == "C" else {}))
            s1 = B.emit("scalar", fn, inc=("SC", 1))
        B.wait("vector", s1)
        for half in range(2):
            m1 = B.emit("vector", lambda e, half=half: e.tensor_mul(
                UEX[:, half * 512:(half + 1) * 512],
                UEX[:, half * 512:(half + 1) * 512],
                WSTE[:, half * 512:(half + 1) * 512]), inc=("VE", 1))
        B.wait("vector", m1)
        for half in range(2):
            hs = half * 512
            if mode == "hold":
                sig = B.emit("vector", lambda e, hs=hs: e.tensor_add(
                    XCOL[:, hs:hs + 512], UEX[:, hs:hs + 512],
                    WST[:, hs:hs + 512]), inc=("VE", 1))
            elif mode == "ws":
                a1 = B.emit("vector", lambda e, hs=hs: e.tensor_add(
                    UEX[:, hs:hs + 512], UEX[:, hs:hs + 512],
                    WST[:, hs:hs + 512]), inc=("VE", 1))
                B.wait("vector", a1)
                sig = B.emit("vector", lambda e, hs=hs: e.tensor_add(
                    WS[:, hs:hs + 512], XCOL[:, hs:hs + 512],
                    UEX[:, hs:hs + 512]), inc=("VE", 1))
            elif mode == "wb":
                sig = B.emit("vector", lambda e, hs=hs: e.tensor_add(
                    WB[:, hs:hs + 512], UEX[:, hs:hs + 512],
                    WST[:, hs:hs + 512]), inc=("VE", 1))
            else:
                sig = B.emit("vector", lambda e, hs=hs: e.scalar_tensor_tensor(
                    out=WC[:, hs:hs + 512], in0=WST[:, hs:hs + 512],
                    scalar=4.0, in1=UEX[:, hs:hs + 512], op0=OP.mult, op1=OP.add),
                    inc=("VE", 1))
        last_prep[0] = sig
        if P == "C":
            wbc_con = sig

    B.wait("gpsimd", wbc_con)
    B.emit("gpsimd", lambda e: e.dma_start(
        out=wag_in.ap()[:, 0:1024], in_=WS[:, 0:1024]), inc=("GQ", 16))
    B.emit("gpsimd", lambda e: e.dma_start(
        out=wag_in.ap()[:, 1024:2048], in_=WB[:, 0:1024]), inc=("GQ", 16))
    agd = B.emit("gpsimd", lambda e: e.dma_start(
        out=wag_in.ap()[:, 2048:3072], in_=WC[:, 0:1024]), inc=("GQ", 16))
    B.wait("gpsimd", agd)
    ag1 = B.emit("gpsimd", lambda e: e.collective_compute(
        "AllGather", OP.bypass, replica_groups=[list(range(NCORES))],
        ins=[wag_in.ap().opt()], outs=[wag_out.ap().opt()]), inc=("CC", 1))

    emit_x_loads()

    for q in range(TC // 4):
        for k in range(KD):
            bi, pb = pool.acquire()
            B.wait("tensor", x_sigs[4 * q + 3])
            if q == 0 and k == 0:
                B.wait("tensor", tiny_sig)
            for j in range(4):
                tc = 4 * q + j
                fn = lambda e, tc=tc, k=k, pb=pb, j=j: e.transpose(
                    pb[:, j * 128:(j + 1) * 128],
                    RT[:, tc * 1024 + k * 128: tc * 1024 + k * 128 + 128],
                    EYE[:, :])
                if j == 3:
                    tsig = B.emit("tensor", fn, inc=("TE", 1))
                else:
                    B.emit("tensor", fn)
            B.wait("scalar", tsig)
            esig = B.emit("scalar", lambda e, k=k, q=q, pb=pb: e.copy(
                out=XB[:, k * T_CORE + q * 512: k * T_CORE + (q + 1) * 512],
                in_=pb[:, :]), inc=("SC", 1))
            pool.release(bi, esig)
    xb_done = esig

    B.wait("sync", ag1)
    pWS, pWB, pWC = _pstep(WS), _pstep(WB), _pstep(WC)
    ws_sig_all = B.emit("sync", lambda e: e.dma_start(
        out=bass.AP(WS, 0, [[pWS, 128], [1024, 8], [1, 1024]]),
        in_=bass.AP(wag_out, 0, [[3072, 128], [128 * 3072, 8], [1, 1024]])),
        inc=("SQ", 16))
    fence("sync", "SQ")
    wb_sig_all = B.emit("sync", lambda e: e.dma_start(
        out=bass.AP(WB, 0, [[pWB, 128], [1024, 8], [1, 1024]]),
        in_=bass.AP(wag_out, 1024, [[3072, 128], [128 * 3072, 8], [1, 1024]])),
        inc=("SQ", 16))
    wc_sig_all = B.emit("sync", lambda e: e.dma_start(
        out=bass.AP(WC, 0, [[pWC, 128], [1024, 8], [1, 1024]]),
        in_=bass.AP(wag_out, 2048, [[3072, 128], [128 * 3072, 8], [1, 1024]])),
        inc=("SQB", 16))
    fence("sync", "SQ")
    fence("sync", "SQB")
    B.wait("scalar", tiny_sig)
    prev_v = None
    for i, dst in enumerate((None, None, BB, BC4)):
        B.wait("scalar", prev_v)
        s1 = B.emit("scalar", lambda e, i=i: e.activation(
            out=BT[:, 0:8], in_=BRAW[:, (3 * i + 1) * 8:(3 * i + 2) * 8],
            func=AF.Exp, scale=0.5), inc=("SC", 1))
        B.wait("vector", s1)
        tgt = BT[:, 8:16] if i == 0 else (BT[:, 16:24] if i == 1 else dst[:, :])
        v1 = B.emit("vector", lambda e, i=i, tgt=tgt: e.tensor_mul(
            tgt, BT[:, 0:8], BRAW[:, (3 * i + 2) * 8:(3 * i + 3) * 8]),
            inc=("VE", 1))
        B.wait("vector", v1)
        v2 = B.emit("vector", lambda e, i=i, tgt=tgt: e.tensor_add(
            tgt, tgt, BRAW[:, (3 * i) * 8:(3 * i + 1) * 8]), inc=("VE", 1))
        B.wait("vector", v2)
        prev_v = v2
    v = B.emit("vector", lambda e: e.tensor_add(
        BS[:, :], BT[:, 8:16], BT[:, 16:24]), inc=("VE", 1))
    B.wait("vector", v)
    bias_done = B.emit("vector", lambda e: e.tensor_scalar(
        out=BC4[:, :], in0=BC4[:, :], scalar1=4.0, scalar2=None,
        op0=OP.mult), inc=("VE", 1))

    ws_sig = {0: ws_sig_all, 1: ws_sig_all}
    wb_sig_h = {0: wb_sig_all, 1: wb_sig_all}
    wc_sig_h = {0: wc_sig_all, 1: wc_sig_all}
    prev_prep = [sig]

    head_state = {}

    def emit_head(u):
        m, h = u // 2, u % 2
        sl = u % 3
        xsl = u % 2
        bsl = u % 4
        st = {}
        for q2 in range(TBH):
            bi, pb = pool.acquire()
            for j in range(4):
                tc = (TC // 2) * h + 4 * q2 + j
                fn = lambda e, tc=tc, m=m, pb=pb, j=j: e.transpose(
                    pb[:, j * 128:(j + 1) * 128],
                    RT[:, tc * 1024 + m * 128: tc * 1024 + m * 128 + 128],
                    EYE[:, :])
                if j == 3:
                    tsig = B.emit("tensor", fn, inc=("TE", 1))
                else:
                    B.emit("tensor", fn)
            B.wait("scalar", tsig)
            if u == 0 and q2 == 0:
                B.wait("scalar", wbc_con)
            if u >= 1:
                B.wait("scalar", head_state.get(("xcol_reader", u - 1)))
            xsig = B.emit("scalar", lambda e, q2=q2, pb=pb: e.copy(
                out=XCOL[:, q2 * 512:(q2 + 1) * 512],
                in_=pb[:, :]), inc=("SC", 1))
            pool.release(bi, xsig)
            st[("xcol", q2)] = xsig
        if u == 0:
            B.wait("tensor", xb_done)
        if m % 4 == 0 and h == 0:
            B.wait("tensor", ws_sig[m // 4])
        sbanks = [pool.acquire() for _ in range(TBH)]
        for k in range(KD):
            for tb in range(TBH):
                fn = lambda e, k=k, m=m, h=h, tb=tb, pb=sbanks[tb][1]: e.matmul(
                    pb[:, :],
                    WS[:, k * 1024 + m * 128: k * 1024 + m * 128 + 128],
                    XB[:, k * T_CORE + h * HALF + tb * 512:
                       k * T_CORE + h * HALF + (tb + 1) * 512],
                    start=(k == 0), stop=(k == KD - 1))
                if k == KD - 1:
                    st[("smm", tb)] = B.emit("tensor", fn, inc=("TE", 1))
                else:
                    B.emit("tensor", fn)
        for tb in range(TBH):
            bi, pb = sbanks[tb]
            B.wait("vector", st[("smm", tb)])
            if u == 0 and tb == 0:
                B.wait("vector", bias_done)
            if u >= 3:
                B.wait("vector", head_state.get(("ebuf_reader", u - 3)))
            vc = B.emit("vector", lambda e, pb=pb, m=m, sl=sl, tb=tb: e.tensor_scalar(
                out=EBUF[:, sl * HALF + tb * 512: sl * HALF + (tb + 1) * 512],
                in0=pb[:, :], scalar1=BS[:, m:m + 1], scalar2=85.0,
                op0=OP.add, op1=OP.min), inc=("VE", 1))
            pool.release(bi, vc)
            st[("e1", tb)] = vc
        for tb in range(TBH):
            B.wait("scalar", st[("e1", tb)])
            s1 = B.emit("scalar", lambda e, sl=sl, tb=tb: e.activation(
                out=EBUF[:, sl * HALF + tb * 512: sl * HALF + (tb + 1) * 512],
                in_=EBUF[:, sl * HALF + tb * 512: sl * HALF + (tb + 1) * 512],
                func=AF.Exp), inc=("SC", 1))
            st[("e1b", tb)] = s1
        B.wait("scalar", st[("e1b", TBH - 1)])
        for tb in range(TBH):
            s2 = B.emit("scalar", lambda e, sl=sl, tb=tb: e.activation(
                out=EBUF[:, sl * HALF + tb * 512: sl * HALF + (tb + 1) * 512],
                in_=EBUF[:, sl * HALF + tb * 512: sl * HALF + (tb + 1) * 512],
                func=AF.Exp, scale=-1.0), inc=("SC", 1))
        st["E"] = s2
        if m % 4 == 0 and h == 0:
            B.wait("tensor", wb_sig_h[m // 4])
        bbanks = [pool.acquire() for _ in range(TBH)]
        bsig = {}
        for k in range(KD):
            for tb in range(TBH):
                fn = lambda e, k=k, m=m, h=h, tb=tb, pb=bbanks[tb][1]: e.matmul(
                    pb[:, :],
                    WB[:, k * 1024 + m * 128: k * 1024 + m * 128 + 128],
                    XB[:, k * T_CORE + h * HALF + tb * 512:
                       k * T_CORE + h * HALF + (tb + 1) * 512],
                    start=(k == 0), stop=(k == KD - 1))
                if k == KD - 1:
                    bsig[tb] = B.emit("tensor", fn, inc=("TE", 1))
                else:
                    B.emit("tensor", fn)
        for tb in range(TBH):
            bi, pb = bbanks[tb]
            B.wait("vector", bsig[tb])
            B.wait("vector", st[("xcol", tb)])
            if u >= 4:
                B.wait("vector", head_state.get(("bx_reader", u - 4)))
            vsig = B.emit("vector", lambda e, pb=pb, m=m, bsl=bsl, tb=tb: e.scalar_tensor_tensor(
                out=BXBUF[:, bsl * HALF + tb * 512: bsl * HALF + (tb + 1) * 512],
                in0=pb[:, :], scalar=BB[:, m:m + 1],
                in1=XCOL[:, tb * 512:(tb + 1) * 512],
                op0=OP.add, op1=OP.mult), inc=("VE", 1))
            pool.release(bi, vsig)
        head_state[("xcol_reader", u)] = vsig
        B.wait("vector", st["E"])
        B.wait("vector", vsig)
        if h == 0:
            init = 0.0
        else:
            B.wait("vector", head_state[u - 1]["scan1"])
            init = bass.AP(HBUF, ((u - 1) % 3) * HALF + HALF - 1, [[pH, 128], [1, 1]])
        s1sig = B.emit("vector", lambda e, sl=sl, bsl=bsl, init=init: e.tensor_tensor_scan(
            out=HBUF[:, sl * HALF:(sl + 1) * HALF],
            data0=EBUF[:, sl * HALF:(sl + 1) * HALF],
            data1=BXBUF[:, bsl * HALF:(bsl + 1) * HALF],
            initial=init, op0=OP.mult, op1=OP.add), inc=("VE", 1))
        st["scan1"] = s1sig
        if h == 1:
            B.wait("vector", s1sig)
            bc = bass.AP(HBUF, sl * HALF + HALF - 1, [[pH, 128], [0, 8]])
            ctsig = B.emit("vector", lambda e, m=m, bc=bc: e.tensor_tensor(
                out=CTB[:, m * 8:(m + 1) * 8], in0=SM[:, :], in1=bc,
                op=OP.mult), inc=("VE", 1))
            B.wait("gpsimd", ctsig)
            d1 = B.emit("gpsimd", lambda e, m=m: e.dma_start(
                out=cc_in[m].ap(), in_=CTB[:, m * 8:(m + 1) * 8]), inc=("GQ", 16))
            B.wait("gpsimd", d1)
            ccsig = B.emit("gpsimd", lambda e, m=m: e.collective_compute(
                "AllReduce", OP.add,
                replica_groups=[list(range(NCORES))],
                ins=[cc_in[m].ap().opt()],
                outs=[cc_out[m].ap().opt()]), inc=("CC", 1))
            B.wait("gpsimd", ccsig)
            arsig = B.emit("gpsimd", lambda e, m=m: e.dma_start(
                out=ARB[:, m * 8:(m + 1) * 8], in_=cc_out[m].ap()), inc=("GQA", 16))
            st["ar"] = arsig
        head_state[u] = st
        return st

    def emit_tail_a(u):
        m, h = u // 2, u % 2
        sl = u % 3
        bsl = u % 4
        if h == 0:
            arsig = head_state[2 * m + 1]["ar"]
            B.wait("vector", arsig)
            z1 = B.emit("vector", lambda e, m=m: e.tensor_mul(
                ZZ[:, :], ARB[:, m * 8:(m + 1) * 8], SR[:, :]), inc=("VE", 1))
            B.wait("vector", z1)
            z2 = B.emit("vector", lambda e, m=m: e.reduce_sum(
                out=INI[:, m:m + 1], in_=ZZ[:, :], axis=AX), inc=("VE", 1))
            B.wait("vector", z2)
            init = bass.AP(INI, m, [[_pstep(INI), 128], [1, 1]])
        else:
            B.wait("vector", head_state[("hcc", m)])
            init = bass.AP(HCC, 0, [[_pstep(HCC), 128], [1, 1]])
        s2sig = B.emit("vector", lambda e, sl=sl, bsl=bsl, init=init: e.tensor_tensor_scan(
            out=HBUF[:, sl * HALF:(sl + 1) * HALF],
            data0=EBUF[:, sl * HALF:(sl + 1) * HALF],
            data1=BXBUF[:, bsl * HALF:(bsl + 1) * HALF],
            initial=init, op0=OP.mult, op1=OP.add), inc=("VE", 1))
        head_state[("ebuf_reader", u)] = s2sig
        if h == 0:
            B.wait("vector", s2sig)
            hcsig = B.emit("vector", lambda e, sl=sl: e.tensor_copy(
                out=HCC[:, 0:1],
                in_=HBUF[:, sl * HALF + HALF - 1: sl * HALF + HALF]), inc=("VE", 1))
            head_state[("hcc", m)] = hcsig
        if m % 4 == 0 and h == 0:
            B.wait("tensor", wc_sig_h[m // 4])
        cbanks = [pool.acquire() for _ in range(TBH)]
        csig = {}
        for k in range(KD):
            for tb in range(TBH):
                fn = lambda e, k=k, m=m, h=h, tb=tb, pb=cbanks[tb][1]: e.matmul(
                    pb[:, :],
                    WC[:, k * 1024 + m * 128: k * 1024 + m * 128 + 128],
                    XB[:, k * T_CORE + h * HALF + tb * 512:
                       k * T_CORE + h * HALF + (tb + 1) * 512],
                    start=(k == 0), stop=(k == KD - 1))
                if k == KD - 1:
                    csig[tb] = B.emit("tensor", fn, inc=("TE", 1))
                else:
                    B.emit("tensor", fn)
        head_state[("cmm", u)] = csig[TBH - 1]
        ysigs = []
        for tb in range(TBH):
            bi, pb = cbanks[tb]
            B.wait("vector", csig[tb])
            B.wait("vector", s2sig)
            ysig = B.emit("vector", lambda e, pb=pb, m=m, sl=sl, bsl=bsl, tb=tb: e.scalar_tensor_tensor(
                out=BXBUF[:, bsl * HALF + tb * 512: bsl * HALF + (tb + 1) * 512],
                in0=pb[:, :], scalar=BC4[:, m:m + 1],
                in1=HBUF[:, sl * HALF + tb * 512: sl * HALF + (tb + 1) * 512],
                op0=OP.add, op1=OP.mult), inc=("VE", 1))
            pool.release(bi, ysig)
            ysigs.append(ysig)
        head_state[("ysigs", u)] = ysigs

    def emit_tail_b(u):
        m, h = u // 2, u % 2
        bsl = u % 4
        ysigs = head_state[("ysigs", u)]
        for q2 in range(TBH):
            B.wait("tensor", ysigs[q2])
            bi, pb = pool.acquire()
            for j in range(4):
                fn = lambda e, bsl=bsl, q2=q2, j=j, pb=pb: e.transpose(
                    pb[:, j * 128:(j + 1) * 128],
                    BXBUF[:, bsl * HALF + q2 * 512 + j * 128:
                          bsl * HALF + q2 * 512 + (j + 1) * 128],
                    EYE[:, :])
                if j == 3:
                    tsig = B.emit("tensor", fn, inc=("TE", 1))
                else:
                    B.emit("tensor", fn)
            B.wait("vector", tsig)
            tc0 = (TC // 2) * h + 4 * q2
            rt_ap = bass.AP(RT, tc0 * 1024 + m * 128, [[pR, 128], [1024, 4], [1, 128]])
            pb_ap = bass.AP(pb, 0, [[_pstep(pb), 128], [128, 4], [1, 128]])
            rsig = B.emit("vector", lambda e, rt_ap=rt_ap, pb_ap=pb_ap: e.scalar_tensor_tensor(
                out=rt_ap, in0=pb_ap, scalar=1.0, in1=rt_ap,
                op0=OP.mult, op1=OP.add), inc=("VE", 1))
            pool.release(bi, rsig)
        head_state[("bx_reader", u)] = rsig
        head_state[("resid", u)] = rsig

    for s in range(NU + 3):
        if s < NU:
            emit_head(s)
        if 2 <= s < NU + 2:
            emit_tail_a(s - 2)
        if s >= 3:
            emit_tail_b(s - 3)

    B.wait("vector", head_state[("resid", NU - 1)])
    B.wait("vector", head_state[("xcol_reader", NU - 1)])
    for tc in range(TC):
        B.emit("vector", lambda e, tc=tc: e.bn_stats(
            out=XCOL[:, tc * 12: tc * 12 + 6],
            in_=RT[:, tc * 1024: tc * 1024 + 512]))
        s2 = B.emit("vector", lambda e, tc=tc: e.bn_stats(
            out=XCOL[:, tc * 12 + 6: tc * 12 + 12],
            in_=RT[:, tc * 1024 + 512: tc * 1024 + 1024]),
            inc=("VE", 1) if tc == TC - 1 else None)
    B.wait("vector", s2)
    for tc in range(TC):
        bnsig = B.emit("vector", lambda e, tc=tc: e.bn_aggr(
            out=XCOL[:, 192 + 2 * tc: 192 + 2 * tc + 2],
            in_=XCOL[:, tc * 12: tc * 12 + 12]),
            inc=("VE", 1) if tc == TC - 1 else None)
    B.wait("vector", bnsig)
    pX = _pstep(XCOL)
    mean_ap = bass.AP(XCOL, 192, [[pX, 128], [2, TC]])
    var_ap = bass.AP(XCOL, 193, [[pX, 128], [2, TC]])
    negm_sig = B.emit("vector", lambda e: e.tensor_scalar(
        out=NEGM[:, :], in0=mean_ap, scalar1=-1.0, scalar2=None,
        op0=OP.mult), inc=("VE", 1))
    B.wait("vector", negm_sig)
    v1 = B.emit("vector", lambda e: e.tensor_scalar(
        out=VARP[:, :], in0=var_ap, scalar1=1.0, scalar2=LN_EPS,
        op0=OP.mult, op1=OP.add), inc=("VE", 1))
    B.wait("scalar", v1)
    sq1 = B.emit("scalar", lambda e: e.activation(
        out=RSTD[:, :], in_=VARP[:, :], func=AF.Sqrt), inc=("SC", 1))
    B.wait("vector", sq1)
    v2 = B.emit("vector", lambda e: e.reciprocal(
        out=RSTD[:, :], in_=RSTD[:, :]), inc=("VE", 1))
    B.wait("vector", v2)
    v3 = B.emit("vector", lambda e: e.tensor_mul(
        NMRS[:, :], NEGM[:, :], RSTD[:, :]), inc=("VE", 1))
    B.wait("scalar", v3)
    ln1_sigs = []
    for tc in range(TC):
        s = B.emit("scalar", lambda e, tc=tc: e.activation(
            out=RT[:, tc * 1024:(tc + 1) * 1024],
            in_=RT[:, tc * 1024:(tc + 1) * 1024],
            func=AF.Identity, scale=RSTD[:, tc:tc + 1],
            bias=NMRS[:, tc:tc + 1]), inc=("SC", 1))
        ln1_sigs.append(s)
    spill_sigs = []
    for tc in range(TC):
        B.wait("gpsimd", ln1_sigs[tc])
        s = B.emit("gpsimd", lambda e, tc=tc: e.dma_start(
            out=x1sp.ap()[tc * 128:(tc + 1) * 128, :],
            in_=RT[:, tc * 1024:(tc + 1) * 1024]), inc=("GQ", 16))
        spill_sigs.append(s)
    x1t_done = None
    for q in range(TC // 4):
        for k in range(KD):
            bi, pb = pool.acquire()
            B.wait("tensor", ln1_sigs[4 * q + 3])
            for j in range(4):
                tc = 4 * q + j
                fn = lambda e, tc=tc, k=k, pb=pb, j=j: e.transpose(
                    pb[:, j * 128:(j + 1) * 128],
                    RT[:, tc * 1024 + k * 128: tc * 1024 + k * 128 + 128],
                    EYE[:, :])
                if j == 3:
                    tsig = B.emit("tensor", fn, inc=("TE", 1))
                else:
                    B.emit("tensor", fn)
            B.wait("scalar", tsig)
            esig = B.emit("scalar", lambda e, k=k, q=q, pb=pb: e.copy(
                out=XB[:, k * T_CORE + q * 512: k * T_CORE + (q + 1) * 512],
                in_=pb[:, :]), inc=("SC", 1))
            pool.release(bi, esig)
            x1t_done = esig

    stackAB.close()

    W1ST = en(nc.sbuf_tensor("W1ST", [128, 2 * 1024], F32))
    W1B = en(nc.sbuf_tensor("W1B", [128, 2 * KD * 1024], BF16))
    W2ST = en(nc.sbuf_tensor("W2ST", [128, 2 * 1024], F32))
    W2B = en(nc.sbuf_tensor("W2B", [128, ICN * 1024], BF16))
    HT1 = en(nc.sbuf_tensor("HT1", [128, 2 * T_CORE], BF16))
    HRD = en(nc.sbuf_tensor("HRD", [128, 3 * ICN * 128], BF16))
    X1RD = en(nc.sbuf_tensor("X1RD", [128, 3 * 1024], F32))
    R2 = en(nc.sbuf_tensor("R2", [128, 2 * 1024], F32))
    SQ2 = en(nc.sbuf_tensor("SQ2", [128, 1024], F32))
    B2B = en(nc.sbuf_tensor("B2B", [1, D], BF16))
    ONES1 = en(nc.sbuf_tensor("ONES1", [1, 128], BF16))
    S2H = en(nc.sbuf_tensor("S2H", [128, 2 * TC], F32))
    NEGM2 = en(nc.sbuf_tensor("NEGM2", [128, TC], F32))
    VARP2 = en(nc.sbuf_tensor("VARP2", [128, TC], F32))
    RSTD2 = en(nc.sbuf_tensor("RSTD2", [128, TC], F32))
    NMRS2 = en(nc.sbuf_tensor("NMRS2", [128, TC], F32))

    gate_c1 = head_state[("cmm", NU - 1)]
    gate_sigs = [x1t_done, spill_sigs[-1], head_state[("resid", NU - 1)],
                 ln1_sigs[-1], v3]
    B.wait("sync", gate_c1)
    b2sig = B.emit("sync", lambda e: e.dma_start(out=W2ST[0:1, 0:1024],
                                                 in_=b2_d[:, :]), inc=("SQ", 16))
    fence("sync", "SQ")
    B.wait("scalar", b2sig)
    b2b = B.emit("scalar", lambda e: e.copy(out=B2B[:, :], in_=W2ST[0:1, 0:1024]),
                 inc=("SC", 1))
    for g in gate_sigs:
        B.wait("vector", g)
    ones_sig = B.emit("vector", lambda e: e.memset(ONES1[:, :], 1.0), inc=("VE", 1))
    for g in gate_sigs:
        B.wait("scalar", g)

    w1_dma = {}
    w2_dma = {}
    w1b_sig = {}
    w2b_sig = {}
    w1b_reader = {}
    h_spill = {}
    ffn_fence = {0: None, 1: None, 2: None, 3: None}
    w2_fence = {0: None, 1: None}
    TBF = T_CORE // 512
    NQ = ICN // 8

    def w2_group(j):
        slot = j % 2
        sem = ("W2Q", "W2QB")[slot]
        if w2_fence[slot] is not None:
            B.wait("scalar", w2_fence[slot])
        sig = B.emit("scalar", lambda e, j=j, slot=slot: e.dma_start(
            out=W2ST[:, slot * 1024:slot * 1024 + 1024],
            in_=w2_d[j * 128:(j + 1) * 128, :]), inc=(sem, 16))
        w2_fence[slot] = sig
        return sig

    w1q_sig = {}
    w1q_reader = {}
    qconv = {}

    w1conv = {}

    def w1_quarter(q):
        qsl = q % 2
        sigs = []
        for k in range(KD):
            g = q * KD + k
            slot = g % 2
            sem = ("FQ", "FQB", "FQC", "FQD")[g % 4]
            if ffn_fence[g % 4] is not None:
                B.wait("sync", ffn_fence[g % 4])
            if g >= 2:
                B.wait("sync", w1conv.get(g - 2))
            dsig = B.emit("sync", lambda e, k=k, q=q, slot=slot: e.dma_start(
                out=W1ST[:, slot * 1024:(slot + 1) * 1024],
                in_=w1_d[k * 128:(k + 1) * 128, q * 1024:(q + 1) * 1024]),
                inc=(sem, 16))
            ffn_fence[g % 4] = dsig
            B.wait("vector", dsig)
            if (q, k) in qconv_war:
                B.wait("vector", qconv_war[(q, k)])
            csig = B.emit("vector", lambda e, k=k, qsl=qsl, slot=slot: e.tensor_copy(
                out=W1B[:, qsl * KD * 1024 + k * 1024: qsl * KD * 1024 + (k + 1) * 1024],
                in_=W1ST[:, slot * 1024:(slot + 1) * 1024]), inc=("VE", 1))
            w1conv[g] = csig
            sigs.append(csig)
        return sigs[-1]

    qconv_war = {}
    w2c_j = [0]
    h_spill = {}
    w1q_sig[0] = w1_quarter(0)
    for ic in range(ICN):
        q = ic // 8
        qsl = q % 2
        icr = ic % 8
        if icr == 0 and q + 1 < NQ:
            if q >= 1:
                B.wait("sync", w1q_reader[q - 1])
                for k in range(KD):
                    qconv_war[(q + 1, k)] = w1q_reader[q - 1]
            w1q_sig[q + 1] = w1_quarter(q + 1)
        if ic < ICN:
            j = ic
            if j == 0:
                B.wait("scalar", b2b)
            if j >= 2:
                B.wait("scalar", w2b_sig[j - 2])
            w2_dma[j] = w2_group(j)
        if w2c_j[0] <= ic:
            j = w2c_j[0]
            B.wait("vector", w2_dma[j])
            w2b_sig[j] = B.emit("vector", lambda e, j=j: e.tensor_copy(
                out=W2B[:, j * 1024:(j + 1) * 1024],
                in_=W2ST[:, (j % 2) * 1024:(j % 2) * 1024 + 1024]), inc=("VE", 1))
            w2c_j[0] += 1
        par = ic % 2
        fbanks = [pool.acquire() for _ in range(TBF)]
        if icr == 0:
            B.wait("tensor", w1q_sig[q])
        fsig = {}
        for k in range(KD):
            for tb in range(TBF):
                fn = lambda e, k=k, qsl=qsl, icr=icr, tb=tb, pb=fbanks[tb][1]: e.matmul(
                    pb[:, :],
                    W1B[:, qsl * KD * 1024 + k * 1024 + icr * 128:
                        qsl * KD * 1024 + k * 1024 + (icr + 1) * 128],
                    XB[:, k * T_CORE + tb * 512: k * T_CORE + (tb + 1) * 512],
                    start=(k == 0), stop=(k == KD - 1))
                if k == KD - 1:
                    fsig[tb] = B.emit("tensor", fn, inc=("TE", 1))
                else:
                    B.emit("tensor", fn)
        if icr == 7:
            w1q_reader[q] = fsig[TBF - 1]
        for tb in range(TBF):
            bi, pb = fbanks[tb]
            B.wait("scalar", fsig[tb])
            if ic >= 2 and tb == 0:
                B.wait("scalar", h_spill.get(ic - 2))
            rsig = B.emit("scalar", lambda e, pb=pb, ic=ic, par=par, tb=tb: e.activation(
                out=HT1[:, par * T_CORE + tb * 512: par * T_CORE + (tb + 1) * 512],
                in_=pb[:, :], func=AF.Relu, bias=B1T[:, ic:ic + 1], scale=1.0),
                inc=("SC", 1))
            pool.release(bi, rsig)
        B.wait("gpsimd", rsig)
        hsp_ap = bass.AP(hsp, ic * 128 * T_CORE, [[T_CORE, 128], [1, T_CORE]])
        h_spill[ic] = B.emit("gpsimd", lambda e, hsp_ap=hsp_ap, par=par: e.dma_start(
            out=hsp_ap,
            in_=HT1[:, par * T_CORE:(par + 1) * T_CORE]), inc=("GQ", 16))
        fence("gpsimd", "GQ")

    rb_sig = {}
    rb_reader = {}
    rb_fence = {0: None, 1: None}
    out_sig = {}
    p_hsp = 128 * T_CORE

    def issue_rb(tc):
        slot3 = tc % 3
        sem = "RQ" if tc % 2 == 0 else "RQB"
        if tc == 0:
            B.wait("sync", h_spill[ICN - 1])
        if rb_fence[tc % 2] is not None:
            B.wait("sync", rb_fence[tc % 2])
        if tc >= 3:
            B.wait("sync", rb_reader[tc - 3])
        src_ap = bass.AP(hsp, tc * 128, [[T_CORE, 128], [p_hsp, ICN], [1, 128]])
        dst_ap = bass.AP(HRD, slot3 * ICN * 128,
                         [[_pstep(HRD), 128], [128, ICN], [1, 128]])
        B.emit("sync", lambda e, s=src_ap, d=dst_ap: e.dma_start(out=d, in_=s),
               inc=(sem, 16))
        sig = B.emit("sync", lambda e, tc=tc, slot3=slot3: e.dma_start(
            out=X1RD[:, slot3 * 1024:(slot3 + 1) * 1024],
            in_=x1sp.ap()[tc * 128:(tc + 1) * 128, :]), inc=(sem, 16))
        rb_fence[tc % 2] = sig
        rb_sig[tc] = sig

    issue_rb(0)
    issue_rb(1)
    for tc in range(TC):
        slot3 = tc % 3
        par = tc % 2
        if tc + 2 < TC:
            issue_rb(tc + 2)
        for halfc in range(2):
            bi, pb = pool.acquire()
            B.wait("tensor", rb_sig[tc])
            if tc == 0 and halfc == 0:
                B.wait("tensor", w2b_sig[ICN - 1])
                B.wait("tensor", b2b)
                B.wait("tensor", ones_sig)
            for ic in range(ICN):
                fn = lambda e, ic=ic, slot3=slot3, halfc=halfc, pb=pb: e.matmul(
                    pb[:, :],
                    HRD[:, slot3 * ICN * 128 + ic * 128: slot3 * ICN * 128 + (ic + 1) * 128],
                    W2B[:, ic * 1024 + halfc * 512: ic * 1024 + (halfc + 1) * 512],
                    start=(ic == 0), stop=False)
                B.emit("tensor", fn)
            tsig = B.emit("tensor", lambda e, halfc=halfc, pb=pb: e.matmul(
                pb[:, :], ONES1[:, :],
                B2B[:, halfc * 512:(halfc + 1) * 512],
                start=False, stop=True), inc=("TE", 1))
            B.wait("vector", tsig)
            B.wait("vector", rb_sig[tc])
            if tc >= 2 and halfc == 0:
                B.wait("vector", out_sig.get(tc - 2))
            vsig = B.emit("vector", lambda e, pb=pb, par=par, slot3=slot3, halfc=halfc, tc=tc:
                          e.scalar_tensor_tensor(
                              out=R2[:, par * 1024 + halfc * 512:
                                     par * 1024 + (halfc + 1) * 512],
                              in0=pb[:, :], scalar=1.0,
                              in1=X1RD[:, slot3 * 1024 + halfc * 512:
                                       slot3 * 1024 + (halfc + 1) * 512],
                              op0=OP.mult, op1=OP.add,
                              accum_out=S2H[:, 2 * tc + halfc: 2 * tc + halfc + 1]),
                          inc=("VE", 1))
            pool.release(bi, vsig)
        rb_reader[tc] = vsig
        B.wait("vector", vsig)
        n1 = B.emit("vector", lambda e, tc=tc: e.tensor_add(
            NEGM2[:, tc:tc + 1], S2H[:, 2 * tc:2 * tc + 1],
            S2H[:, 2 * tc + 1:2 * tc + 2]), inc=("VE", 1))
        B.wait("vector", n1)
        n2 = B.emit("vector", lambda e, tc=tc: e.tensor_scalar(
            out=NEGM2[:, tc:tc + 1], in0=NEGM2[:, tc:tc + 1],
            scalar1=-1.0 / D, scalar2=None, op0=OP.mult), inc=("VE", 1))
        B.wait("scalar", n2)
        sq = B.emit("scalar", lambda e, tc=tc, par=par: e.activation(
            out=SQ2[:, :], in_=R2[:, par * 1024:(par + 1) * 1024],
            func=AF.Square, bias=NEGM2[:, tc:tc + 1], scale=1.0,
            accum_out=VARP2[:, tc:tc + 1]), inc=("SC", 1))
        B.wait("vector", sq)
        n3 = B.emit("vector", lambda e, tc=tc: e.tensor_scalar(
            out=VARP2[:, tc:tc + 1], in0=VARP2[:, tc:tc + 1],
            scalar1=1.0 / D, scalar2=LN_EPS, op0=OP.mult, op1=OP.add), inc=("VE", 1))
        B.wait("scalar", n3)
        n4 = B.emit("scalar", lambda e, tc=tc: e.activation(
            out=RSTD2[:, tc:tc + 1], in_=VARP2[:, tc:tc + 1],
            func=AF.Sqrt), inc=("SC", 1))
        B.wait("vector", n4)
        n5 = B.emit("vector", lambda e, tc=tc: e.reciprocal(
            out=RSTD2[:, tc:tc + 1], in_=RSTD2[:, tc:tc + 1]), inc=("VE", 1))
        B.wait("vector", n5)
        n6 = B.emit("vector", lambda e, tc=tc: e.tensor_mul(
            NMRS2[:, tc:tc + 1], NEGM2[:, tc:tc + 1], RSTD2[:, tc:tc + 1]),
            inc=("VE", 1))
        B.wait("scalar", n6)
        so = B.emit("scalar", lambda e, tc=tc, par=par: e.activation(
            out=R2[:, par * 1024:(par + 1) * 1024],
            in_=R2[:, par * 1024:(par + 1) * 1024],
            func=AF.Identity, scale=RSTD2[:, tc:tc + 1],
            bias=NMRS2[:, tc:tc + 1]), inc=("SC", 1))
        B.wait("gpsimd", so)
        out_sig[tc] = B.emit("gpsimd", lambda e, tc=tc, par=par: e.dma_start(
            out=out_d[tc * 128:(tc + 1) * 128, :],
            in_=R2[:, par * 1024:(par + 1) * 1024]), inc=("GQ", 16))
        fence("gpsimd", "GQ")

    B.wait("gpsimd", out_sig[TC - 1])

    B.run()
    stack.close()
    return nc


_NC_CACHE = {}


def _get_nc(T_CORE=2048):
    if T_CORE not in _NC_CACHE:
        _NC_CACHE[T_CORE] = build_nc(T_CORE)
    return _NC_CACHE[T_CORE]


def make_in_maps(inputs, T_CORE=2048):
    import jax

    x = np.ascontiguousarray(np.asarray(inputs["L0_attn_x"], dtype=np.float32)
                             .reshape(-1, D))
    n_tok = x.shape[0]
    assert n_tok == NCORES * T_CORE

    cpu = jax.devices("cpu")[0]
    with jax.default_device(cpu):
        key = jax.random.key(42, impl="threefry2x32")
        kA, kB, kC, kdt = jax.random.split(key, 4)
        eps = {}
        for name, kk in (("A", kA), ("Bp", kB), ("C", kC), ("dt", kdt)):
            kw, kb = jax.random.split(kk)
            eps[(name, "w")] = np.asarray(jax.random.normal(kw, (D, D), "float32"))
            eps[(name, "b")] = np.asarray(jax.random.normal(kb, (D,), "float32"))

    import ml_dtypes
    common = {}
    percore = {c: {} for c in range(NCORES)}
    for P in ("A", "dt", "Bp", "C"):
        wmu = np.asarray(inputs[f"L0_attn_{P}_wmu"], dtype=np.float32)
        wlv = np.asarray(inputs[f"L0_attn_{P}_wlv"], dtype=np.float32)
        wmuT = np.ascontiguousarray(wmu.T)
        wlvT = np.ascontiguousarray(wlv.T)
        epsT = np.ascontiguousarray(eps[(P, "w")].T.astype(ml_dtypes.bfloat16))
        for c in range(NCORES):
            percore[c][f"{P}_wmuT_my"] = np.ascontiguousarray(
                wmuT[c * 128:(c + 1) * 128])
            percore[c][f"{P}_wlvT_my"] = np.ascontiguousarray(
                wlvT[c * 128:(c + 1) * 128])
            percore[c][f"{P}_epsT_my"] = np.ascontiguousarray(
                epsT[c * 128:(c + 1) * 128])
        bmu = np.asarray(inputs[f"L0_attn_{P}_bmu"], dtype=np.float32)
        blv = np.asarray(inputs[f"L0_attn_{P}_blv"], dtype=np.float32)
        common[f"{P}_bmu8"] = np.ascontiguousarray(bmu.reshape(8, 128).T)
        common[f"{P}_blv8"] = np.ascontiguousarray(blv.reshape(8, 128).T)
        common[f"{P}_epsb8"] = np.ascontiguousarray(eps[(P, "b")].reshape(8, 128).T)
    common["W1"] = np.ascontiguousarray(np.asarray(inputs["L0_ffn_W1"], np.float32))
    common["b1_32"] = np.ascontiguousarray(
        np.asarray(inputs["L0_ffn_b1"], np.float32).reshape(ICN, 128).T)
    common["W2"] = np.ascontiguousarray(np.asarray(inputs["L0_ffn_W2"], np.float32))
    common["b2row"] = np.ascontiguousarray(
        np.asarray(inputs["L0_ffn_b2"], np.float32).reshape(1, D))
    common["eye"] = np.eye(128, dtype=np.float32)

    in_maps = []
    for c in range(NCORES):
        ms = np.zeros((128, 8), dtype=np.float32)
        ms[:, (c + 1) % 8] = 1.0
        rr = np.zeros((128, 8), dtype=np.float32)
        if c % 2 == 1:
            rr[:, c] = 1.0
        m = dict(common)
        m.update(percore[c])
        m["x"] = np.ascontiguousarray(x[c * T_CORE:(c + 1) * T_CORE])
        m["msend"] = ms
        m["rread"] = rr
        in_maps.append(m)
    return in_maps


def kernel(**inputs):
    T_CORE = 2048
    nc = _get_nc(T_CORE)
    in_maps = make_in_maps(inputs, T_CORE)
    res = run_bass_kernel_spmd(nc, in_maps, core_ids=list(range(NCORES)))
    out = np.concatenate([np.asarray(res.results[c]["out"]) for c in range(NCORES)],
                         axis=0)
    return out.reshape(4, 4096, 1024).astype(np.float32)


# version 21 (final)
# speedup vs baseline: 1.0543x; 1.0543x over previous
import numpy as np

import concourse.bass as bass
import concourse.mybir as mybir
from concourse.bass_utils import run_bass_kernel_spmd

F32 = mybir.dt.float32
BF16 = mybir.dt.bfloat16
AF = mybir.ActivationFunctionType
OP = mybir.AluOpType
AX = mybir.AxisListType.X

NCORES = 8
D = 1024
KD = 8
MC = 8
INTER = 4096
ICN = 32
LN_EPS = 1e-5


def _pstep(t):
    return t.ap().ap[0][0]


class Builder:

    ENGINES = ("sync", "tensor", "scalar", "vector", "gpsimd")

    def __init__(self, nc):
        self.nc = nc
        self.thunks = {e: [] for e in self.ENGINES}
        self.sems = {}
        self.counts = {}

    def sem(self, name):
        handle = self.nc.semaphore(name).__enter__()
        self.sems[name] = handle
        self.counts[name] = 0
        return handle

    def emit(self, eng, fn, inc=None):
        sig = None
        if inc is not None:
            semname, k = inc
            self.counts[semname] += k
            sig = (semname, self.counts[semname])
            handle = self.sems[semname]

            def wrapped(e, fn=fn, handle=handle, k=k):
                fn(e).then_inc(handle, k)
            self.thunks[eng].append(wrapped)
        else:
            self.thunks[eng].append(fn)
        return sig

    def wait(self, eng, sig):
        if sig is None:
            return
        semname, val = sig
        handle = self.sems[semname]
        self.thunks[eng].append(lambda e, h=handle, v=val: e.wait_ge(h, v))

    def run(self):
        nc = self.nc
        with nc.Block() as block:
            @block.sync
            def _(eng):
                for t in self.thunks["sync"]:
                    t(eng)

            @block.tensor
            def _(eng):
                for t in self.thunks["tensor"]:
                    t(eng)

            @block.scalar
            def _(eng):
                for t in self.thunks["scalar"]:
                    t(eng)

            @block.vector
            def _(eng):
                for t in self.thunks["vector"]:
                    t(eng)

            @block.gpsimd
            def _(eng):
                for t in self.thunks["gpsimd"]:
                    t(eng)


class BankPool:

    def __init__(self, B, banks):
        self.B = B
        self.banks = banks
        self.free_at = [None] * len(banks)
        self.rr = 0

    def acquire(self):
        i = self.rr % len(self.banks)
        self.rr += 1
        if self.free_at[i] is not None:
            self.B.wait("tensor", self.free_at[i])
            self.free_at[i] = None
        return i, self.banks[i]

    def release(self, i, sig):
        self.free_at[i] = sig


def build_nc(T_CORE=2048):
    TC = T_CORE // 128
    HALF = T_CORE // 2
    TBH = HALF // 512
    NU = 2 * MC

    nc = bass.Bass("TRN2", target_bir_lowering=False)

    x_d = nc.dram_tensor("x", [T_CORE, D], F32, kind="ExternalInput").ap()
    proj_d = {}
    for P in ("A", "dt", "Bp", "C"):
        proj_d[(P, "wmuT")] = nc.dram_tensor(f"{P}_wmuT_my", [128, D], F32,
                                             kind="ExternalInput").ap()
        proj_d[(P, "wlvT")] = nc.dram_tensor(f"{P}_wlvT_my", [128, D], F32,
                                             kind="ExternalInput").ap()
        proj_d[(P, "epsT")] = nc.dram_tensor(f"{P}_epsT_my", [128, D], BF16,
                                             kind="ExternalInput").ap()
        for part in ("bmu8", "blv8", "epsb8"):
            proj_d[(P, part)] = nc.dram_tensor(f"{P}_{part}", [128, 8], F32,
                                               kind="ExternalInput").ap()
    wag_in = nc.dram_tensor("wag_in", [128, 3072], BF16)
    wag_out = nc.dram_tensor("wag_out", [8, 128, 3072], BF16, addr_space="Shared")
    w1_d = nc.dram_tensor("W1", [D, INTER], F32, kind="ExternalInput").ap()
    b1_d = nc.dram_tensor("b1_32", [128, ICN], F32, kind="ExternalInput").ap()
    w2_d = nc.dram_tensor("W2", [INTER, D], F32, kind="ExternalInput").ap()
    b2_d = nc.dram_tensor("b2row", [1, D], F32, kind="ExternalInput").ap()
    eye_d = nc.dram_tensor("eye", [128, 128], F32, kind="ExternalInput").ap()
    msend_d = nc.dram_tensor("msend", [128, 8], F32, kind="ExternalInput").ap()
    rread_d = nc.dram_tensor("rread", [128, 8], F32, kind="ExternalInput").ap()
    out_d = nc.dram_tensor("out", [T_CORE, D], F32, kind="ExternalOutput").ap()

    hsp = nc.dram_tensor("hsp", [ICN, 128, T_CORE], BF16)
    x1sp = nc.dram_tensor("x1sp", [T_CORE, D], F32)
    cc_in = [nc.dram_tensor(f"cc_in{m}", [128, 8], F32) for m in range(MC)]
    cc_out = [nc.dram_tensor(f"cc_out{m}", [128, 8], F32) for m in range(MC)]

    from contextlib import ExitStack
    stack = ExitStack()
    en = stack.enter_context

    B = Builder(nc)
    for s in ("SQ", "SQB", "SQC", "GQ", "GQA", "SCQ", "TE", "SC", "VE", "CC",
              "FQ", "FQB", "FQC", "FQD", "W2Q", "W2QB", "RQ", "RQB"):
        B.sem(s)

    def fence(eng, semname):
        B.wait(eng, (semname, B.counts[semname]))

    XB = en(nc.sbuf_tensor("XB", [128, KD * T_CORE], BF16))
    EYE = en(nc.sbuf_tensor("EYE", [128, 128], F32))
    SM = en(nc.sbuf_tensor("SM", [128, 8], F32))
    SR = en(nc.sbuf_tensor("SR", [128, 8], F32))
    BRAW = en(nc.sbuf_tensor("BRAW", [128, 8 * 12], F32))
    BS = en(nc.sbuf_tensor("BS", [128, 8], F32))
    BB = en(nc.sbuf_tensor("BB", [128, 8], F32))
    BC4 = en(nc.sbuf_tensor("BC4", [128, 8], F32))
    BT = en(nc.sbuf_tensor("BT", [128, 8 * 3], F32))
    B1T = en(nc.sbuf_tensor("B1T", [128, ICN], F32))
    CTB = en(nc.sbuf_tensor("CTB", [128, 8 * MC], F32))
    ARB = en(nc.sbuf_tensor("ARB", [128, 8 * MC], F32))
    INI = en(nc.sbuf_tensor("INI", [128, MC], F32))
    ZZ = en(nc.sbuf_tensor("ZZ", [128, 8], F32))
    HCC = en(nc.sbuf_tensor("HCC", [128, 1], F32))
    LN4C = en(nc.sbuf_tensor("LN4C", [128, 1], F32))
    SUMS = en(nc.sbuf_tensor("SUMS", [128, TC], F32))
    SQS = en(nc.sbuf_tensor("SQS", [128, TC], F32))
    NEGM = en(nc.sbuf_tensor("NEGM", [128, TC], F32))
    VARP = en(nc.sbuf_tensor("VARP", [128, TC], F32))
    RSTD = en(nc.sbuf_tensor("RSTD", [128, TC], F32))
    NMRS = en(nc.sbuf_tensor("NMRS", [128, TC], F32))

    banks = [en(nc.psum_tensor(f"pb{i}", [128, 512], F32)) for i in range(8)]
    pool = BankPool(B, banks)

    stackAB = ExitStack()
    enB = stackAB.enter_context
    WS = enB(nc.sbuf_tensor("WS", [128, KD * 1024], BF16))
    WB = enB(nc.sbuf_tensor("WB", [128, KD * 1024], BF16))
    WC = enB(nc.sbuf_tensor("WC", [128, KD * 1024], BF16))
    RT = enB(nc.sbuf_tensor("RT", [128, TC * 1024], F32))
    EBUF = enB(nc.sbuf_tensor("EBUF", [128, 3 * HALF], F32))
    BXBUF = enB(nc.sbuf_tensor("BXBUF", [128, 4 * HALF], F32))
    HBUF = enB(nc.sbuf_tensor("HBUF", [128, 3 * HALF], F32))
    XCOL = enB(nc.sbuf_tensor("XCOL", [128, 1024], F32))
    WST = enB(nc.sbuf_tensor("WST", [128, 2 * 1024], F32))
    WSTE = enB(nc.sbuf_tensor("WSTE", [128, 1024], BF16))
    UEX = enB(nc.sbuf_tensor("UEX", [128, 1024], F32))

    pR = _pstep(RT)
    pH = _pstep(HBUF)

    x_sigs = []

    def emit_x_loads():
        for tc in range(TC):
            sig = B.emit("gpsimd", lambda e, tc=tc: e.dma_start(
                out=RT[:, tc * 1024:(tc + 1) * 1024],
                in_=x_d[tc * 128:(tc + 1) * 128, :]), inc=("GQ", 16))
            x_sigs.append(sig)
            if tc % 4 == 3:
                fence("gpsimd", "GQ")

    tiny = [(EYE[:, :], eye_d), (SM[:, :], msend_d), (SR[:, :], rread_d),
            (B1T[:, :], b1_d)]
    for i, (P, part) in enumerate([(P, p) for P in ("A", "dt", "Bp", "C")
                                   for p in ("bmu8", "blv8", "epsb8")]):
        tiny.append((BRAW[:, i * 8:(i + 1) * 8], proj_d[(P, part)]))
    tiny_sig = None
    for dst, src in tiny:
        tiny_sig = B.emit("gpsimd", lambda e, dst=dst, src=src: e.dma_start(
            out=dst, in_=src[:, :]), inc=("GQ", 16))
    fence("gpsimd", "GQ")


    B.wait("tensor", tiny_sig)
    for w in range(70):
        bi, pb = pool.acquire()
        for j in range(4):
            fn = lambda e, pb=pb, j=j: e.transpose(
                pb[:, j * 128:(j + 1) * 128], EYE[:, :], EYE[:, :])
            if j == 3:
                wsig = B.emit("tensor", fn, inc=("TE", 1))
            else:
                B.emit("tensor", fn)
        pool.release(bi, wsig)

    ln4_sig = B.emit("vector", lambda e: e.memset(LN4C[:, :], float(np.log(4.0))),
                     inc=("VE", 1))

    prep_dma = {}
    last_prep = [None]
    for P in ("A", "dt", "Bp", "C"):
        B.wait("sync", last_prep[0])
        for j, part in enumerate(("wmuT", "wlvT")):
            B.emit("sync", lambda e, P=P, part=part, j=j: e.dma_start(
                out=WST[:, j * 1024:(j + 1) * 1024], in_=proj_d[(P, part)][:, :]),
                inc=("SQ", 16))
        sig = B.emit("sync", lambda e, P=P: e.dma_start(
            out=WSTE[:, :], in_=proj_d[(P, "epsT")][:, :]), inc=("SQ", 16))
        prep_dma[P] = sig
        fence("sync", "SQ")

        mode = {"A": "hold", "dt": "ws", "Bp": "wb", "C": "wc4"}[P]
        B.wait("scalar", prep_dma[P])
        B.wait("scalar", last_prep[0])
        if P == "C":
            B.wait("scalar", ln4_sig)
        for half in range(2):
            fn = lambda e, half=half, P=P: e.activation(
                out=UEX[:, half * 512:(half + 1) * 512],
                in_=WST[:, 1024 + half * 512: 1024 + (half + 1) * 512],
                func=AF.Exp, scale=0.5,
                **({"bias": LN4C[:, 0:1]} if P == "C" else {}))
            s1 = B.emit("scalar", fn, inc=("SC", 1))
        B.wait("vector", s1)
        for half in range(2):
            m1 = B.emit("vector", lambda e, half=half: e.tensor_mul(
                UEX[:, half * 512:(half + 1) * 512],
                UEX[:, half * 512:(half + 1) * 512],
                WSTE[:, half * 512:(half + 1) * 512]), inc=("VE", 1))
        B.wait("vector", m1)
        for half in range(2):
            hs = half * 512
            if mode == "hold":
                sig = B.emit("vector", lambda e, hs=hs: e.tensor_add(
                    XCOL[:, hs:hs + 512], UEX[:, hs:hs + 512],
                    WST[:, hs:hs + 512]), inc=("VE", 1))
            elif mode == "ws":
                a1 = B.emit("vector", lambda e, hs=hs: e.tensor_add(
                    UEX[:, hs:hs + 512], UEX[:, hs:hs + 512],
                    WST[:, hs:hs + 512]), inc=("VE", 1))
                B.wait("vector", a1)
                sig = B.emit("vector", lambda e, hs=hs: e.tensor_add(
                    WS[:, hs:hs + 512], XCOL[:, hs:hs + 512],
                    UEX[:, hs:hs + 512]), inc=("VE", 1))
            elif mode == "wb":
                sig = B.emit("vector", lambda e, hs=hs: e.tensor_add(
                    WB[:, hs:hs + 512], UEX[:, hs:hs + 512],
                    WST[:, hs:hs + 512]), inc=("VE", 1))
            else:
                sig = B.emit("vector", lambda e, hs=hs: e.scalar_tensor_tensor(
                    out=WC[:, hs:hs + 512], in0=WST[:, hs:hs + 512],
                    scalar=4.0, in1=UEX[:, hs:hs + 512], op0=OP.mult, op1=OP.add),
                    inc=("VE", 1))
        last_prep[0] = sig
        if P == "C":
            wbc_con = sig

    B.wait("gpsimd", wbc_con)
    B.emit("gpsimd", lambda e: e.dma_start(
        out=wag_in.ap()[:, 0:1024], in_=WS[:, 0:1024]), inc=("GQ", 16))
    B.emit("gpsimd", lambda e: e.dma_start(
        out=wag_in.ap()[:, 1024:2048], in_=WB[:, 0:1024]), inc=("GQ", 16))
    agd = B.emit("gpsimd", lambda e: e.dma_start(
        out=wag_in.ap()[:, 2048:3072], in_=WC[:, 0:1024]), inc=("GQ", 16))
    B.wait("gpsimd", agd)
    ag1 = B.emit("gpsimd", lambda e: e.collective_compute(
        "AllGather", OP.bypass, replica_groups=[list(range(NCORES))],
        ins=[wag_in.ap().opt()], outs=[wag_out.ap().opt()]), inc=("CC", 1))

    emit_x_loads()

    for q in range(TC // 4):
        for k in range(KD):
            bi, pb = pool.acquire()
            B.wait("tensor", x_sigs[4 * q + 3])
            if q == 0 and k == 0:
                B.wait("tensor", tiny_sig)
            for j in range(4):
                tc = 4 * q + j
                fn = lambda e, tc=tc, k=k, pb=pb, j=j: e.transpose(
                    pb[:, j * 128:(j + 1) * 128],
                    RT[:, tc * 1024 + k * 128: tc * 1024 + k * 128 + 128],
                    EYE[:, :])
                if j == 3:
                    tsig = B.emit("tensor", fn, inc=("TE", 1))
                else:
                    B.emit("tensor", fn)
            B.wait("scalar", tsig)
            esig = B.emit("scalar", lambda e, k=k, q=q, pb=pb: e.copy(
                out=XB[:, k * T_CORE + q * 512: k * T_CORE + (q + 1) * 512],
                in_=pb[:, :]), inc=("SC", 1))
            pool.release(bi, esig)
    xb_done = esig

    B.wait("sync", ag1)
    pWS, pWB, pWC = _pstep(WS), _pstep(WB), _pstep(WC)
    ws_sig_all = B.emit("sync", lambda e: e.dma_start(
        out=bass.AP(WS, 0, [[pWS, 128], [1024, 8], [1, 1024]]),
        in_=bass.AP(wag_out, 0, [[3072, 128], [128 * 3072, 8], [1, 1024]])),
        inc=("SQ", 16))
    fence("sync", "SQ")
    wb_sig_all = B.emit("sync", lambda e: e.dma_start(
        out=bass.AP(WB, 0, [[pWB, 128], [1024, 8], [1, 1024]]),
        in_=bass.AP(wag_out, 1024, [[3072, 128], [128 * 3072, 8], [1, 1024]])),
        inc=("SQ", 16))
    wc_sig_all = B.emit("sync", lambda e: e.dma_start(
        out=bass.AP(WC, 0, [[pWC, 128], [1024, 8], [1, 1024]]),
        in_=bass.AP(wag_out, 2048, [[3072, 128], [128 * 3072, 8], [1, 1024]])),
        inc=("SQB", 16))
    fence("sync", "SQ")
    fence("sync", "SQB")
    B.wait("scalar", tiny_sig)
    prev_v = None
    for i, dst in enumerate((None, None, BB, BC4)):
        B.wait("scalar", prev_v)
        s1 = B.emit("scalar", lambda e, i=i: e.activation(
            out=BT[:, 0:8], in_=BRAW[:, (3 * i + 1) * 8:(3 * i + 2) * 8],
            func=AF.Exp, scale=0.5), inc=("SC", 1))
        B.wait("vector", s1)
        tgt = BT[:, 8:16] if i == 0 else (BT[:, 16:24] if i == 1 else dst[:, :])
        v1 = B.emit("vector", lambda e, i=i, tgt=tgt: e.tensor_mul(
            tgt, BT[:, 0:8], BRAW[:, (3 * i + 2) * 8:(3 * i + 3) * 8]),
            inc=("VE", 1))
        B.wait("vector", v1)
        v2 = B.emit("vector", lambda e, i=i, tgt=tgt: e.tensor_add(
            tgt, tgt, BRAW[:, (3 * i) * 8:(3 * i + 1) * 8]), inc=("VE", 1))
        B.wait("vector", v2)
        prev_v = v2
    v = B.emit("vector", lambda e: e.tensor_add(
        BS[:, :], BT[:, 8:16], BT[:, 16:24]), inc=("VE", 1))
    B.wait("vector", v)
    bias_done = B.emit("vector", lambda e: e.tensor_scalar(
        out=BC4[:, :], in0=BC4[:, :], scalar1=4.0, scalar2=None,
        op0=OP.mult), inc=("VE", 1))

    ws_sig = {0: ws_sig_all, 1: ws_sig_all}
    wb_sig_h = {0: wb_sig_all, 1: wb_sig_all}
    wc_sig_h = {0: wc_sig_all, 1: wc_sig_all}
    prev_prep = [sig]

    head_state = {}

    def emit_head(u):
        m, h = u // 2, u % 2
        sl = u % 3
        xsl = u % 2
        bsl = u % 4
        st = {}
        for q2 in range(TBH):
            bi, pb = pool.acquire()
            for j in range(4):
                tc = (TC // 2) * h + 4 * q2 + j
                fn = lambda e, tc=tc, m=m, pb=pb, j=j: e.transpose(
                    pb[:, j * 128:(j + 1) * 128],
                    RT[:, tc * 1024 + m * 128: tc * 1024 + m * 128 + 128],
                    EYE[:, :])
                if j == 3:
                    tsig = B.emit("tensor", fn, inc=("TE", 1))
                else:
                    B.emit("tensor", fn)
            B.wait("scalar", tsig)
            if u == 0 and q2 == 0:
                B.wait("scalar", wbc_con)
            if u >= 1:
                B.wait("scalar", head_state.get(("xcol_reader", u - 1)))
            xsig = B.emit("scalar", lambda e, q2=q2, pb=pb: e.copy(
                out=XCOL[:, q2 * 512:(q2 + 1) * 512],
                in_=pb[:, :]), inc=("SC", 1))
            pool.release(bi, xsig)
            st[("xcol", q2)] = xsig
        if u == 0:
            B.wait("tensor", xb_done)
        if m % 4 == 0 and h == 0:
            B.wait("tensor", ws_sig[m // 4])
        sbanks = [pool.acquire() for _ in range(TBH)]
        for k in range(KD):
            for tb in range(TBH):
                fn = lambda e, k=k, m=m, h=h, tb=tb, pb=sbanks[tb][1]: e.matmul(
                    pb[:, :],
                    WS[:, k * 1024 + m * 128: k * 1024 + m * 128 + 128],
                    XB[:, k * T_CORE + h * HALF + tb * 512:
                       k * T_CORE + h * HALF + (tb + 1) * 512],
                    start=(k == 0), stop=(k == KD - 1))
                if k == KD - 1:
                    st[("smm", tb)] = B.emit("tensor", fn, inc=("TE", 1))
                else:
                    B.emit("tensor", fn)
        for tb in range(TBH):
            bi, pb = sbanks[tb]
            B.wait("vector", st[("smm", tb)])
            if u == 0 and tb == 0:
                B.wait("vector", bias_done)
            if u >= 3:
                B.wait("vector", head_state.get(("ebuf_reader", u - 3)))
            vc = B.emit("vector", lambda e, pb=pb, m=m, sl=sl, tb=tb: e.tensor_scalar(
                out=EBUF[:, sl * HALF + tb * 512: sl * HALF + (tb + 1) * 512],
                in0=pb[:, :], scalar1=BS[:, m:m + 1], scalar2=85.0,
                op0=OP.add, op1=OP.min), inc=("VE", 1))
            pool.release(bi, vc)
            st[("e1", tb)] = vc
        for tb in range(TBH):
            B.wait("scalar", st[("e1", tb)])
            s1 = B.emit("scalar", lambda e, sl=sl, tb=tb: e.activation(
                out=EBUF[:, sl * HALF + tb * 512: sl * HALF + (tb + 1) * 512],
                in_=EBUF[:, sl * HALF + tb * 512: sl * HALF + (tb + 1) * 512],
                func=AF.Exp), inc=("SC", 1))
            st[("e1b", tb)] = s1
        B.wait("scalar", st[("e1b", TBH - 1)])
        for tb in range(TBH):
            s2 = B.emit("scalar", lambda e, sl=sl, tb=tb: e.activation(
                out=EBUF[:, sl * HALF + tb * 512: sl * HALF + (tb + 1) * 512],
                in_=EBUF[:, sl * HALF + tb * 512: sl * HALF + (tb + 1) * 512],
                func=AF.Exp, scale=-1.0), inc=("SC", 1))
        st["E"] = s2
        if m % 4 == 0 and h == 0:
            B.wait("tensor", wb_sig_h[m // 4])
        bbanks = [pool.acquire() for _ in range(TBH)]
        bsig = {}
        for k in range(KD):
            for tb in range(TBH):
                fn = lambda e, k=k, m=m, h=h, tb=tb, pb=bbanks[tb][1]: e.matmul(
                    pb[:, :],
                    WB[:, k * 1024 + m * 128: k * 1024 + m * 128 + 128],
                    XB[:, k * T_CORE + h * HALF + tb * 512:
                       k * T_CORE + h * HALF + (tb + 1) * 512],
                    start=(k == 0), stop=(k == KD - 1))
                if k == KD - 1:
                    bsig[tb] = B.emit("tensor", fn, inc=("TE", 1))
                else:
                    B.emit("tensor", fn)
        for tb in range(TBH):
            bi, pb = bbanks[tb]
            B.wait("vector", bsig[tb])
            B.wait("vector", st[("xcol", tb)])
            if u >= 4:
                B.wait("vector", head_state.get(("bx_reader", u - 4)))
            vsig = B.emit("vector", lambda e, pb=pb, m=m, bsl=bsl, tb=tb: e.scalar_tensor_tensor(
                out=BXBUF[:, bsl * HALF + tb * 512: bsl * HALF + (tb + 1) * 512],
                in0=pb[:, :], scalar=BB[:, m:m + 1],
                in1=XCOL[:, tb * 512:(tb + 1) * 512],
                op0=OP.add, op1=OP.mult), inc=("VE", 1))
            pool.release(bi, vsig)
        head_state[("xcol_reader", u)] = vsig
        B.wait("vector", st["E"])
        B.wait("vector", vsig)
        if h == 0:
            init = 0.0
        else:
            B.wait("vector", head_state[u - 1]["scan1"])
            init = bass.AP(HBUF, ((u - 1) % 3) * HALF + HALF - 1, [[pH, 128], [1, 1]])
        s1sig = B.emit("vector", lambda e, sl=sl, bsl=bsl, init=init: e.tensor_tensor_scan(
            out=HBUF[:, sl * HALF:(sl + 1) * HALF],
            data0=EBUF[:, sl * HALF:(sl + 1) * HALF],
            data1=BXBUF[:, bsl * HALF:(bsl + 1) * HALF],
            initial=init, op0=OP.mult, op1=OP.add), inc=("VE", 1))
        st["scan1"] = s1sig
        if h == 1:
            B.wait("vector", s1sig)
            bc = bass.AP(HBUF, sl * HALF + HALF - 1, [[pH, 128], [0, 8]])
            ctsig = B.emit("vector", lambda e, m=m, bc=bc: e.tensor_tensor(
                out=CTB[:, m * 8:(m + 1) * 8], in0=SM[:, :], in1=bc,
                op=OP.mult), inc=("VE", 1))
            B.wait("gpsimd", ctsig)
            d1 = B.emit("gpsimd", lambda e, m=m: e.dma_start(
                out=cc_in[m].ap(), in_=CTB[:, m * 8:(m + 1) * 8]), inc=("GQ", 16))
            B.wait("gpsimd", d1)
            ccsig = B.emit("gpsimd", lambda e, m=m: e.collective_compute(
                "AllReduce", OP.add,
                replica_groups=[list(range(NCORES))],
                ins=[cc_in[m].ap().opt()],
                outs=[cc_out[m].ap().opt()]), inc=("CC", 1))
            B.wait("gpsimd", ccsig)
            arsig = B.emit("gpsimd", lambda e, m=m: e.dma_start(
                out=ARB[:, m * 8:(m + 1) * 8], in_=cc_out[m].ap()), inc=("GQA", 16))
            st["ar"] = arsig
        head_state[u] = st
        return st

    def emit_tail_a(u):
        m, h = u // 2, u % 2
        sl = u % 3
        bsl = u % 4
        if h == 0:
            arsig = head_state[2 * m + 1]["ar"]
            B.wait("vector", arsig)
            z1 = B.emit("vector", lambda e, m=m: e.tensor_mul(
                ZZ[:, :], ARB[:, m * 8:(m + 1) * 8], SR[:, :]), inc=("VE", 1))
            B.wait("vector", z1)
            z2 = B.emit("vector", lambda e, m=m: e.reduce_sum(
                out=INI[:, m:m + 1], in_=ZZ[:, :], axis=AX), inc=("VE", 1))
            B.wait("vector", z2)
            init = bass.AP(INI, m, [[_pstep(INI), 128], [1, 1]])
        else:
            B.wait("vector", head_state[("hcc", m)])
            init = bass.AP(HCC, 0, [[_pstep(HCC), 128], [1, 1]])
        s2sig = B.emit("vector", lambda e, sl=sl, bsl=bsl, init=init: e.tensor_tensor_scan(
            out=HBUF[:, sl * HALF:(sl + 1) * HALF],
            data0=EBUF[:, sl * HALF:(sl + 1) * HALF],
            data1=BXBUF[:, bsl * HALF:(bsl + 1) * HALF],
            initial=init, op0=OP.mult, op1=OP.add), inc=("VE", 1))
        head_state[("ebuf_reader", u)] = s2sig
        if h == 0:
            B.wait("vector", s2sig)
            hcsig = B.emit("vector", lambda e, sl=sl: e.tensor_copy(
                out=HCC[:, 0:1],
                in_=HBUF[:, sl * HALF + HALF - 1: sl * HALF + HALF]), inc=("VE", 1))
            head_state[("hcc", m)] = hcsig
        if m % 4 == 0 and h == 0:
            B.wait("tensor", wc_sig_h[m // 4])
        cbanks = [pool.acquire() for _ in range(TBH)]
        csig = {}
        for k in range(KD):
            for tb in range(TBH):
                fn = lambda e, k=k, m=m, h=h, tb=tb, pb=cbanks[tb][1]: e.matmul(
                    pb[:, :],
                    WC[:, k * 1024 + m * 128: k * 1024 + m * 128 + 128],
                    XB[:, k * T_CORE + h * HALF + tb * 512:
                       k * T_CORE + h * HALF + (tb + 1) * 512],
                    start=(k == 0), stop=(k == KD - 1))
                if k == KD - 1:
                    csig[tb] = B.emit("tensor", fn, inc=("TE", 1))
                else:
                    B.emit("tensor", fn)
        head_state[("cmm", u)] = csig[TBH - 1]
        ysigs = []
        for tb in range(TBH):
            bi, pb = cbanks[tb]
            B.wait("vector", csig[tb])
            B.wait("vector", s2sig)
            ysig = B.emit("vector", lambda e, pb=pb, m=m, sl=sl, bsl=bsl, tb=tb: e.scalar_tensor_tensor(
                out=BXBUF[:, bsl * HALF + tb * 512: bsl * HALF + (tb + 1) * 512],
                in0=pb[:, :], scalar=BC4[:, m:m + 1],
                in1=HBUF[:, sl * HALF + tb * 512: sl * HALF + (tb + 1) * 512],
                op0=OP.add, op1=OP.mult), inc=("VE", 1))
            pool.release(bi, ysig)
            ysigs.append(ysig)
        head_state[("ysigs", u)] = ysigs

    def emit_tail_b(u):
        m, h = u // 2, u % 2
        bsl = u % 4
        ysigs = head_state[("ysigs", u)]
        for q2 in range(TBH):
            B.wait("tensor", ysigs[q2])
            bi, pb = pool.acquire()
            for j in range(4):
                fn = lambda e, bsl=bsl, q2=q2, j=j, pb=pb: e.transpose(
                    pb[:, j * 128:(j + 1) * 128],
                    BXBUF[:, bsl * HALF + q2 * 512 + j * 128:
                          bsl * HALF + q2 * 512 + (j + 1) * 128],
                    EYE[:, :])
                if j == 3:
                    tsig = B.emit("tensor", fn, inc=("TE", 1))
                else:
                    B.emit("tensor", fn)
            B.wait("vector", tsig)
            tc0 = (TC // 2) * h + 4 * q2
            rt_ap = bass.AP(RT, tc0 * 1024 + m * 128, [[pR, 128], [1024, 4], [1, 128]])
            pb_ap = bass.AP(pb, 0, [[_pstep(pb), 128], [128, 4], [1, 128]])
            rsig = B.emit("vector", lambda e, rt_ap=rt_ap, pb_ap=pb_ap: e.scalar_tensor_tensor(
                out=rt_ap, in0=pb_ap, scalar=1.0, in1=rt_ap,
                op0=OP.mult, op1=OP.add), inc=("VE", 1))
            pool.release(bi, rsig)
        head_state[("bx_reader", u)] = rsig
        head_state[("resid", u)] = rsig

    for s in range(NU + 3):
        if s < NU:
            emit_head(s)
        if 2 <= s < NU + 2:
            emit_tail_a(s - 2)
        if s >= 3:
            emit_tail_b(s - 3)

    B.wait("vector", head_state[("resid", NU - 1)])
    B.wait("vector", head_state[("xcol_reader", NU - 1)])
    pX = _pstep(XCOL)
    ln1_sigs = {}
    spill_sigs = []
    x1t_done = None
    for g in range(TC // 4):
        t0 = 4 * g
        for j in range(4):
            tc = t0 + j
            B.emit("vector", lambda e, tc=tc: e.bn_stats(
                out=XCOL[:, tc * 12: tc * 12 + 6],
                in_=RT[:, tc * 1024: tc * 1024 + 512]))
            s2 = B.emit("vector", lambda e, tc=tc: e.bn_stats(
                out=XCOL[:, tc * 12 + 6: tc * 12 + 12],
                in_=RT[:, tc * 1024 + 512: tc * 1024 + 1024]),
                inc=("VE", 1) if j == 3 else None)
        B.wait("vector", s2)
        for j in range(4):
            tc = t0 + j
            bnsig = B.emit("vector", lambda e, tc=tc: e.bn_aggr(
                out=XCOL[:, 192 + 2 * tc: 192 + 2 * tc + 2],
                in_=XCOL[:, tc * 12: tc * 12 + 12]),
                inc=("VE", 1) if j == 3 else None)
        B.wait("vector", bnsig)
        mean_g = bass.AP(XCOL, 192 + 8 * g, [[pX, 128], [2, 4]])
        var_g = bass.AP(XCOL, 193 + 8 * g, [[pX, 128], [2, 4]])
        ng = B.emit("vector", lambda e, mean_g=mean_g, t0=t0: e.tensor_scalar(
            out=NEGM[:, t0:t0 + 4], in0=mean_g, scalar1=-1.0, scalar2=None,
            op0=OP.mult), inc=("VE", 1))
        vg = B.emit("vector", lambda e, var_g=var_g, t0=t0: e.tensor_scalar(
            out=VARP[:, t0:t0 + 4], in0=var_g, scalar1=1.0, scalar2=LN_EPS,
            op0=OP.mult, op1=OP.add), inc=("VE", 1))
        B.wait("scalar", vg)
        sq1 = B.emit("scalar", lambda e, t0=t0: e.activation(
            out=RSTD[:, t0:t0 + 4], in_=VARP[:, t0:t0 + 4], func=AF.Sqrt),
            inc=("SC", 1))
        B.wait("vector", sq1)
        v2 = B.emit("vector", lambda e, t0=t0: e.reciprocal(
            out=RSTD[:, t0:t0 + 4], in_=RSTD[:, t0:t0 + 4]), inc=("VE", 1))
        B.wait("vector", v2)
        B.wait("vector", ng)
        v3 = B.emit("vector", lambda e, t0=t0: e.tensor_mul(
            NMRS[:, t0:t0 + 4], NEGM[:, t0:t0 + 4], RSTD[:, t0:t0 + 4]),
            inc=("VE", 1))
        B.wait("scalar", v3)
        for j in range(4):
            tc = t0 + j
            s = B.emit("scalar", lambda e, tc=tc: e.activation(
                out=RT[:, tc * 1024:(tc + 1) * 1024],
                in_=RT[:, tc * 1024:(tc + 1) * 1024],
                func=AF.Identity, scale=RSTD[:, tc:tc + 1],
                bias=NMRS[:, tc:tc + 1]), inc=("SC", 1))
            ln1_sigs[tc] = s
            B.wait("gpsimd", s)
            sp = B.emit("gpsimd", lambda e, tc=tc: e.dma_start(
                out=x1sp.ap()[tc * 128:(tc + 1) * 128, :],
                in_=RT[:, tc * 1024:(tc + 1) * 1024]), inc=("GQ", 16))
            spill_sigs.append(sp)
        for k in range(KD):
            bi, pb = pool.acquire()
            B.wait("tensor", ln1_sigs[t0 + 3])
            for j in range(4):
                tc = t0 + j
                fn = lambda e, tc=tc, k=k, pb=pb, j=j: e.transpose(
                    pb[:, j * 128:(j + 1) * 128],
                    RT[:, tc * 1024 + k * 128: tc * 1024 + k * 128 + 128],
                    EYE[:, :])
                if j == 3:
                    tsig = B.emit("tensor", fn, inc=("TE", 1))
                else:
                    B.emit("tensor", fn)
            B.wait("scalar", tsig)
            esig = B.emit("scalar", lambda e, k=k, g=g, pb=pb: e.copy(
                out=XB[:, k * T_CORE + g * 512: k * T_CORE + (g + 1) * 512],
                in_=pb[:, :]), inc=("SC", 1))
            pool.release(bi, esig)
            x1t_done = esig
    ln1_sigs = [ln1_sigs[tc] for tc in range(TC)]
    v3 = x1t_done

    stackAB.close()

    W1ST = en(nc.sbuf_tensor("W1ST", [128, 2 * 1024], F32))
    W1B = en(nc.sbuf_tensor("W1B", [128, 2 * KD * 1024], BF16))
    W2ST = en(nc.sbuf_tensor("W2ST", [128, 2 * 1024], F32))
    W2B = en(nc.sbuf_tensor("W2B", [128, ICN * 1024], BF16))
    HT1 = en(nc.sbuf_tensor("HT1", [128, 2 * T_CORE], BF16))
    HRD = en(nc.sbuf_tensor("HRD", [128, 3 * ICN * 128], BF16))
    X1RD = en(nc.sbuf_tensor("X1RD", [128, 3 * 1024], F32))
    R2 = en(nc.sbuf_tensor("R2", [128, 2 * 1024], F32))
    SQ2 = en(nc.sbuf_tensor("SQ2", [128, 1024], F32))
    B2B = en(nc.sbuf_tensor("B2B", [1, D], BF16))
    ONES1 = en(nc.sbuf_tensor("ONES1", [1, 128], BF16))
    S2H = en(nc.sbuf_tensor("S2H", [128, 2 * TC], F32))
    NEGM2 = en(nc.sbuf_tensor("NEGM2", [128, TC], F32))
    VARP2 = en(nc.sbuf_tensor("VARP2", [128, TC], F32))
    RSTD2 = en(nc.sbuf_tensor("RSTD2", [128, TC], F32))
    NMRS2 = en(nc.sbuf_tensor("NMRS2", [128, TC], F32))

    gate_c1 = head_state[("cmm", NU - 1)]
    gate_sigs = [x1t_done, spill_sigs[-1], head_state[("resid", NU - 1)],
                 ln1_sigs[-1], v3]
    B.wait("sync", gate_c1)
    b2sig = B.emit("sync", lambda e: e.dma_start(out=W2ST[0:1, 0:1024],
                                                 in_=b2_d[:, :]), inc=("SQ", 16))
    fence("sync", "SQ")
    B.wait("scalar", b2sig)
    b2b = B.emit("scalar", lambda e: e.copy(out=B2B[:, :], in_=W2ST[0:1, 0:1024]),
                 inc=("SC", 1))
    for g in gate_sigs:
        B.wait("vector", g)
    ones_sig = B.emit("vector", lambda e: e.memset(ONES1[:, :], 1.0), inc=("VE", 1))
    for g in gate_sigs:
        B.wait("scalar", g)

    w1_dma = {}
    w2_dma = {}
    w1b_sig = {}
    w2b_sig = {}
    w1b_reader = {}
    h_spill = {}
    ffn_fence = {0: None, 1: None, 2: None, 3: None}
    w2_fence = {0: None, 1: None}
    TBF = T_CORE // 512
    NQ = ICN // 8

    def w2_group(j):
        slot = j % 2
        sem = ("W2Q", "W2QB")[slot]
        if w2_fence[slot] is not None:
            B.wait("scalar", w2_fence[slot])
        sig = B.emit("scalar", lambda e, j=j, slot=slot: e.dma_start(
            out=W2ST[:, slot * 1024:slot * 1024 + 1024],
            in_=w2_d[j * 128:(j + 1) * 128, :]), inc=(sem, 16))
        w2_fence[slot] = sig
        return sig

    w1q_sig = {}
    w1q_reader = {}
    qconv = {}

    w1conv = {}

    def w1_quarter(q):
        qsl = q % 2
        sigs = []
        for k in range(KD):
            g = q * KD + k
            slot = g % 2
            sem = ("FQ", "FQB", "FQC", "FQD")[g % 4]
            if ffn_fence[g % 4] is not None:
                B.wait("sync", ffn_fence[g % 4])
            if g >= 2:
                B.wait("sync", w1conv.get(g - 2))
            dsig = B.emit("sync", lambda e, k=k, q=q, slot=slot: e.dma_start(
                out=W1ST[:, slot * 1024:(slot + 1) * 1024],
                in_=w1_d[k * 128:(k + 1) * 128, q * 1024:(q + 1) * 1024]),
                inc=(sem, 16))
            ffn_fence[g % 4] = dsig
            B.wait("vector", dsig)
            if (q, k) in qconv_war:
                B.wait("vector", qconv_war[(q, k)])
            csig = B.emit("vector", lambda e, k=k, qsl=qsl, slot=slot: e.tensor_copy(
                out=W1B[:, qsl * KD * 1024 + k * 1024: qsl * KD * 1024 + (k + 1) * 1024],
                in_=W1ST[:, slot * 1024:(slot + 1) * 1024]), inc=("VE", 1))
            w1conv[g] = csig
            sigs.append(csig)
        return sigs[-1]

    qconv_war = {}
    w2c_j = [0]
    h_spill = {}
    w1q_sig[0] = w1_quarter(0)
    for ic in range(ICN):
        q = ic // 8
        qsl = q % 2
        icr = ic % 8
        if icr == 0 and q + 1 < NQ:
            if q >= 1:
                B.wait("sync", w1q_reader[q - 1])
                for k in range(KD):
                    qconv_war[(q + 1, k)] = w1q_reader[q - 1]
            w1q_sig[q + 1] = w1_quarter(q + 1)
        if ic < ICN:
            j = ic
            if j == 0:
                B.wait("scalar", b2b)
            if j >= 2:
                B.wait("scalar", w2b_sig[j - 2])
            w2_dma[j] = w2_group(j)
        if w2c_j[0] <= ic:
            j = w2c_j[0]
            B.wait("vector", w2_dma[j])
            w2b_sig[j] = B.emit("vector", lambda e, j=j: e.tensor_copy(
                out=W2B[:, j * 1024:(j + 1) * 1024],
                in_=W2ST[:, (j % 2) * 1024:(j % 2) * 1024 + 1024]), inc=("VE", 1))
            w2c_j[0] += 1
        par = ic % 2
        fbanks = [pool.acquire() for _ in range(TBF)]
        if icr == 0:
            B.wait("tensor", w1q_sig[q])
        fsig = {}
        for k in range(KD):
            for tb in range(TBF):
                fn = lambda e, k=k, qsl=qsl, icr=icr, tb=tb, pb=fbanks[tb][1]: e.matmul(
                    pb[:, :],
                    W1B[:, qsl * KD * 1024 + k * 1024 + icr * 128:
                        qsl * KD * 1024 + k * 1024 + (icr + 1) * 128],
                    XB[:, k * T_CORE + tb * 512: k * T_CORE + (tb + 1) * 512],
                    start=(k == 0), stop=(k == KD - 1))
                if k == KD - 1:
                    fsig[tb] = B.emit("tensor", fn, inc=("TE", 1))
                else:
                    B.emit("tensor", fn)
        if icr == 7:
            w1q_reader[q] = fsig[TBF - 1]
        for tb in range(TBF):
            bi, pb = fbanks[tb]
            B.wait("scalar", fsig[tb])
            if ic >= 2 and tb == 0:
                B.wait("scalar", h_spill.get(ic - 2))
            rsig = B.emit("scalar", lambda e, pb=pb, ic=ic, par=par, tb=tb: e.activation(
                out=HT1[:, par * T_CORE + tb * 512: par * T_CORE + (tb + 1) * 512],
                in_=pb[:, :], func=AF.Relu, bias=B1T[:, ic:ic + 1], scale=1.0),
                inc=("SC", 1))
            pool.release(bi, rsig)
        B.wait("gpsimd", rsig)
        hsp_ap = bass.AP(hsp, ic * 128 * T_CORE, [[T_CORE, 128], [1, T_CORE]])
        h_spill[ic] = B.emit("gpsimd", lambda e, hsp_ap=hsp_ap, par=par: e.dma_start(
            out=hsp_ap,
            in_=HT1[:, par * T_CORE:(par + 1) * T_CORE]), inc=("GQ", 16))
        fence("gpsimd", "GQ")

    rb_sig = {}
    rb_reader = {}
    rb_fence = {0: None, 1: None}
    out_sig = {}
    p_hsp = 128 * T_CORE

    def issue_rb(tc):
        slot3 = tc % 3
        sem = "RQ" if tc % 2 == 0 else "RQB"
        if tc == 0:
            B.wait("sync", h_spill[ICN - 1])
        if rb_fence[tc % 2] is not None:
            B.wait("sync", rb_fence[tc % 2])
        if tc >= 3:
            B.wait("sync", rb_reader[tc - 3])
        src_ap = bass.AP(hsp, tc * 128, [[T_CORE, 128], [p_hsp, ICN], [1, 128]])
        dst_ap = bass.AP(HRD, slot3 * ICN * 128,
                         [[_pstep(HRD), 128], [128, ICN], [1, 128]])
        B.emit("sync", lambda e, s=src_ap, d=dst_ap: e.dma_start(out=d, in_=s),
               inc=(sem, 16))
        sig = B.emit("sync", lambda e, tc=tc, slot3=slot3: e.dma_start(
            out=X1RD[:, slot3 * 1024:(slot3 + 1) * 1024],
            in_=x1sp.ap()[tc * 128:(tc + 1) * 128, :]), inc=(sem, 16))
        rb_fence[tc % 2] = sig
        rb_sig[tc] = sig

    issue_rb(0)
    issue_rb(1)
    for tc in range(TC):
        slot3 = tc % 3
        par = tc % 2
        if tc + 2 < TC:
            issue_rb(tc + 2)
        for halfc in range(2):
            bi, pb = pool.acquire()
            B.wait("tensor", rb_sig[tc])
            if tc == 0 and halfc == 0:
                B.wait("tensor", w2b_sig[ICN - 1])
                B.wait("tensor", b2b)
                B.wait("tensor", ones_sig)
            for ic in range(ICN):
                fn = lambda e, ic=ic, slot3=slot3, halfc=halfc, pb=pb: e.matmul(
                    pb[:, :],
                    HRD[:, slot3 * ICN * 128 + ic * 128: slot3 * ICN * 128 + (ic + 1) * 128],
                    W2B[:, ic * 1024 + halfc * 512: ic * 1024 + (halfc + 1) * 512],
                    start=(ic == 0), stop=False)
                B.emit("tensor", fn)
            tsig = B.emit("tensor", lambda e, halfc=halfc, pb=pb: e.matmul(
                pb[:, :], ONES1[:, :],
                B2B[:, halfc * 512:(halfc + 1) * 512],
                start=False, stop=True), inc=("TE", 1))
            B.wait("vector", tsig)
            B.wait("vector", rb_sig[tc])
            if tc >= 2 and halfc == 0:
                B.wait("vector", out_sig.get(tc - 2))
            vsig = B.emit("vector", lambda e, pb=pb, par=par, slot3=slot3, halfc=halfc, tc=tc:
                          e.scalar_tensor_tensor(
                              out=R2[:, par * 1024 + halfc * 512:
                                     par * 1024 + (halfc + 1) * 512],
                              in0=pb[:, :], scalar=1.0,
                              in1=X1RD[:, slot3 * 1024 + halfc * 512:
                                       slot3 * 1024 + (halfc + 1) * 512],
                              op0=OP.mult, op1=OP.add,
                              accum_out=S2H[:, 2 * tc + halfc: 2 * tc + halfc + 1]),
                          inc=("VE", 1))
            pool.release(bi, vsig)
        rb_reader[tc] = vsig
        B.wait("vector", vsig)
        n1 = B.emit("vector", lambda e, tc=tc: e.tensor_add(
            NEGM2[:, tc:tc + 1], S2H[:, 2 * tc:2 * tc + 1],
            S2H[:, 2 * tc + 1:2 * tc + 2]), inc=("VE", 1))
        B.wait("vector", n1)
        n2 = B.emit("vector", lambda e, tc=tc: e.tensor_scalar(
            out=NEGM2[:, tc:tc + 1], in0=NEGM2[:, tc:tc + 1],
            scalar1=-1.0 / D, scalar2=None, op0=OP.mult), inc=("VE", 1))
        B.wait("scalar", n2)
        sq = B.emit("scalar", lambda e, tc=tc, par=par: e.activation(
            out=SQ2[:, :], in_=R2[:, par * 1024:(par + 1) * 1024],
            func=AF.Square, bias=NEGM2[:, tc:tc + 1], scale=1.0,
            accum_out=VARP2[:, tc:tc + 1]), inc=("SC", 1))
        B.wait("vector", sq)
        n3 = B.emit("vector", lambda e, tc=tc: e.tensor_scalar(
            out=VARP2[:, tc:tc + 1], in0=VARP2[:, tc:tc + 1],
            scalar1=1.0 / D, scalar2=LN_EPS, op0=OP.mult, op1=OP.add), inc=("VE", 1))
        B.wait("scalar", n3)
        n4 = B.emit("scalar", lambda e, tc=tc: e.activation(
            out=RSTD2[:, tc:tc + 1], in_=VARP2[:, tc:tc + 1],
            func=AF.Sqrt), inc=("SC", 1))
        B.wait("vector", n4)
        n5 = B.emit("vector", lambda e, tc=tc: e.reciprocal(
            out=RSTD2[:, tc:tc + 1], in_=RSTD2[:, tc:tc + 1]), inc=("VE", 1))
        B.wait("vector", n5)
        n6 = B.emit("vector", lambda e, tc=tc: e.tensor_mul(
            NMRS2[:, tc:tc + 1], NEGM2[:, tc:tc + 1], RSTD2[:, tc:tc + 1]),
            inc=("VE", 1))
        B.wait("scalar", n6)
        so = B.emit("scalar", lambda e, tc=tc, par=par: e.activation(
            out=R2[:, par * 1024:(par + 1) * 1024],
            in_=R2[:, par * 1024:(par + 1) * 1024],
            func=AF.Identity, scale=RSTD2[:, tc:tc + 1],
            bias=NMRS2[:, tc:tc + 1]), inc=("SC", 1))
        B.wait("gpsimd", so)
        out_sig[tc] = B.emit("gpsimd", lambda e, tc=tc, par=par: e.dma_start(
            out=out_d[tc * 128:(tc + 1) * 128, :],
            in_=R2[:, par * 1024:(par + 1) * 1024]), inc=("GQ", 16))
        fence("gpsimd", "GQ")

    B.wait("gpsimd", out_sig[TC - 1])

    B.run()
    stack.close()
    return nc


_NC_CACHE = {}


def _get_nc(T_CORE=2048):
    if T_CORE not in _NC_CACHE:
        _NC_CACHE[T_CORE] = build_nc(T_CORE)
    return _NC_CACHE[T_CORE]


def make_in_maps(inputs, T_CORE=2048):
    import jax

    x = np.ascontiguousarray(np.asarray(inputs["L0_attn_x"], dtype=np.float32)
                             .reshape(-1, D))
    n_tok = x.shape[0]
    assert n_tok == NCORES * T_CORE

    cpu = jax.devices("cpu")[0]
    with jax.default_device(cpu):
        key = jax.random.key(42, impl="threefry2x32")
        kA, kB, kC, kdt = jax.random.split(key, 4)
        eps = {}
        for name, kk in (("A", kA), ("Bp", kB), ("C", kC), ("dt", kdt)):
            kw, kb = jax.random.split(kk)
            eps[(name, "w")] = np.asarray(jax.random.normal(kw, (D, D), "float32"))
            eps[(name, "b")] = np.asarray(jax.random.normal(kb, (D,), "float32"))

    import ml_dtypes
    common = {}
    percore = {c: {} for c in range(NCORES)}
    for P in ("A", "dt", "Bp", "C"):
        wmu = np.asarray(inputs[f"L0_attn_{P}_wmu"], dtype=np.float32)
        wlv = np.asarray(inputs[f"L0_attn_{P}_wlv"], dtype=np.float32)
        wmuT = np.ascontiguousarray(wmu.T)
        wlvT = np.ascontiguousarray(wlv.T)
        epsT = np.ascontiguousarray(eps[(P, "w")].T.astype(ml_dtypes.bfloat16))
        for c in range(NCORES):
            percore[c][f"{P}_wmuT_my"] = np.ascontiguousarray(
                wmuT[c * 128:(c + 1) * 128])
            percore[c][f"{P}_wlvT_my"] = np.ascontiguousarray(
                wlvT[c * 128:(c + 1) * 128])
            percore[c][f"{P}_epsT_my"] = np.ascontiguousarray(
                epsT[c * 128:(c + 1) * 128])
        bmu = np.asarray(inputs[f"L0_attn_{P}_bmu"], dtype=np.float32)
        blv = np.asarray(inputs[f"L0_attn_{P}_blv"], dtype=np.float32)
        common[f"{P}_bmu8"] = np.ascontiguousarray(bmu.reshape(8, 128).T)
        common[f"{P}_blv8"] = np.ascontiguousarray(blv.reshape(8, 128).T)
        common[f"{P}_epsb8"] = np.ascontiguousarray(eps[(P, "b")].reshape(8, 128).T)
    common["W1"] = np.ascontiguousarray(np.asarray(inputs["L0_ffn_W1"], np.float32))
    common["b1_32"] = np.ascontiguousarray(
        np.asarray(inputs["L0_ffn_b1"], np.float32).reshape(ICN, 128).T)
    common["W2"] = np.ascontiguousarray(np.asarray(inputs["L0_ffn_W2"], np.float32))
    common["b2row"] = np.ascontiguousarray(
        np.asarray(inputs["L0_ffn_b2"], np.float32).reshape(1, D))
    common["eye"] = np.eye(128, dtype=np.float32)

    in_maps = []
    for c in range(NCORES):
        ms = np.zeros((128, 8), dtype=np.float32)
        ms[:, (c + 1) % 8] = 1.0
        rr = np.zeros((128, 8), dtype=np.float32)
        if c % 2 == 1:
            rr[:, c] = 1.0
        m = dict(common)
        m.update(percore[c])
        m["x"] = np.ascontiguousarray(x[c * T_CORE:(c + 1) * T_CORE])
        m["msend"] = ms
        m["rread"] = rr
        in_maps.append(m)
    return in_maps


def kernel(**inputs):
    T_CORE = 2048
    nc = _get_nc(T_CORE)
    in_maps = make_in_maps(inputs, T_CORE)
    res = run_bass_kernel_spmd(nc, in_maps, core_ids=list(range(NCORES)))
    out = np.concatenate([np.asarray(res.results[c]["out"]) for c in range(NCORES)],
                         axis=0)
    return out.reshape(4, 4096, 1024).astype(np.float32)


# version 22
# speedup vs baseline: 1.0698x; 1.0147x over previous
import numpy as np

import concourse.bass as bass
import concourse.mybir as mybir
from concourse.bass_utils import run_bass_kernel_spmd

F32 = mybir.dt.float32
BF16 = mybir.dt.bfloat16
AF = mybir.ActivationFunctionType
OP = mybir.AluOpType
AX = mybir.AxisListType.X

NCORES = 8
D = 1024
KD = 8
MC = 8
INTER = 4096
ICN = 32
LN_EPS = 1e-5


def _pstep(t):
    return t.ap().ap[0][0]


class Builder:

    ENGINES = ("sync", "tensor", "scalar", "vector", "gpsimd")

    def __init__(self, nc):
        self.nc = nc
        self.thunks = {e: [] for e in self.ENGINES}
        self.sems = {}
        self.counts = {}

    def sem(self, name):
        handle = self.nc.semaphore(name).__enter__()
        self.sems[name] = handle
        self.counts[name] = 0
        return handle

    def emit(self, eng, fn, inc=None):
        sig = None
        if inc is not None:
            semname, k = inc
            self.counts[semname] += k
            sig = (semname, self.counts[semname])
            handle = self.sems[semname]

            def wrapped(e, fn=fn, handle=handle, k=k):
                fn(e).then_inc(handle, k)
            self.thunks[eng].append(wrapped)
        else:
            self.thunks[eng].append(fn)
        return sig

    def wait(self, eng, sig):
        if sig is None:
            return
        semname, val = sig
        handle = self.sems[semname]
        self.thunks[eng].append(lambda e, h=handle, v=val: e.wait_ge(h, v))

    def run(self):
        nc = self.nc
        with nc.Block() as block:
            @block.sync
            def _(eng):
                for t in self.thunks["sync"]:
                    t(eng)

            @block.tensor
            def _(eng):
                for t in self.thunks["tensor"]:
                    t(eng)

            @block.scalar
            def _(eng):
                for t in self.thunks["scalar"]:
                    t(eng)

            @block.vector
            def _(eng):
                for t in self.thunks["vector"]:
                    t(eng)

            @block.gpsimd
            def _(eng):
                for t in self.thunks["gpsimd"]:
                    t(eng)


class BankPool:

    def __init__(self, B, banks):
        self.B = B
        self.banks = banks
        self.free_at = [None] * len(banks)
        self.rr = 0

    def acquire(self):
        i = self.rr % len(self.banks)
        self.rr += 1
        if self.free_at[i] is not None:
            self.B.wait("tensor", self.free_at[i])
            self.free_at[i] = None
        return i, self.banks[i]

    def release(self, i, sig):
        self.free_at[i] = sig


def build_nc(T_CORE=2048):
    TC = T_CORE // 128
    HALF = T_CORE // 2
    TBH = HALF // 512
    NU = 2 * MC

    nc = bass.Bass("TRN2", target_bir_lowering=False)

    x_d = nc.dram_tensor("x", [T_CORE, D], F32, kind="ExternalInput").ap()
    proj_d = {}
    for P in ("A", "dt", "Bp", "C"):
        proj_d[(P, "wmuT")] = nc.dram_tensor(f"{P}_wmuT_my", [128, D], F32,
                                             kind="ExternalInput").ap()
        proj_d[(P, "wlvT")] = nc.dram_tensor(f"{P}_wlvT_my", [128, D], F32,
                                             kind="ExternalInput").ap()
        proj_d[(P, "epsT")] = nc.dram_tensor(f"{P}_epsT_my", [128, D], BF16,
                                             kind="ExternalInput").ap()
        for part in ("bmu8", "blv8", "epsb8"):
            proj_d[(P, part)] = nc.dram_tensor(f"{P}_{part}", [128, 8], F32,
                                               kind="ExternalInput").ap()
    wag_in = nc.dram_tensor("wag_in", [128, 3072], BF16)
    wag_out = nc.dram_tensor("wag_out", [8, 128, 3072], BF16, addr_space="Shared")
    w1_d = nc.dram_tensor("W1", [D, INTER], F32, kind="ExternalInput").ap()
    b1_d = nc.dram_tensor("b1_32", [128, ICN], F32, kind="ExternalInput").ap()
    w2_d = nc.dram_tensor("W2", [INTER, D], F32, kind="ExternalInput").ap()
    b2_d = nc.dram_tensor("b2row", [1, D], F32, kind="ExternalInput").ap()
    eye_d = nc.dram_tensor("eye", [128, 128], F32, kind="ExternalInput").ap()
    msend_d = nc.dram_tensor("msend", [128, 8], F32, kind="ExternalInput").ap()
    rread_d = nc.dram_tensor("rread", [128, 8], F32, kind="ExternalInput").ap()
    out_d = nc.dram_tensor("out", [T_CORE, D], F32, kind="ExternalOutput").ap()

    hsp = nc.dram_tensor("hsp", [ICN, 128, T_CORE], BF16)
    x1sp = nc.dram_tensor("x1sp", [T_CORE, D], F32)
    cc_in = [nc.dram_tensor(f"cc_in{m}", [128, 8], F32) for m in range(MC)]
    cc_out = [nc.dram_tensor(f"cc_out{m}", [128, 8], F32) for m in range(MC)]

    from contextlib import ExitStack
    stack = ExitStack()
    en = stack.enter_context

    B = Builder(nc)
    for s in ("SQ", "SQB", "SQC", "GQ", "GQA", "SCQ", "TE", "SC", "VE", "CC",
              "FQ", "FQB", "FQC", "FQD", "W2Q", "W2QB", "RQ", "RQB"):
        B.sem(s)

    def fence(eng, semname):
        B.wait(eng, (semname, B.counts[semname]))

    XB = en(nc.sbuf_tensor("XB", [128, KD * T_CORE], BF16))
    EYE = en(nc.sbuf_tensor("EYE", [128, 128], F32))
    SM = en(nc.sbuf_tensor("SM", [128, 8], F32))
    SR = en(nc.sbuf_tensor("SR", [128, 8], F32))
    BRAW = en(nc.sbuf_tensor("BRAW", [128, 8 * 12], F32))
    BS = en(nc.sbuf_tensor("BS", [128, 8], F32))
    BB = en(nc.sbuf_tensor("BB", [128, 8], F32))
    BC4 = en(nc.sbuf_tensor("BC4", [128, 8], F32))
    BT = en(nc.sbuf_tensor("BT", [128, 8 * 3], F32))
    B1T = en(nc.sbuf_tensor("B1T", [128, ICN], F32))
    CTB = en(nc.sbuf_tensor("CTB", [128, 8 * MC], F32))
    ARB = en(nc.sbuf_tensor("ARB", [128, 8 * MC], F32))
    INI = en(nc.sbuf_tensor("INI", [128, MC], F32))
    ZZ = en(nc.sbuf_tensor("ZZ", [128, 8], F32))
    HCC = en(nc.sbuf_tensor("HCC", [128, 1], F32))
    LN4C = en(nc.sbuf_tensor("LN4C", [128, 1], F32))
    SUMS = en(nc.sbuf_tensor("SUMS", [128, TC], F32))
    SQS = en(nc.sbuf_tensor("SQS", [128, TC], F32))
    NEGM = en(nc.sbuf_tensor("NEGM", [128, TC], F32))
    VARP = en(nc.sbuf_tensor("VARP", [128, TC], F32))
    RSTD = en(nc.sbuf_tensor("RSTD", [128, TC], F32))
    NMRS = en(nc.sbuf_tensor("NMRS", [128, TC], F32))

    banks = [en(nc.psum_tensor(f"pb{i}", [128, 512], F32)) for i in range(8)]
    pool = BankPool(B, banks)

    stackAB = ExitStack()
    enB = stackAB.enter_context
    WS = enB(nc.sbuf_tensor("WS", [128, KD * 1024], BF16))
    WB = enB(nc.sbuf_tensor("WB", [128, KD * 1024], BF16))
    WC = enB(nc.sbuf_tensor("WC", [128, KD * 1024], BF16))
    RT = enB(nc.sbuf_tensor("RT", [128, TC * 1024], F32))
    EBUF = enB(nc.sbuf_tensor("EBUF", [128, 3 * HALF], F32))
    BXBUF = enB(nc.sbuf_tensor("BXBUF", [128, 4 * HALF], F32))
    HBUF = enB(nc.sbuf_tensor("HBUF", [128, 3 * HALF], F32))
    XCOL = enB(nc.sbuf_tensor("XCOL", [128, 1024], F32))
    WST = enB(nc.sbuf_tensor("WST", [128, 2 * 1024], F32))
    WSTE = enB(nc.sbuf_tensor("WSTE", [128, 1024], BF16))
    UEX = enB(nc.sbuf_tensor("UEX", [128, 1024], F32))

    pR = _pstep(RT)
    pH = _pstep(HBUF)

    x_sigs = []

    def emit_x_loads():
        for tc in range(TC):
            sig = B.emit("gpsimd", lambda e, tc=tc: e.dma_start(
                out=RT[:, tc * 1024:(tc + 1) * 1024],
                in_=x_d[tc * 128:(tc + 1) * 128, :]), inc=("GQ", 16))
            x_sigs.append(sig)
            if tc % 4 == 3:
                fence("gpsimd", "GQ")

    tiny = [(EYE[:, :], eye_d), (SM[:, :], msend_d), (SR[:, :], rread_d),
            (B1T[:, :], b1_d)]
    for i, (P, part) in enumerate([(P, p) for P in ("A", "dt", "Bp", "C")
                                   for p in ("bmu8", "blv8", "epsb8")]):
        tiny.append((BRAW[:, i * 8:(i + 1) * 8], proj_d[(P, part)]))
    tiny_sig = None
    for dst, src in tiny:
        tiny_sig = B.emit("gpsimd", lambda e, dst=dst, src=src: e.dma_start(
            out=dst, in_=src[:, :]), inc=("GQ", 16))
    fence("gpsimd", "GQ")


    B.wait("tensor", tiny_sig)
    for w in range(70):
        bi, pb = pool.acquire()
        for j in range(4):
            fn = lambda e, pb=pb, j=j: e.transpose(
                pb[:, j * 128:(j + 1) * 128], EYE[:, :], EYE[:, :])
            if j == 3:
                wsig = B.emit("tensor", fn, inc=("TE", 1))
            else:
                B.emit("tensor", fn)
        pool.release(bi, wsig)

    ln4_sig = B.emit("vector", lambda e: e.memset(LN4C[:, :], float(np.log(4.0))),
                     inc=("VE", 1))

    prep_dma = {}
    last_prep = [None]
    slot_reader = {0: None, 1: None}
    eps_reader = [None]
    for Pi, P in enumerate(("A", "dt", "Bp", "C")):
        slot = Pi % 2
        WSTs = (WST, BXBUF)[slot]
        qsem = ("SQ", "SQB")[slot]
        B.wait("sync", slot_reader[slot])
        for j, part in enumerate(("wmuT", "wlvT")):
            B.emit("sync", lambda e, P=P, part=part, j=j, WSTs=WSTs: e.dma_start(
                out=WSTs[:, j * 1024:(j + 1) * 1024],
                in_=proj_d[(P, part)][:, :]), inc=(qsem, 16))
        B.wait("sync", eps_reader[0])
        sig = B.emit("sync", lambda e, P=P: e.dma_start(
            out=WSTE[:, :], in_=proj_d[(P, "epsT")][:, :]), inc=(qsem, 16))
        prep_dma[P] = sig

        mode = {"A": "hold", "dt": "ws", "Bp": "wb", "C": "wc4"}[P]
        B.wait("scalar", prep_dma[P])
        B.wait("scalar", last_prep[0])
        if P == "C":
            B.wait("scalar", ln4_sig)
        for half in range(2):
            fn = lambda e, half=half, P=P, WSTs=WSTs: e.activation(
                out=UEX[:, half * 512:(half + 1) * 512],
                in_=WSTs[:, 1024 + half * 512: 1024 + (half + 1) * 512],
                func=AF.Exp, scale=0.5,
                **({"bias": LN4C[:, 0:1]} if P == "C" else {}))
            s1 = B.emit("scalar", fn, inc=("SC", 1))
        B.wait("vector", s1)
        for half in range(2):
            m1 = B.emit("vector", lambda e, half=half: e.tensor_mul(
                UEX[:, half * 512:(half + 1) * 512],
                UEX[:, half * 512:(half + 1) * 512],
                WSTE[:, half * 512:(half + 1) * 512]), inc=("VE", 1))
        B.wait("vector", m1)
        eps_reader[0] = m1
        for half in range(2):
            hs = half * 512
            if mode == "hold":
                sig = B.emit("vector", lambda e, hs=hs, WSTs=WSTs: e.tensor_add(
                    XCOL[:, hs:hs + 512], UEX[:, hs:hs + 512],
                    WSTs[:, hs:hs + 512]), inc=("VE", 1))
            elif mode == "ws":
                a1 = B.emit("vector", lambda e, hs=hs, WSTs=WSTs: e.tensor_add(
                    UEX[:, hs:hs + 512], UEX[:, hs:hs + 512],
                    WSTs[:, hs:hs + 512]), inc=("VE", 1))
                B.wait("vector", a1)
                sig = B.emit("vector", lambda e, hs=hs: e.tensor_add(
                    WS[:, hs:hs + 512], XCOL[:, hs:hs + 512],
                    UEX[:, hs:hs + 512]), inc=("VE", 1))
            elif mode == "wb":
                sig = B.emit("vector", lambda e, hs=hs, WSTs=WSTs: e.tensor_add(
                    WB[:, hs:hs + 512], UEX[:, hs:hs + 512],
                    WSTs[:, hs:hs + 512]), inc=("VE", 1))
            else:
                sig = B.emit("vector", lambda e, hs=hs, WSTs=WSTs: e.scalar_tensor_tensor(
                    out=WC[:, hs:hs + 512], in0=WSTs[:, hs:hs + 512],
                    scalar=4.0, in1=UEX[:, hs:hs + 512], op0=OP.mult, op1=OP.add),
                    inc=("VE", 1))
        last_prep[0] = sig
        slot_reader[slot] = sig
        if P == "C":
            wbc_con = sig

    B.wait("gpsimd", wbc_con)
    B.emit("gpsimd", lambda e: e.dma_start(
        out=wag_in.ap()[:, 0:1024], in_=WS[:, 0:1024]), inc=("GQ", 16))
    B.emit("gpsimd", lambda e: e.dma_start(
        out=wag_in.ap()[:, 1024:2048], in_=WB[:, 0:1024]), inc=("GQ", 16))
    agd = B.emit("gpsimd", lambda e: e.dma_start(
        out=wag_in.ap()[:, 2048:3072], in_=WC[:, 0:1024]), inc=("GQ", 16))
    B.wait("gpsimd", agd)
    ag1 = B.emit("gpsimd", lambda e: e.collective_compute(
        "AllGather", OP.bypass, replica_groups=[list(range(NCORES))],
        ins=[wag_in.ap().opt()], outs=[wag_out.ap().opt()]), inc=("CC", 1))

    emit_x_loads()

    for q in range(TC // 4):
        for k in range(KD):
            bi, pb = pool.acquire()
            B.wait("tensor", x_sigs[4 * q + 3])
            if q == 0 and k == 0:
                B.wait("tensor", tiny_sig)
            for j in range(4):
                tc = 4 * q + j
                fn = lambda e, tc=tc, k=k, pb=pb, j=j: e.transpose(
                    pb[:, j * 128:(j + 1) * 128],
                    RT[:, tc * 1024 + k * 128: tc * 1024 + k * 128 + 128],
                    EYE[:, :])
                if j == 3:
                    tsig = B.emit("tensor", fn, inc=("TE", 1))
                else:
                    B.emit("tensor", fn)
            B.wait("scalar", tsig)
            esig = B.emit("scalar", lambda e, k=k, q=q, pb=pb: e.copy(
                out=XB[:, k * T_CORE + q * 512: k * T_CORE + (q + 1) * 512],
                in_=pb[:, :]), inc=("SC", 1))
            pool.release(bi, esig)
    xb_done = esig

    B.wait("sync", ag1)
    pWS, pWB, pWC = _pstep(WS), _pstep(WB), _pstep(WC)
    ws_sig_all = B.emit("sync", lambda e: e.dma_start(
        out=bass.AP(WS, 0, [[pWS, 128], [1024, 8], [1, 1024]]),
        in_=bass.AP(wag_out, 0, [[3072, 128], [128 * 3072, 8], [1, 1024]])),
        inc=("SQ", 16))
    wb_sig_all = B.emit("sync", lambda e: e.dma_start(
        out=bass.AP(WB, 0, [[pWB, 128], [1024, 8], [1, 1024]]),
        in_=bass.AP(wag_out, 1024, [[3072, 128], [128 * 3072, 8], [1, 1024]])),
        inc=("SQC", 16))
    wc_sig_all = B.emit("sync", lambda e: e.dma_start(
        out=bass.AP(WC, 0, [[pWC, 128], [1024, 8], [1, 1024]]),
        in_=bass.AP(wag_out, 2048, [[3072, 128], [128 * 3072, 8], [1, 1024]])),
        inc=("SQB", 16))
    fence("sync", "SQ")
    fence("sync", "SQC")
    fence("sync", "SQB")
    B.wait("scalar", tiny_sig)
    prev_v = None
    for i, dst in enumerate((None, None, BB, BC4)):
        B.wait("scalar", prev_v)
        s1 = B.emit("scalar", lambda e, i=i: e.activation(
            out=BT[:, 0:8], in_=BRAW[:, (3 * i + 1) * 8:(3 * i + 2) * 8],
            func=AF.Exp, scale=0.5), inc=("SC", 1))
        B.wait("vector", s1)
        tgt = BT[:, 8:16] if i == 0 else (BT[:, 16:24] if i == 1 else dst[:, :])
        v1 = B.emit("vector", lambda e, i=i, tgt=tgt: e.tensor_mul(
            tgt, BT[:, 0:8], BRAW[:, (3 * i + 2) * 8:(3 * i + 3) * 8]),
            inc=("VE", 1))
        B.wait("vector", v1)
        v2 = B.emit("vector", lambda e, i=i, tgt=tgt: e.tensor_add(
            tgt, tgt, BRAW[:, (3 * i) * 8:(3 * i + 1) * 8]), inc=("VE", 1))
        B.wait("vector", v2)
        prev_v = v2
    v = B.emit("vector", lambda e: e.tensor_add(
        BS[:, :], BT[:, 8:16], BT[:, 16:24]), inc=("VE", 1))
    B.wait("vector", v)
    bias_done = B.emit("vector", lambda e: e.tensor_scalar(
        out=BC4[:, :], in0=BC4[:, :], scalar1=4.0, scalar2=None,
        op0=OP.mult), inc=("VE", 1))

    ws_sig = {0: ws_sig_all, 1: ws_sig_all}
    wb_sig_h = {0: wb_sig_all, 1: wb_sig_all}
    wc_sig_h = {0: wc_sig_all, 1: wc_sig_all}
    prev_prep = [sig]

    head_state = {}

    def emit_head(u):
        m, h = u // 2, u % 2
        sl = u % 3
        xsl = u % 2
        bsl = u % 4
        st = {}
        for q2 in range(TBH):
            bi, pb = pool.acquire()
            for j in range(4):
                tc = (TC // 2) * h + 4 * q2 + j
                fn = lambda e, tc=tc, m=m, pb=pb, j=j: e.transpose(
                    pb[:, j * 128:(j + 1) * 128],
                    RT[:, tc * 1024 + m * 128: tc * 1024 + m * 128 + 128],
                    EYE[:, :])
                if j == 3:
                    tsig = B.emit("tensor", fn, inc=("TE", 1))
                else:
                    B.emit("tensor", fn)
            B.wait("scalar", tsig)
            if u == 0 and q2 == 0:
                B.wait("scalar", wbc_con)
            if u >= 1:
                B.wait("scalar", head_state.get(("xcol_reader", u - 1)))
            xsig = B.emit("scalar", lambda e, q2=q2, pb=pb: e.copy(
                out=XCOL[:, q2 * 512:(q2 + 1) * 512],
                in_=pb[:, :]), inc=("SC", 1))
            pool.release(bi, xsig)
            st[("xcol", q2)] = xsig
        if u == 0:
            B.wait("tensor", xb_done)
        if m % 4 == 0 and h == 0:
            B.wait("tensor", ws_sig[m // 4])
        sbanks = [pool.acquire() for _ in range(TBH)]
        for k in range(KD):
            for tb in range(TBH):
                fn = lambda e, k=k, m=m, h=h, tb=tb, pb=sbanks[tb][1]: e.matmul(
                    pb[:, :],
                    WS[:, k * 1024 + m * 128: k * 1024 + m * 128 + 128],
                    XB[:, k * T_CORE + h * HALF + tb * 512:
                       k * T_CORE + h * HALF + (tb + 1) * 512],
                    start=(k == 0), stop=(k == KD - 1))
                if k == KD - 1:
                    st[("smm", tb)] = B.emit("tensor", fn, inc=("TE", 1))
                else:
                    B.emit("tensor", fn)
        for tb in range(TBH):
            bi, pb = sbanks[tb]
            B.wait("vector", st[("smm", tb)])
            if u == 0 and tb == 0:
                B.wait("vector", bias_done)
            if u >= 3:
                B.wait("vector", head_state.get(("ebuf_reader", u - 3)))
            vc = B.emit("vector", lambda e, pb=pb, m=m, sl=sl, tb=tb: e.tensor_scalar(
                out=EBUF[:, sl * HALF + tb * 512: sl * HALF + (tb + 1) * 512],
                in0=pb[:, :], scalar1=BS[:, m:m + 1], scalar2=85.0,
                op0=OP.add, op1=OP.min), inc=("VE", 1))
            pool.release(bi, vc)
            st[("e1", tb)] = vc
        for tb in range(TBH):
            B.wait("scalar", st[("e1", tb)])
            s1 = B.emit("scalar", lambda e, sl=sl, tb=tb: e.activation(
                out=EBUF[:, sl * HALF + tb * 512: sl * HALF + (tb + 1) * 512],
                in_=EBUF[:, sl * HALF + tb * 512: sl * HALF + (tb + 1) * 512],
                func=AF.Exp), inc=("SC", 1))
            st[("e1b", tb)] = s1
        B.wait("scalar", st[("e1b", TBH - 1)])
        for tb in range(TBH):
            s2 = B.emit("scalar", lambda e, sl=sl, tb=tb: e.activation(
                out=EBUF[:, sl * HALF + tb * 512: sl * HALF + (tb + 1) * 512],
                in_=EBUF[:, sl * HALF + tb * 512: sl * HALF + (tb + 1) * 512],
                func=AF.Exp, scale=-1.0), inc=("SC", 1))
        st["E"] = s2
        if m % 4 == 0 and h == 0:
            B.wait("tensor", wb_sig_h[m // 4])
        bbanks = [pool.acquire() for _ in range(TBH)]
        bsig = {}
        for k in range(KD):
            for tb in range(TBH):
                fn = lambda e, k=k, m=m, h=h, tb=tb, pb=bbanks[tb][1]: e.matmul(
                    pb[:, :],
                    WB[:, k * 1024 + m * 128: k * 1024 + m * 128 + 128],
                    XB[:, k * T_CORE + h * HALF + tb * 512:
                       k * T_CORE + h * HALF + (tb + 1) * 512],
                    start=(k == 0), stop=(k == KD - 1))
                if k == KD - 1:
                    bsig[tb] = B.emit("tensor", fn, inc=("TE", 1))
                else:
                    B.emit("tensor", fn)
        for tb in range(TBH):
            bi, pb = bbanks[tb]
            B.wait("vector", bsig[tb])
            B.wait("vector", st[("xcol", tb)])
            if u >= 4:
                B.wait("vector", head_state.get(("bx_reader", u - 4)))
            vsig = B.emit("vector", lambda e, pb=pb, m=m, bsl=bsl, tb=tb: e.scalar_tensor_tensor(
                out=BXBUF[:, bsl * HALF + tb * 512: bsl * HALF + (tb + 1) * 512],
                in0=pb[:, :], scalar=BB[:, m:m + 1],
                in1=XCOL[:, tb * 512:(tb + 1) * 512],
                op0=OP.add, op1=OP.mult), inc=("VE", 1))
            pool.release(bi, vsig)
        head_state[("xcol_reader", u)] = vsig
        B.wait("vector", st["E"])
        B.wait("vector", vsig)
        if h == 0:
            init = 0.0
        else:
            B.wait("vector", head_state[u - 1]["scan1"])
            init = bass.AP(HBUF, ((u - 1) % 3) * HALF + HALF - 1, [[pH, 128], [1, 1]])
        s1sig = B.emit("vector", lambda e, sl=sl, bsl=bsl, init=init: e.tensor_tensor_scan(
            out=HBUF[:, sl * HALF:(sl + 1) * HALF],
            data0=EBUF[:, sl * HALF:(sl + 1) * HALF],
            data1=BXBUF[:, bsl * HALF:(bsl + 1) * HALF],
            initial=init, op0=OP.mult, op1=OP.add), inc=("VE", 1))
        st["scan1"] = s1sig
        if h == 1:
            B.wait("vector", s1sig)
            bc = bass.AP(HBUF, sl * HALF + HALF - 1, [[pH, 128], [0, 8]])
            ctsig = B.emit("vector", lambda e, m=m, bc=bc: e.tensor_tensor(
                out=CTB[:, m * 8:(m + 1) * 8], in0=SM[:, :], in1=bc,
                op=OP.mult), inc=("VE", 1))
            B.wait("gpsimd", ctsig)
            d1 = B.emit("gpsimd", lambda e, m=m: e.dma_start(
                out=cc_in[m].ap(), in_=CTB[:, m * 8:(m + 1) * 8]), inc=("GQ", 16))
            B.wait("gpsimd", d1)
            ccsig = B.emit("gpsimd", lambda e, m=m: e.collective_compute(
                "AllReduce", OP.add,
                replica_groups=[list(range(NCORES))],
                ins=[cc_in[m].ap().opt()],
                outs=[cc_out[m].ap().opt()]), inc=("CC", 1))
            B.wait("gpsimd", ccsig)
            arsig = B.emit("gpsimd", lambda e, m=m: e.dma_start(
                out=ARB[:, m * 8:(m + 1) * 8], in_=cc_out[m].ap()), inc=("GQA", 16))
            st["ar"] = arsig
        head_state[u] = st
        return st

    def emit_tail_a(u):
        m, h = u // 2, u % 2
        sl = u % 3
        bsl = u % 4
        if h == 0:
            arsig = head_state[2 * m + 1]["ar"]
            B.wait("vector", arsig)
            z1 = B.emit("vector", lambda e, m=m: e.tensor_mul(
                ZZ[:, :], ARB[:, m * 8:(m + 1) * 8], SR[:, :]), inc=("VE", 1))
            B.wait("vector", z1)
            z2 = B.emit("vector", lambda e, m=m: e.reduce_sum(
                out=INI[:, m:m + 1], in_=ZZ[:, :], axis=AX), inc=("VE", 1))
            B.wait("vector", z2)
            init = bass.AP(INI, m, [[_pstep(INI), 128], [1, 1]])
        else:
            B.wait("vector", head_state[("hcc", m)])
            init = bass.AP(HCC, 0, [[_pstep(HCC), 128], [1, 1]])
        s2sig = B.emit("vector", lambda e, sl=sl, bsl=bsl, init=init: e.tensor_tensor_scan(
            out=HBUF[:, sl * HALF:(sl + 1) * HALF],
            data0=EBUF[:, sl * HALF:(sl + 1) * HALF],
            data1=BXBUF[:, bsl * HALF:(bsl + 1) * HALF],
            initial=init, op0=OP.mult, op1=OP.add), inc=("VE", 1))
        head_state[("ebuf_reader", u)] = s2sig
        if h == 0:
            B.wait("vector", s2sig)
            hcsig = B.emit("vector", lambda e, sl=sl: e.tensor_copy(
                out=HCC[:, 0:1],
                in_=HBUF[:, sl * HALF + HALF - 1: sl * HALF + HALF]), inc=("VE", 1))
            head_state[("hcc", m)] = hcsig
        if m % 4 == 0 and h == 0:
            B.wait("tensor", wc_sig_h[m // 4])
        cbanks = [pool.acquire() for _ in range(TBH)]
        csig = {}
        for k in range(KD):
            for tb in range(TBH):
                fn = lambda e, k=k, m=m, h=h, tb=tb, pb=cbanks[tb][1]: e.matmul(
                    pb[:, :],
                    WC[:, k * 1024 + m * 128: k * 1024 + m * 128 + 128],
                    XB[:, k * T_CORE + h * HALF + tb * 512:
                       k * T_CORE + h * HALF + (tb + 1) * 512],
                    start=(k == 0), stop=(k == KD - 1))
                if k == KD - 1:
                    csig[tb] = B.emit("tensor", fn, inc=("TE", 1))
                else:
                    B.emit("tensor", fn)
        head_state[("cmm", u)] = csig[TBH - 1]
        ysigs = []
        for tb in range(TBH):
            bi, pb = cbanks[tb]
            B.wait("vector", csig[tb])
            B.wait("vector", s2sig)
            ysig = B.emit("vector", lambda e, pb=pb, m=m, sl=sl, bsl=bsl, tb=tb: e.scalar_tensor_tensor(
                out=BXBUF[:, bsl * HALF + tb * 512: bsl * HALF + (tb + 1) * 512],
                in0=pb[:, :], scalar=BC4[:, m:m + 1],
                in1=HBUF[:, sl * HALF + tb * 512: sl * HALF + (tb + 1) * 512],
                op0=OP.add, op1=OP.mult), inc=("VE", 1))
            pool.release(bi, ysig)
            ysigs.append(ysig)
        head_state[("ysigs", u)] = ysigs

    def emit_tail_b(u):
        m, h = u // 2, u % 2
        bsl = u % 4
        ysigs = head_state[("ysigs", u)]
        for q2 in range(TBH):
            B.wait("tensor", ysigs[q2])
            bi, pb = pool.acquire()
            for j in range(4):
                fn = lambda e, bsl=bsl, q2=q2, j=j, pb=pb: e.transpose(
                    pb[:, j * 128:(j + 1) * 128],
                    BXBUF[:, bsl * HALF + q2 * 512 + j * 128:
                          bsl * HALF + q2 * 512 + (j + 1) * 128],
                    EYE[:, :])
                if j == 3:
                    tsig = B.emit("tensor", fn, inc=("TE", 1))
                else:
                    B.emit("tensor", fn)
            B.wait("vector", tsig)
            tc0 = (TC // 2) * h + 4 * q2
            rt_ap = bass.AP(RT, tc0 * 1024 + m * 128, [[pR, 128], [1024, 4], [1, 128]])
            pb_ap = bass.AP(pb, 0, [[_pstep(pb), 128], [128, 4], [1, 128]])
            rsig = B.emit("vector", lambda e, rt_ap=rt_ap, pb_ap=pb_ap: e.scalar_tensor_tensor(
                out=rt_ap, in0=pb_ap, scalar=1.0, in1=rt_ap,
                op0=OP.mult, op1=OP.add), inc=("VE", 1))
            pool.release(bi, rsig)
        head_state[("bx_reader", u)] = rsig
        head_state[("resid", u)] = rsig

    for s in range(NU + 3):
        if s < NU:
            emit_head(s)
        if 2 <= s < NU + 2:
            emit_tail_a(s - 2)
        if s >= 3:
            emit_tail_b(s - 3)

    B.wait("vector", head_state[("resid", NU - 1)])
    B.wait("vector", head_state[("xcol_reader", NU - 1)])
    pX = _pstep(XCOL)
    ln1_sigs = {}
    spill_sigs = []
    x1t_done = None
    for g in range(TC // 4):
        t0 = 4 * g
        for j in range(4):
            tc = t0 + j
            B.emit("vector", lambda e, tc=tc: e.bn_stats(
                out=XCOL[:, tc * 12: tc * 12 + 6],
                in_=RT[:, tc * 1024: tc * 1024 + 512]))
            s2 = B.emit("vector", lambda e, tc=tc: e.bn_stats(
                out=XCOL[:, tc * 12 + 6: tc * 12 + 12],
                in_=RT[:, tc * 1024 + 512: tc * 1024 + 1024]),
                inc=("VE", 1) if j == 3 else None)
        B.wait("vector", s2)
        for j in range(4):
            tc = t0 + j
            bnsig = B.emit("vector", lambda e, tc=tc: e.bn_aggr(
                out=XCOL[:, 192 + 2 * tc: 192 + 2 * tc + 2],
                in_=XCOL[:, tc * 12: tc * 12 + 12]),
                inc=("VE", 1) if j == 3 else None)
        B.wait("vector", bnsig)
        mean_g = bass.AP(XCOL, 192 + 8 * g, [[pX, 128], [2, 4]])
        var_g = bass.AP(XCOL, 193 + 8 * g, [[pX, 128], [2, 4]])
        ng = B.emit("vector", lambda e, mean_g=mean_g, t0=t0: e.tensor_scalar(
            out=NEGM[:, t0:t0 + 4], in0=mean_g, scalar1=-1.0, scalar2=None,
            op0=OP.mult), inc=("VE", 1))
        vg = B.emit("vector", lambda e, var_g=var_g, t0=t0: e.tensor_scalar(
            out=VARP[:, t0:t0 + 4], in0=var_g, scalar1=1.0, scalar2=LN_EPS,
            op0=OP.mult, op1=OP.add), inc=("VE", 1))
        B.wait("scalar", vg)
        sq1 = B.emit("scalar", lambda e, t0=t0: e.activation(
            out=RSTD[:, t0:t0 + 4], in_=VARP[:, t0:t0 + 4], func=AF.Sqrt),
            inc=("SC", 1))
        B.wait("vector", sq1)
        v2 = B.emit("vector", lambda e, t0=t0: e.reciprocal(
            out=RSTD[:, t0:t0 + 4], in_=RSTD[:, t0:t0 + 4]), inc=("VE", 1))
        B.wait("vector", v2)
        B.wait("vector", ng)
        v3 = B.emit("vector", lambda e, t0=t0: e.tensor_mul(
            NMRS[:, t0:t0 + 4], NEGM[:, t0:t0 + 4], RSTD[:, t0:t0 + 4]),
            inc=("VE", 1))
        B.wait("scalar", v3)
        for j in range(4):
            tc = t0 + j
            s = B.emit("scalar", lambda e, tc=tc: e.activation(
                out=RT[:, tc * 1024:(tc + 1) * 1024],
                in_=RT[:, tc * 1024:(tc + 1) * 1024],
                func=AF.Identity, scale=RSTD[:, tc:tc + 1],
                bias=NMRS[:, tc:tc + 1]), inc=("SC", 1))
            ln1_sigs[tc] = s
            B.wait("gpsimd", s)
            sp = B.emit("gpsimd", lambda e, tc=tc: e.dma_start(
                out=x1sp.ap()[tc * 128:(tc + 1) * 128, :],
                in_=RT[:, tc * 1024:(tc + 1) * 1024]), inc=("GQ", 16))
            spill_sigs.append(sp)
        for k in range(KD):
            bi, pb = pool.acquire()
            B.wait("tensor", ln1_sigs[t0 + 3])
            for j in range(4):
                tc = t0 + j
                fn = lambda e, tc=tc, k=k, pb=pb, j=j: e.transpose(
                    pb[:, j * 128:(j + 1) * 128],
                    RT[:, tc * 1024 + k * 128: tc * 1024 + k * 128 + 128],
                    EYE[:, :])
                if j == 3:
                    tsig = B.emit("tensor", fn, inc=("TE", 1))
                else:
                    B.emit("tensor", fn)
            B.wait("scalar", tsig)
            esig = B.emit("scalar", lambda e, k=k, g=g, pb=pb: e.copy(
                out=XB[:, k * T_CORE + g * 512: k * T_CORE + (g + 1) * 512],
                in_=pb[:, :]), inc=("SC", 1))
            pool.release(bi, esig)
            x1t_done = esig
    ln1_sigs = [ln1_sigs[tc] for tc in range(TC)]
    v3 = x1t_done

    stackAB.close()

    W1ST = en(nc.sbuf_tensor("W1ST", [128, 2 * 1024], F32))
    W1B = en(nc.sbuf_tensor("W1B", [128, 2 * KD * 1024], BF16))
    W2ST = en(nc.sbuf_tensor("W2ST", [128, 2 * 1024], F32))
    W2B = en(nc.sbuf_tensor("W2B", [128, ICN * 1024], BF16))
    HT1 = en(nc.sbuf_tensor("HT1", [128, 2 * T_CORE], BF16))
    HRD = en(nc.sbuf_tensor("HRD", [128, 3 * ICN * 128], BF16))
    X1RD = en(nc.sbuf_tensor("X1RD", [128, 3 * 1024], F32))
    R2 = en(nc.sbuf_tensor("R2", [128, 2 * 1024], F32))
    SQ2 = en(nc.sbuf_tensor("SQ2", [128, 1024], F32))
    B2B = en(nc.sbuf_tensor("B2B", [1, D], BF16))
    ONES1 = en(nc.sbuf_tensor("ONES1", [1, 128], BF16))
    S2H = en(nc.sbuf_tensor("S2H", [128, 2 * TC], F32))
    NEGM2 = en(nc.sbuf_tensor("NEGM2", [128, TC], F32))
    VARP2 = en(nc.sbuf_tensor("VARP2", [128, TC], F32))
    RSTD2 = en(nc.sbuf_tensor("RSTD2", [128, TC], F32))
    NMRS2 = en(nc.sbuf_tensor("NMRS2", [128, TC], F32))

    gate_c1 = head_state[("cmm", NU - 1)]
    gate_sigs = [x1t_done, spill_sigs[-1], head_state[("resid", NU - 1)],
                 ln1_sigs[-1], v3]
    B.wait("sync", gate_c1)
    b2sig = B.emit("sync", lambda e: e.dma_start(out=W2ST[0:1, 0:1024],
                                                 in_=b2_d[:, :]), inc=("SQ", 16))
    fence("sync", "SQ")
    B.wait("scalar", b2sig)
    b2b = B.emit("scalar", lambda e: e.copy(out=B2B[:, :], in_=W2ST[0:1, 0:1024]),
                 inc=("SC", 1))
    for g in gate_sigs:
        B.wait("vector", g)
    ones_sig = B.emit("vector", lambda e: e.memset(ONES1[:, :], 1.0), inc=("VE", 1))
    for g in gate_sigs:
        B.wait("scalar", g)

    w1_dma = {}
    w2_dma = {}
    w1b_sig = {}
    w2b_sig = {}
    w1b_reader = {}
    h_spill = {}
    ffn_fence = {0: None, 1: None, 2: None, 3: None}
    w2_fence = {0: None, 1: None}
    TBF = T_CORE // 512
    NQ = ICN // 8

    def w2_group(j):
        slot = j % 2
        sem = ("W2Q", "W2QB")[slot]
        if w2_fence[slot] is not None:
            B.wait("scalar", w2_fence[slot])
        sig = B.emit("scalar", lambda e, j=j, slot=slot: e.dma_start(
            out=W2ST[:, slot * 1024:slot * 1024 + 1024],
            in_=w2_d[j * 128:(j + 1) * 128, :]), inc=(sem, 16))
        w2_fence[slot] = sig
        return sig

    w1q_sig = {}
    w1q_reader = {}
    qconv = {}

    w1conv = {}

    def w1_quarter(q):
        qsl = q % 2
        sigs = []
        for k in range(KD):
            g = q * KD + k
            slot = g % 2
            sem = ("FQ", "FQB", "FQC", "FQD")[g % 4]
            if ffn_fence[g % 4] is not None:
                B.wait("sync", ffn_fence[g % 4])
            if g >= 2:
                B.wait("sync", w1conv.get(g - 2))
            dsig = B.emit("sync", lambda e, k=k, q=q, slot=slot: e.dma_start(
                out=W1ST[:, slot * 1024:(slot + 1) * 1024],
                in_=w1_d[k * 128:(k + 1) * 128, q * 1024:(q + 1) * 1024]),
                inc=(sem, 16))
            ffn_fence[g % 4] = dsig
            B.wait("vector", dsig)
            if (q, k) in qconv_war:
                B.wait("vector", qconv_war[(q, k)])
            csig = B.emit("vector", lambda e, k=k, qsl=qsl, slot=slot: e.tensor_copy(
                out=W1B[:, qsl * KD * 1024 + k * 1024: qsl * KD * 1024 + (k + 1) * 1024],
                in_=W1ST[:, slot * 1024:(slot + 1) * 1024]), inc=("VE", 1))
            w1conv[g] = csig
            sigs.append(csig)
        return sigs[-1]

    qconv_war = {}
    w2c_j = [0]
    h_spill = {}
    w1q_sig[0] = w1_quarter(0)
    for ic in range(ICN):
        q = ic // 8
        qsl = q % 2
        icr = ic % 8
        if icr == 0 and q + 1 < NQ:
            if q >= 1:
                B.wait("sync", w1q_reader[q - 1])
                for k in range(KD):
                    qconv_war[(q + 1, k)] = w1q_reader[q - 1]
            w1q_sig[q + 1] = w1_quarter(q + 1)
        if ic < ICN:
            j = ic
            if j == 0:
                B.wait("scalar", b2b)
            if j >= 2:
                B.wait("scalar", w2b_sig[j - 2])
            w2_dma[j] = w2_group(j)
        if w2c_j[0] <= ic:
            j = w2c_j[0]
            B.wait("vector", w2_dma[j])
            w2b_sig[j] = B.emit("vector", lambda e, j=j: e.tensor_copy(
                out=W2B[:, j * 1024:(j + 1) * 1024],
                in_=W2ST[:, (j % 2) * 1024:(j % 2) * 1024 + 1024]), inc=("VE", 1))
            w2c_j[0] += 1
        par = ic % 2
        fbanks = [pool.acquire() for _ in range(TBF)]
        if icr == 0:
            B.wait("tensor", w1q_sig[q])
        fsig = {}
        for k in range(KD):
            for tb in range(TBF):
                fn = lambda e, k=k, qsl=qsl, icr=icr, tb=tb, pb=fbanks[tb][1]: e.matmul(
                    pb[:, :],
                    W1B[:, qsl * KD * 1024 + k * 1024 + icr * 128:
                        qsl * KD * 1024 + k * 1024 + (icr + 1) * 128],
                    XB[:, k * T_CORE + tb * 512: k * T_CORE + (tb + 1) * 512],
                    start=(k == 0), stop=(k == KD - 1))
                if k == KD - 1:
                    fsig[tb] = B.emit("tensor", fn, inc=("TE", 1))
                else:
                    B.emit("tensor", fn)
        if icr == 7:
            w1q_reader[q] = fsig[TBF - 1]
        for tb in range(TBF):
            bi, pb = fbanks[tb]
            B.wait("scalar", fsig[tb])
            if ic >= 2 and tb == 0:
                B.wait("scalar", h_spill.get(ic - 2))
            rsig = B.emit("scalar", lambda e, pb=pb, ic=ic, par=par, tb=tb: e.activation(
                out=HT1[:, par * T_CORE + tb * 512: par * T_CORE + (tb + 1) * 512],
                in_=pb[:, :], func=AF.Relu, bias=B1T[:, ic:ic + 1], scale=1.0),
                inc=("SC", 1))
            pool.release(bi, rsig)
        B.wait("gpsimd", rsig)
        hsp_ap = bass.AP(hsp, ic * 128 * T_CORE, [[T_CORE, 128], [1, T_CORE]])
        h_spill[ic] = B.emit("gpsimd", lambda e, hsp_ap=hsp_ap, par=par: e.dma_start(
            out=hsp_ap,
            in_=HT1[:, par * T_CORE:(par + 1) * T_CORE]), inc=("GQ", 16))
        fence("gpsimd", "GQ")

    rb_sig = {}
    rb_reader = {}
    rb_fence = {0: None, 1: None}
    out_sig = {}
    p_hsp = 128 * T_CORE

    def issue_rb(tc):
        slot3 = tc % 3
        sem = "RQ" if tc % 2 == 0 else "RQB"
        if tc == 0:
            B.wait("sync", h_spill[ICN - 1])
        if rb_fence[tc % 2] is not None:
            B.wait("sync", rb_fence[tc % 2])
        if tc >= 3:
            B.wait("sync", rb_reader[tc - 3])
        src_ap = bass.AP(hsp, tc * 128, [[T_CORE, 128], [p_hsp, ICN], [1, 128]])
        dst_ap = bass.AP(HRD, slot3 * ICN * 128,
                         [[_pstep(HRD), 128], [128, ICN], [1, 128]])
        B.emit("sync", lambda e, s=src_ap, d=dst_ap: e.dma_start(out=d, in_=s),
               inc=(sem, 16))
        sig = B.emit("sync", lambda e, tc=tc, slot3=slot3: e.dma_start(
            out=X1RD[:, slot3 * 1024:(slot3 + 1) * 1024],
            in_=x1sp.ap()[tc * 128:(tc + 1) * 128, :]), inc=(sem, 16))
        rb_fence[tc % 2] = sig
        rb_sig[tc] = sig

    issue_rb(0)
    issue_rb(1)
    for tc in range(TC):
        slot3 = tc % 3
        par = tc % 2
        if tc + 2 < TC:
            issue_rb(tc + 2)
        for halfc in range(2):
            bi, pb = pool.acquire()
            B.wait("tensor", rb_sig[tc])
            if tc == 0 and halfc == 0:
                B.wait("tensor", w2b_sig[ICN - 1])
                B.wait("tensor", b2b)
                B.wait("tensor", ones_sig)
            for ic in range(ICN):
                fn = lambda e, ic=ic, slot3=slot3, halfc=halfc, pb=pb: e.matmul(
                    pb[:, :],
                    HRD[:, slot3 * ICN * 128 + ic * 128: slot3 * ICN * 128 + (ic + 1) * 128],
                    W2B[:, ic * 1024 + halfc * 512: ic * 1024 + (halfc + 1) * 512],
                    start=(ic == 0), stop=False)
                B.emit("tensor", fn)
            tsig = B.emit("tensor", lambda e, halfc=halfc, pb=pb: e.matmul(
                pb[:, :], ONES1[:, :],
                B2B[:, halfc * 512:(halfc + 1) * 512],
                start=False, stop=True), inc=("TE", 1))
            B.wait("vector", tsig)
            B.wait("vector", rb_sig[tc])
            if tc >= 2 and halfc == 0:
                B.wait("vector", out_sig.get(tc - 2))
            vsig = B.emit("vector", lambda e, pb=pb, par=par, slot3=slot3, halfc=halfc, tc=tc:
                          e.scalar_tensor_tensor(
                              out=R2[:, par * 1024 + halfc * 512:
                                     par * 1024 + (halfc + 1) * 512],
                              in0=pb[:, :], scalar=1.0,
                              in1=X1RD[:, slot3 * 1024 + halfc * 512:
                                       slot3 * 1024 + (halfc + 1) * 512],
                              op0=OP.mult, op1=OP.add,
                              accum_out=S2H[:, 2 * tc + halfc: 2 * tc + halfc + 1]),
                          inc=("VE", 1))
            pool.release(bi, vsig)
        rb_reader[tc] = vsig
        B.wait("vector", vsig)
        n1 = B.emit("vector", lambda e, tc=tc: e.tensor_add(
            NEGM2[:, tc:tc + 1], S2H[:, 2 * tc:2 * tc + 1],
            S2H[:, 2 * tc + 1:2 * tc + 2]), inc=("VE", 1))
        B.wait("vector", n1)
        n2 = B.emit("vector", lambda e, tc=tc: e.tensor_scalar(
            out=NEGM2[:, tc:tc + 1], in0=NEGM2[:, tc:tc + 1],
            scalar1=-1.0 / D, scalar2=None, op0=OP.mult), inc=("VE", 1))
        B.wait("scalar", n2)
        sq = B.emit("scalar", lambda e, tc=tc, par=par: e.activation(
            out=SQ2[:, :], in_=R2[:, par * 1024:(par + 1) * 1024],
            func=AF.Square, bias=NEGM2[:, tc:tc + 1], scale=1.0,
            accum_out=VARP2[:, tc:tc + 1]), inc=("SC", 1))
        B.wait("vector", sq)
        n3 = B.emit("vector", lambda e, tc=tc: e.tensor_scalar(
            out=VARP2[:, tc:tc + 1], in0=VARP2[:, tc:tc + 1],
            scalar1=1.0 / D, scalar2=LN_EPS, op0=OP.mult, op1=OP.add), inc=("VE", 1))
        B.wait("scalar", n3)
        n4 = B.emit("scalar", lambda e, tc=tc: e.activation(
            out=RSTD2[:, tc:tc + 1], in_=VARP2[:, tc:tc + 1],
            func=AF.Sqrt), inc=("SC", 1))
        B.wait("vector", n4)
        n5 = B.emit("vector", lambda e, tc=tc: e.reciprocal(
            out=RSTD2[:, tc:tc + 1], in_=RSTD2[:, tc:tc + 1]), inc=("VE", 1))
        B.wait("vector", n5)
        n6 = B.emit("vector", lambda e, tc=tc: e.tensor_mul(
            NMRS2[:, tc:tc + 1], NEGM2[:, tc:tc + 1], RSTD2[:, tc:tc + 1]),
            inc=("VE", 1))
        B.wait("scalar", n6)
        so = B.emit("scalar", lambda e, tc=tc, par=par: e.activation(
            out=R2[:, par * 1024:(par + 1) * 1024],
            in_=R2[:, par * 1024:(par + 1) * 1024],
            func=AF.Identity, scale=RSTD2[:, tc:tc + 1],
            bias=NMRS2[:, tc:tc + 1]), inc=("SC", 1))
        B.wait("gpsimd", so)
        out_sig[tc] = B.emit("gpsimd", lambda e, tc=tc, par=par: e.dma_start(
            out=out_d[tc * 128:(tc + 1) * 128, :],
            in_=R2[:, par * 1024:(par + 1) * 1024]), inc=("GQ", 16))
        fence("gpsimd", "GQ")

    B.wait("gpsimd", out_sig[TC - 1])

    B.run()
    stack.close()
    return nc


_NC_CACHE = {}


def _get_nc(T_CORE=2048):
    if T_CORE not in _NC_CACHE:
        _NC_CACHE[T_CORE] = build_nc(T_CORE)
    return _NC_CACHE[T_CORE]


def make_in_maps(inputs, T_CORE=2048):
    import jax

    x = np.ascontiguousarray(np.asarray(inputs["L0_attn_x"], dtype=np.float32)
                             .reshape(-1, D))
    n_tok = x.shape[0]
    assert n_tok == NCORES * T_CORE

    cpu = jax.devices("cpu")[0]
    with jax.default_device(cpu):
        key = jax.random.key(42, impl="threefry2x32")
        kA, kB, kC, kdt = jax.random.split(key, 4)
        eps = {}
        for name, kk in (("A", kA), ("Bp", kB), ("C", kC), ("dt", kdt)):
            kw, kb = jax.random.split(kk)
            eps[(name, "w")] = np.asarray(jax.random.normal(kw, (D, D), "float32"))
            eps[(name, "b")] = np.asarray(jax.random.normal(kb, (D,), "float32"))

    import ml_dtypes
    common = {}
    percore = {c: {} for c in range(NCORES)}
    for P in ("A", "dt", "Bp", "C"):
        wmu = np.asarray(inputs[f"L0_attn_{P}_wmu"], dtype=np.float32)
        wlv = np.asarray(inputs[f"L0_attn_{P}_wlv"], dtype=np.float32)
        wmuT = np.ascontiguousarray(wmu.T)
        wlvT = np.ascontiguousarray(wlv.T)
        epsT = np.ascontiguousarray(eps[(P, "w")].T.astype(ml_dtypes.bfloat16))
        for c in range(NCORES):
            percore[c][f"{P}_wmuT_my"] = np.ascontiguousarray(
                wmuT[c * 128:(c + 1) * 128])
            percore[c][f"{P}_wlvT_my"] = np.ascontiguousarray(
                wlvT[c * 128:(c + 1) * 128])
            percore[c][f"{P}_epsT_my"] = np.ascontiguousarray(
                epsT[c * 128:(c + 1) * 128])
        bmu = np.asarray(inputs[f"L0_attn_{P}_bmu"], dtype=np.float32)
        blv = np.asarray(inputs[f"L0_attn_{P}_blv"], dtype=np.float32)
        common[f"{P}_bmu8"] = np.ascontiguousarray(bmu.reshape(8, 128).T)
        common[f"{P}_blv8"] = np.ascontiguousarray(blv.reshape(8, 128).T)
        common[f"{P}_epsb8"] = np.ascontiguousarray(eps[(P, "b")].reshape(8, 128).T)
    common["W1"] = np.ascontiguousarray(np.asarray(inputs["L0_ffn_W1"], np.float32))
    common["b1_32"] = np.ascontiguousarray(
        np.asarray(inputs["L0_ffn_b1"], np.float32).reshape(ICN, 128).T)
    common["W2"] = np.ascontiguousarray(np.asarray(inputs["L0_ffn_W2"], np.float32))
    common["b2row"] = np.ascontiguousarray(
        np.asarray(inputs["L0_ffn_b2"], np.float32).reshape(1, D))
    common["eye"] = np.eye(128, dtype=np.float32)

    in_maps = []
    for c in range(NCORES):
        ms = np.zeros((128, 8), dtype=np.float32)
        ms[:, (c + 1) % 8] = 1.0
        rr = np.zeros((128, 8), dtype=np.float32)
        if c % 2 == 1:
            rr[:, c] = 1.0
        m = dict(common)
        m.update(percore[c])
        m["x"] = np.ascontiguousarray(x[c * T_CORE:(c + 1) * T_CORE])
        m["msend"] = ms
        m["rread"] = rr
        in_maps.append(m)
    return in_maps


def kernel(**inputs):
    T_CORE = 2048
    nc = _get_nc(T_CORE)
    in_maps = make_in_maps(inputs, T_CORE)
    res = run_bass_kernel_spmd(nc, in_maps, core_ids=list(range(NCORES)))
    out = np.concatenate([np.asarray(res.results[c]["out"]) for c in range(NCORES)],
                         axis=0)
    return out.reshape(4, 4096, 1024).astype(np.float32)


# version 23
# speedup vs baseline: 1.0730x; 1.0030x over previous
import numpy as np

import concourse.bass as bass
import concourse.mybir as mybir
from concourse.bass_utils import run_bass_kernel_spmd

F32 = mybir.dt.float32
BF16 = mybir.dt.bfloat16
AF = mybir.ActivationFunctionType
OP = mybir.AluOpType
AX = mybir.AxisListType.X

NCORES = 8
D = 1024
KD = 8
MC = 8
INTER = 4096
ICN = 32
LN_EPS = 1e-5


def _pstep(t):
    return t.ap().ap[0][0]


class Builder:

    ENGINES = ("sync", "tensor", "scalar", "vector", "gpsimd")

    def __init__(self, nc):
        self.nc = nc
        self.thunks = {e: [] for e in self.ENGINES}
        self.sems = {}
        self.counts = {}

    def sem(self, name):
        handle = self.nc.semaphore(name).__enter__()
        self.sems[name] = handle
        self.counts[name] = 0
        return handle

    def emit(self, eng, fn, inc=None):
        sig = None
        if inc is not None:
            semname, k = inc
            self.counts[semname] += k
            sig = (semname, self.counts[semname])
            handle = self.sems[semname]

            def wrapped(e, fn=fn, handle=handle, k=k):
                fn(e).then_inc(handle, k)
            self.thunks[eng].append(wrapped)
        else:
            self.thunks[eng].append(fn)
        return sig

    def wait(self, eng, sig):
        if sig is None:
            return
        semname, val = sig
        handle = self.sems[semname]
        self.thunks[eng].append(lambda e, h=handle, v=val: e.wait_ge(h, v))

    def run(self):
        nc = self.nc
        with nc.Block() as block:
            @block.sync
            def _(eng):
                for t in self.thunks["sync"]:
                    t(eng)

            @block.tensor
            def _(eng):
                for t in self.thunks["tensor"]:
                    t(eng)

            @block.scalar
            def _(eng):
                for t in self.thunks["scalar"]:
                    t(eng)

            @block.vector
            def _(eng):
                for t in self.thunks["vector"]:
                    t(eng)

            @block.gpsimd
            def _(eng):
                for t in self.thunks["gpsimd"]:
                    t(eng)


class BankPool:

    def __init__(self, B, banks):
        self.B = B
        self.banks = banks
        self.free_at = [None] * len(banks)
        self.rr = 0

    def acquire(self):
        i = self.rr % len(self.banks)
        self.rr += 1
        if self.free_at[i] is not None:
            self.B.wait("tensor", self.free_at[i])
            self.free_at[i] = None
        return i, self.banks[i]

    def release(self, i, sig):
        self.free_at[i] = sig


def build_nc(T_CORE=2048):
    TC = T_CORE // 128
    HALF = T_CORE // 2
    TBH = HALF // 512
    NU = 2 * MC

    nc = bass.Bass("TRN2", target_bir_lowering=False)

    x_d = nc.dram_tensor("x", [T_CORE, D], F32, kind="ExternalInput").ap()
    proj_d = {}
    for P in ("A", "dt", "Bp", "C"):
        proj_d[(P, "wmuT")] = nc.dram_tensor(f"{P}_wmuT_my", [128, D], F32,
                                             kind="ExternalInput").ap()
        proj_d[(P, "wlvT")] = nc.dram_tensor(f"{P}_wlvT_my", [128, D], F32,
                                             kind="ExternalInput").ap()
        proj_d[(P, "epsT")] = nc.dram_tensor(f"{P}_epsT_my", [128, D], BF16,
                                             kind="ExternalInput").ap()
        for part in ("bmu8", "blv8", "epsb8"):
            proj_d[(P, part)] = nc.dram_tensor(f"{P}_{part}", [128, 8], F32,
                                               kind="ExternalInput").ap()
    wag_in = nc.dram_tensor("wag_in", [128, 3072], BF16)
    wag_out = nc.dram_tensor("wag_out", [8, 128, 3072], BF16, addr_space="Shared")
    w1_d = nc.dram_tensor("W1", [D, INTER], F32, kind="ExternalInput").ap()
    b1_d = nc.dram_tensor("b1_32", [128, ICN], F32, kind="ExternalInput").ap()
    w2_d = nc.dram_tensor("W2", [INTER, D], F32, kind="ExternalInput").ap()
    b2_d = nc.dram_tensor("b2row", [1, D], F32, kind="ExternalInput").ap()
    eye_d = nc.dram_tensor("eye", [128, 128], F32, kind="ExternalInput").ap()
    msend_d = nc.dram_tensor("msend", [128, 8], F32, kind="ExternalInput").ap()
    rread_d = nc.dram_tensor("rread", [128, 8], F32, kind="ExternalInput").ap()
    out_d = nc.dram_tensor("out", [T_CORE, D], F32, kind="ExternalOutput").ap()

    hsp = nc.dram_tensor("hsp", [ICN, 128, T_CORE], BF16)
    x1sp = nc.dram_tensor("x1sp", [T_CORE, D], F32)
    cc_in = [nc.dram_tensor(f"cc_in{m}", [128, 8], F32) for m in range(MC)]
    cc_out = [nc.dram_tensor(f"cc_out{m}", [128, 8], F32) for m in range(MC)]

    from contextlib import ExitStack
    stack = ExitStack()
    en = stack.enter_context

    B = Builder(nc)
    for s in ("SQ", "SQB", "SQC", "GQ", "GQA", "SCQ", "TE", "SC", "VE", "CC",
              "FQ", "FQB", "FQC", "FQD", "W2Q", "W2QB", "RQ", "RQB"):
        B.sem(s)

    def fence(eng, semname):
        B.wait(eng, (semname, B.counts[semname]))

    XB = en(nc.sbuf_tensor("XB", [128, KD * T_CORE], BF16))
    EYE = en(nc.sbuf_tensor("EYE", [128, 128], F32))
    SM = en(nc.sbuf_tensor("SM", [128, 8], F32))
    SR = en(nc.sbuf_tensor("SR", [128, 8], F32))
    BRAW = en(nc.sbuf_tensor("BRAW", [128, 8 * 12], F32))
    BS = en(nc.sbuf_tensor("BS", [128, 8], F32))
    BB = en(nc.sbuf_tensor("BB", [128, 8], F32))
    BC4 = en(nc.sbuf_tensor("BC4", [128, 8], F32))
    BT = en(nc.sbuf_tensor("BT", [128, 8 * 3], F32))
    B1T = en(nc.sbuf_tensor("B1T", [128, ICN], F32))
    CTB = en(nc.sbuf_tensor("CTB", [128, 8 * MC], F32))
    ARB = en(nc.sbuf_tensor("ARB", [128, 8 * MC], F32))
    INI = en(nc.sbuf_tensor("INI", [128, MC], F32))
    ZZ = en(nc.sbuf_tensor("ZZ", [128, 8], F32))
    HCC = en(nc.sbuf_tensor("HCC", [128, 1], F32))
    LN4C = en(nc.sbuf_tensor("LN4C", [128, 1], F32))
    SUMS = en(nc.sbuf_tensor("SUMS", [128, TC], F32))
    SQS = en(nc.sbuf_tensor("SQS", [128, TC], F32))
    NEGM = en(nc.sbuf_tensor("NEGM", [128, TC], F32))
    VARP = en(nc.sbuf_tensor("VARP", [128, TC], F32))
    RSTD = en(nc.sbuf_tensor("RSTD", [128, TC], F32))
    NMRS = en(nc.sbuf_tensor("NMRS", [128, TC], F32))

    banks = [en(nc.psum_tensor(f"pb{i}", [128, 512], F32)) for i in range(8)]
    pool = BankPool(B, banks)

    stackAB = ExitStack()
    enB = stackAB.enter_context
    WS = enB(nc.sbuf_tensor("WS", [128, KD * 1024], BF16))
    WB = enB(nc.sbuf_tensor("WB", [128, KD * 1024], BF16))
    WC = enB(nc.sbuf_tensor("WC", [128, KD * 1024], BF16))
    RT = enB(nc.sbuf_tensor("RT", [128, TC * 1024], F32))
    EBUF = enB(nc.sbuf_tensor("EBUF", [128, 3 * HALF], F32))
    BXBUF = enB(nc.sbuf_tensor("BXBUF", [128, 4 * HALF], F32))
    HBUF = enB(nc.sbuf_tensor("HBUF", [128, 3 * HALF], F32))
    XCOL = enB(nc.sbuf_tensor("XCOL", [128, 1024], F32))
    WST = enB(nc.sbuf_tensor("WST", [128, 2 * 1024], F32))
    WSTE = enB(nc.sbuf_tensor("WSTE", [128, 1024], BF16))
    UEX = enB(nc.sbuf_tensor("UEX", [128, 1024], F32))

    pR = _pstep(RT)
    pH = _pstep(HBUF)

    x_sigs = []

    def emit_x_loads():
        for tc in range(TC):
            sig = B.emit("gpsimd", lambda e, tc=tc: e.dma_start(
                out=RT[:, tc * 1024:(tc + 1) * 1024],
                in_=x_d[tc * 128:(tc + 1) * 128, :]), inc=("GQ", 16))
            x_sigs.append(sig)
            if tc % 4 == 3:
                fence("gpsimd", "GQ")

    tiny = [(EYE[:, :], eye_d), (SM[:, :], msend_d), (SR[:, :], rread_d),
            (B1T[:, :], b1_d)]
    for i, (P, part) in enumerate([(P, p) for P in ("A", "dt", "Bp", "C")
                                   for p in ("bmu8", "blv8", "epsb8")]):
        tiny.append((BRAW[:, i * 8:(i + 1) * 8], proj_d[(P, part)]))
    tiny_sig = None
    for dst, src in tiny:
        tiny_sig = B.emit("gpsimd", lambda e, dst=dst, src=src: e.dma_start(
            out=dst, in_=src[:, :]), inc=("GQ", 16))
    fence("gpsimd", "GQ")


    B.wait("tensor", tiny_sig)
    for w in range(70):
        bi, pb = pool.acquire()
        for j in range(4):
            fn = lambda e, pb=pb, j=j: e.transpose(
                pb[:, j * 128:(j + 1) * 128], EYE[:, :], EYE[:, :])
            if j == 3:
                wsig = B.emit("tensor", fn, inc=("TE", 1))
            else:
                B.emit("tensor", fn)
        pool.release(bi, wsig)

    ln4_sig = B.emit("vector", lambda e: e.memset(LN4C[:, :], float(np.log(4.0))),
                     inc=("VE", 1))

    prep_dma = {}
    last_prep = [None]
    for P in ("A", "dt", "Bp", "C"):
        B.wait("sync", last_prep[0])
        for j, part in enumerate(("wmuT", "wlvT")):
            B.emit("sync", lambda e, P=P, part=part, j=j: e.dma_start(
                out=WST[:, j * 1024:(j + 1) * 1024], in_=proj_d[(P, part)][:, :]),
                inc=("SQ", 16))
        sig = B.emit("sync", lambda e, P=P: e.dma_start(
            out=WSTE[:, :], in_=proj_d[(P, "epsT")][:, :]), inc=("SQ", 16))
        prep_dma[P] = sig
        fence("sync", "SQ")

        mode = {"A": "hold", "dt": "ws", "Bp": "wb", "C": "wc4"}[P]
        B.wait("scalar", prep_dma[P])
        B.wait("scalar", last_prep[0])
        if P == "C":
            B.wait("scalar", ln4_sig)
        for half in range(2):
            fn = lambda e, half=half, P=P: e.activation(
                out=UEX[:, half * 512:(half + 1) * 512],
                in_=WST[:, 1024 + half * 512: 1024 + (half + 1) * 512],
                func=AF.Exp, scale=0.5,
                **({"bias": LN4C[:, 0:1]} if P == "C" else {}))
            s1 = B.emit("scalar", fn, inc=("SC", 1))
        B.wait("vector", s1)
        for half in range(2):
            m1 = B.emit("vector", lambda e, half=half: e.tensor_mul(
                UEX[:, half * 512:(half + 1) * 512],
                UEX[:, half * 512:(half + 1) * 512],
                WSTE[:, half * 512:(half + 1) * 512]), inc=("VE", 1))
        B.wait("vector", m1)
        for half in range(2):
            hs = half * 512
            if mode == "hold":
                sig = B.emit("vector", lambda e, hs=hs: e.tensor_add(
                    XCOL[:, hs:hs + 512], UEX[:, hs:hs + 512],
                    WST[:, hs:hs + 512]), inc=("VE", 1))
            elif mode == "ws":
                a1 = B.emit("vector", lambda e, hs=hs: e.tensor_add(
                    UEX[:, hs:hs + 512], UEX[:, hs:hs + 512],
                    WST[:, hs:hs + 512]), inc=("VE", 1))
                B.wait("vector", a1)
                sig = B.emit("vector", lambda e, hs=hs: e.tensor_add(
                    WS[:, hs:hs + 512], XCOL[:, hs:hs + 512],
                    UEX[:, hs:hs + 512]), inc=("VE", 1))
            elif mode == "wb":
                sig = B.emit("vector", lambda e, hs=hs: e.tensor_add(
                    WB[:, hs:hs + 512], UEX[:, hs:hs + 512],
                    WST[:, hs:hs + 512]), inc=("VE", 1))
            else:
                sig = B.emit("vector", lambda e, hs=hs: e.scalar_tensor_tensor(
                    out=WC[:, hs:hs + 512], in0=WST[:, hs:hs + 512],
                    scalar=4.0, in1=UEX[:, hs:hs + 512], op0=OP.mult, op1=OP.add),
                    inc=("VE", 1))
        last_prep[0] = sig
        if P == "C":
            wbc_con = sig

    B.wait("gpsimd", wbc_con)
    B.emit("gpsimd", lambda e: e.dma_start(
        out=wag_in.ap()[:, 0:1024], in_=WS[:, 0:1024]), inc=("GQ", 16))
    B.emit("gpsimd", lambda e: e.dma_start(
        out=wag_in.ap()[:, 1024:2048], in_=WB[:, 0:1024]), inc=("GQ", 16))
    agd = B.emit("gpsimd", lambda e: e.dma_start(
        out=wag_in.ap()[:, 2048:3072], in_=WC[:, 0:1024]), inc=("GQ", 16))
    B.wait("gpsimd", agd)
    ag1 = B.emit("gpsimd", lambda e: e.collective_compute(
        "AllGather", OP.bypass, replica_groups=[list(range(NCORES))],
        ins=[wag_in.ap().opt()], outs=[wag_out.ap().opt()]), inc=("CC", 1))

    emit_x_loads()

    for q in range(TC // 4):
        for k in range(KD):
            bi, pb = pool.acquire()
            B.wait("tensor", x_sigs[4 * q + 3])
            if q == 0 and k == 0:
                B.wait("tensor", tiny_sig)
            for j in range(4):
                tc = 4 * q + j
                fn = lambda e, tc=tc, k=k, pb=pb, j=j: e.transpose(
                    pb[:, j * 128:(j + 1) * 128],
                    RT[:, tc * 1024 + k * 128: tc * 1024 + k * 128 + 128],
                    EYE[:, :])
                if j == 3:
                    tsig = B.emit("tensor", fn, inc=("TE", 1))
                else:
                    B.emit("tensor", fn)
            B.wait("scalar", tsig)
            esig = B.emit("scalar", lambda e, k=k, q=q, pb=pb: e.copy(
                out=XB[:, k * T_CORE + q * 512: k * T_CORE + (q + 1) * 512],
                in_=pb[:, :]), inc=("SC", 1))
            pool.release(bi, esig)
    xb_done = esig

    B.wait("sync", ag1)
    pWS, pWB, pWC = _pstep(WS), _pstep(WB), _pstep(WC)
    ws_sig_all = B.emit("sync", lambda e: e.dma_start(
        out=bass.AP(WS, 0, [[pWS, 128], [1024, 8], [1, 1024]]),
        in_=bass.AP(wag_out, 0, [[3072, 128], [128 * 3072, 8], [1, 1024]])),
        inc=("SQ", 16))
    wb_sig_all = B.emit("sync", lambda e: e.dma_start(
        out=bass.AP(WB, 0, [[pWB, 128], [1024, 8], [1, 1024]]),
        in_=bass.AP(wag_out, 1024, [[3072, 128], [128 * 3072, 8], [1, 1024]])),
        inc=("SQC", 16))
    wc_sig_all = B.emit("sync", lambda e: e.dma_start(
        out=bass.AP(WC, 0, [[pWC, 128], [1024, 8], [1, 1024]]),
        in_=bass.AP(wag_out, 2048, [[3072, 128], [128 * 3072, 8], [1, 1024]])),
        inc=("SQB", 16))
    fence("sync", "SQ")
    fence("sync", "SQC")
    fence("sync", "SQB")
    B.wait("scalar", tiny_sig)
    prev_v = None
    for i, dst in enumerate((None, None, BB, BC4)):
        B.wait("scalar", prev_v)
        s1 = B.emit("scalar", lambda e, i=i: e.activation(
            out=BT[:, 0:8], in_=BRAW[:, (3 * i + 1) * 8:(3 * i + 2) * 8],
            func=AF.Exp, scale=0.5), inc=("SC", 1))
        B.wait("vector", s1)
        tgt = BT[:, 8:16] if i == 0 else (BT[:, 16:24] if i == 1 else dst[:, :])
        v1 = B.emit("vector", lambda e, i=i, tgt=tgt: e.tensor_mul(
            tgt, BT[:, 0:8], BRAW[:, (3 * i + 2) * 8:(3 * i + 3) * 8]),
            inc=("VE", 1))
        B.wait("vector", v1)
        v2 = B.emit("vector", lambda e, i=i, tgt=tgt: e.tensor_add(
            tgt, tgt, BRAW[:, (3 * i) * 8:(3 * i + 1) * 8]), inc=("VE", 1))
        B.wait("vector", v2)
        prev_v = v2
    v = B.emit("vector", lambda e: e.tensor_add(
        BS[:, :], BT[:, 8:16], BT[:, 16:24]), inc=("VE", 1))
    B.wait("vector", v)
    bias_done = B.emit("vector", lambda e: e.tensor_scalar(
        out=BC4[:, :], in0=BC4[:, :], scalar1=4.0, scalar2=None,
        op0=OP.mult), inc=("VE", 1))

    ws_sig = {0: ws_sig_all, 1: ws_sig_all}
    wb_sig_h = {0: wb_sig_all, 1: wb_sig_all}
    wc_sig_h = {0: wc_sig_all, 1: wc_sig_all}
    prev_prep = [sig]

    head_state = {}

    def emit_head(u):
        m, h = u // 2, u % 2
        sl = u % 3
        xsl = u % 2
        bsl = u % 4
        st = {}
        for q2 in range(TBH):
            bi, pb = pool.acquire()
            for j in range(4):
                tc = (TC // 2) * h + 4 * q2 + j
                fn = lambda e, tc=tc, m=m, pb=pb, j=j: e.transpose(
                    pb[:, j * 128:(j + 1) * 128],
                    RT[:, tc * 1024 + m * 128: tc * 1024 + m * 128 + 128],
                    EYE[:, :])
                if j == 3:
                    tsig = B.emit("tensor", fn, inc=("TE", 1))
                else:
                    B.emit("tensor", fn)
            B.wait("scalar", tsig)
            if u == 0 and q2 == 0:
                B.wait("scalar", wbc_con)
            if u >= 1:
                B.wait("scalar", head_state.get(("xcol_reader", u - 1)))
            xsig = B.emit("scalar", lambda e, q2=q2, pb=pb: e.copy(
                out=XCOL[:, q2 * 512:(q2 + 1) * 512],
                in_=pb[:, :]), inc=("SC", 1))
            pool.release(bi, xsig)
            st[("xcol", q2)] = xsig
        if u == 0:
            B.wait("tensor", xb_done)
        if m % 4 == 0 and h == 0:
            B.wait("tensor", ws_sig[m // 4])
        sbanks = [pool.acquire() for _ in range(TBH)]
        for k in range(KD):
            for tb in range(TBH):
                fn = lambda e, k=k, m=m, h=h, tb=tb, pb=sbanks[tb][1]: e.matmul(
                    pb[:, :],
                    WS[:, k * 1024 + m * 128: k * 1024 + m * 128 + 128],
                    XB[:, k * T_CORE + h * HALF + tb * 512:
                       k * T_CORE + h * HALF + (tb + 1) * 512],
                    start=(k == 0), stop=(k == KD - 1))
                if k == KD - 1:
                    st[("smm", tb)] = B.emit("tensor", fn, inc=("TE", 1))
                else:
                    B.emit("tensor", fn)
        for tb in range(TBH):
            bi, pb = sbanks[tb]
            B.wait("vector", st[("smm", tb)])
            if u == 0 and tb == 0:
                B.wait("vector", bias_done)
            if u >= 3:
                B.wait("vector", head_state.get(("ebuf_reader", u - 3)))
            vc = B.emit("vector", lambda e, pb=pb, m=m, sl=sl, tb=tb: e.tensor_scalar(
                out=EBUF[:, sl * HALF + tb * 512: sl * HALF + (tb + 1) * 512],
                in0=pb[:, :], scalar1=BS[:, m:m + 1], scalar2=85.0,
                op0=OP.add, op1=OP.min), inc=("VE", 1))
            pool.release(bi, vc)
            st[("e1", tb)] = vc
        for tb in range(TBH):
            B.wait("scalar", st[("e1", tb)])
            s1 = B.emit("scalar", lambda e, sl=sl, tb=tb: e.activation(
                out=EBUF[:, sl * HALF + tb * 512: sl * HALF + (tb + 1) * 512],
                in_=EBUF[:, sl * HALF + tb * 512: sl * HALF + (tb + 1) * 512],
                func=AF.Exp), inc=("SC", 1))
            st[("e1b", tb)] = s1
        B.wait("scalar", st[("e1b", TBH - 1)])
        for tb in range(TBH):
            s2 = B.emit("scalar", lambda e, sl=sl, tb=tb: e.activation(
                out=EBUF[:, sl * HALF + tb * 512: sl * HALF + (tb + 1) * 512],
                in_=EBUF[:, sl * HALF + tb * 512: sl * HALF + (tb + 1) * 512],
                func=AF.Exp, scale=-1.0), inc=("SC", 1))
        st["E"] = s2
        if m % 4 == 0 and h == 0:
            B.wait("tensor", wb_sig_h[m // 4])
        bbanks = [pool.acquire() for _ in range(TBH)]
        bsig = {}
        for k in range(KD):
            for tb in range(TBH):
                fn = lambda e, k=k, m=m, h=h, tb=tb, pb=bbanks[tb][1]: e.matmul(
                    pb[:, :],
                    WB[:, k * 1024 + m * 128: k * 1024 + m * 128 + 128],
                    XB[:, k * T_CORE + h * HALF + tb * 512:
                       k * T_CORE + h * HALF + (tb + 1) * 512],
                    start=(k == 0), stop=(k == KD - 1))
                if k == KD - 1:
                    bsig[tb] = B.emit("tensor", fn, inc=("TE", 1))
                else:
                    B.emit("tensor", fn)
        for tb in range(TBH):
            bi, pb = bbanks[tb]
            B.wait("vector", bsig[tb])
            B.wait("vector", st[("xcol", tb)])
            if u >= 4:
                B.wait("vector", head_state.get(("bx_reader", u - 4)))
            vsig = B.emit("vector", lambda e, pb=pb, m=m, bsl=bsl, tb=tb: e.scalar_tensor_tensor(
                out=BXBUF[:, bsl * HALF + tb * 512: bsl * HALF + (tb + 1) * 512],
                in0=pb[:, :], scalar=BB[:, m:m + 1],
                in1=XCOL[:, tb * 512:(tb + 1) * 512],
                op0=OP.add, op1=OP.mult), inc=("VE", 1))
            pool.release(bi, vsig)
        head_state[("xcol_reader", u)] = vsig
        B.wait("vector", st["E"])
        B.wait("vector", vsig)
        if h == 0:
            init = 0.0
        else:
            B.wait("vector", head_state[u - 1]["scan1"])
            init = bass.AP(HBUF, ((u - 1) % 3) * HALF + HALF - 1, [[pH, 128], [1, 1]])
        s1sig = B.emit("vector", lambda e, sl=sl, bsl=bsl, init=init: e.tensor_tensor_scan(
            out=HBUF[:, sl * HALF:(sl + 1) * HALF],
            data0=EBUF[:, sl * HALF:(sl + 1) * HALF],
            data1=BXBUF[:, bsl * HALF:(bsl + 1) * HALF],
            initial=init, op0=OP.mult, op1=OP.add), inc=("VE", 1))
        st["scan1"] = s1sig
        if h == 1:
            B.wait("vector", s1sig)
            bc = bass.AP(HBUF, sl * HALF + HALF - 1, [[pH, 128], [0, 8]])
            ctsig = B.emit("vector", lambda e, m=m, bc=bc: e.tensor_tensor(
                out=CTB[:, m * 8:(m + 1) * 8], in0=SM[:, :], in1=bc,
                op=OP.mult), inc=("VE", 1))
            B.wait("gpsimd", ctsig)
            d1 = B.emit("gpsimd", lambda e, m=m: e.dma_start(
                out=cc_in[m].ap(), in_=CTB[:, m * 8:(m + 1) * 8]), inc=("GQ", 16))
            B.wait("gpsimd", d1)
            ccsig = B.emit("gpsimd", lambda e, m=m: e.collective_compute(
                "AllReduce", OP.add,
                replica_groups=[list(range(NCORES))],
                ins=[cc_in[m].ap().opt()],
                outs=[cc_out[m].ap().opt()]), inc=("CC", 1))
            B.wait("gpsimd", ccsig)
            arsig = B.emit("gpsimd", lambda e, m=m: e.dma_start(
                out=ARB[:, m * 8:(m + 1) * 8], in_=cc_out[m].ap()), inc=("GQA", 16))
            st["ar"] = arsig
        head_state[u] = st
        return st

    def emit_tail_a(u):
        m, h = u // 2, u % 2
        sl = u % 3
        bsl = u % 4
        if h == 0:
            arsig = head_state[2 * m + 1]["ar"]
            B.wait("vector", arsig)
            z1 = B.emit("vector", lambda e, m=m: e.tensor_mul(
                ZZ[:, :], ARB[:, m * 8:(m + 1) * 8], SR[:, :]), inc=("VE", 1))
            B.wait("vector", z1)
            z2 = B.emit("vector", lambda e, m=m: e.reduce_sum(
                out=INI[:, m:m + 1], in_=ZZ[:, :], axis=AX), inc=("VE", 1))
            B.wait("vector", z2)
            init = bass.AP(INI, m, [[_pstep(INI), 128], [1, 1]])
        else:
            B.wait("vector", head_state[("hcc", m)])
            init = bass.AP(HCC, 0, [[_pstep(HCC), 128], [1, 1]])
        s2sig = B.emit("vector", lambda e, sl=sl, bsl=bsl, init=init: e.tensor_tensor_scan(
            out=HBUF[:, sl * HALF:(sl + 1) * HALF],
            data0=EBUF[:, sl * HALF:(sl + 1) * HALF],
            data1=BXBUF[:, bsl * HALF:(bsl + 1) * HALF],
            initial=init, op0=OP.mult, op1=OP.add), inc=("VE", 1))
        head_state[("ebuf_reader", u)] = s2sig
        if h == 0:
            B.wait("vector", s2sig)
            hcsig = B.emit("vector", lambda e, sl=sl: e.tensor_copy(
                out=HCC[:, 0:1],
                in_=HBUF[:, sl * HALF + HALF - 1: sl * HALF + HALF]), inc=("VE", 1))
            head_state[("hcc", m)] = hcsig
        if m % 4 == 0 and h == 0:
            B.wait("tensor", wc_sig_h[m // 4])
        cbanks = [pool.acquire() for _ in range(TBH)]
        csig = {}
        for k in range(KD):
            for tb in range(TBH):
                fn = lambda e, k=k, m=m, h=h, tb=tb, pb=cbanks[tb][1]: e.matmul(
                    pb[:, :],
                    WC[:, k * 1024 + m * 128: k * 1024 + m * 128 + 128],
                    XB[:, k * T_CORE + h * HALF + tb * 512:
                       k * T_CORE + h * HALF + (tb + 1) * 512],
                    start=(k == 0), stop=(k == KD - 1))
                if k == KD - 1:
                    csig[tb] = B.emit("tensor", fn, inc=("TE", 1))
                else:
                    B.emit("tensor", fn)
        head_state[("cmm", u)] = csig[TBH - 1]
        ysigs = []
        for tb in range(TBH):
            bi, pb = cbanks[tb]
            B.wait("vector", csig[tb])
            B.wait("vector", s2sig)
            ysig = B.emit("vector", lambda e, pb=pb, m=m, sl=sl, bsl=bsl, tb=tb: e.scalar_tensor_tensor(
                out=BXBUF[:, bsl * HALF + tb * 512: bsl * HALF + (tb + 1) * 512],
                in0=pb[:, :], scalar=BC4[:, m:m + 1],
                in1=HBUF[:, sl * HALF + tb * 512: sl * HALF + (tb + 1) * 512],
                op0=OP.add, op1=OP.mult), inc=("VE", 1))
            pool.release(bi, ysig)
            ysigs.append(ysig)
        head_state[("ysigs", u)] = ysigs

    def emit_tail_b(u):
        m, h = u // 2, u % 2
        bsl = u % 4
        ysigs = head_state[("ysigs", u)]
        for q2 in range(TBH):
            B.wait("tensor", ysigs[q2])
            bi, pb = pool.acquire()
            for j in range(4):
                fn = lambda e, bsl=bsl, q2=q2, j=j, pb=pb: e.transpose(
                    pb[:, j * 128:(j + 1) * 128],
                    BXBUF[:, bsl * HALF + q2 * 512 + j * 128:
                          bsl * HALF + q2 * 512 + (j + 1) * 128],
                    EYE[:, :])
                if j == 3:
                    tsig = B.emit("tensor", fn, inc=("TE", 1))
                else:
                    B.emit("tensor", fn)
            B.wait("vector", tsig)
            tc0 = (TC // 2) * h + 4 * q2
            rt_ap = bass.AP(RT, tc0 * 1024 + m * 128, [[pR, 128], [1024, 4], [1, 128]])
            pb_ap = bass.AP(pb, 0, [[_pstep(pb), 128], [128, 4], [1, 128]])
            rsig = B.emit("vector", lambda e, rt_ap=rt_ap, pb_ap=pb_ap: e.scalar_tensor_tensor(
                out=rt_ap, in0=pb_ap, scalar=1.0, in1=rt_ap,
                op0=OP.mult, op1=OP.add), inc=("VE", 1))
            pool.release(bi, rsig)
        head_state[("bx_reader", u)] = rsig
        head_state[("resid", u)] = rsig

    for s in range(NU + 3):
        if s < NU:
            emit_head(s)
        if 2 <= s < NU + 2:
            emit_tail_a(s - 2)
        if s >= 3:
            emit_tail_b(s - 3)

    B.wait("vector", head_state[("resid", NU - 1)])
    B.wait("vector", head_state[("xcol_reader", NU - 1)])
    pX = _pstep(XCOL)
    ln1_sigs = {}
    spill_sigs = []
    x1t_done = None
    for g in range(TC // 4):
        t0 = 4 * g
        for j in range(4):
            tc = t0 + j
            B.emit("vector", lambda e, tc=tc: e.bn_stats(
                out=XCOL[:, tc * 12: tc * 12 + 6],
                in_=RT[:, tc * 1024: tc * 1024 + 512]))
            s2 = B.emit("vector", lambda e, tc=tc: e.bn_stats(
                out=XCOL[:, tc * 12 + 6: tc * 12 + 12],
                in_=RT[:, tc * 1024 + 512: tc * 1024 + 1024]),
                inc=("VE", 1) if j == 3 else None)
        B.wait("vector", s2)
        for j in range(4):
            tc = t0 + j
            bnsig = B.emit("vector", lambda e, tc=tc: e.bn_aggr(
                out=XCOL[:, 192 + 2 * tc: 192 + 2 * tc + 2],
                in_=XCOL[:, tc * 12: tc * 12 + 12]),
                inc=("VE", 1) if j == 3 else None)
        B.wait("vector", bnsig)
        mean_g = bass.AP(XCOL, 192 + 8 * g, [[pX, 128], [2, 4]])
        var_g = bass.AP(XCOL, 193 + 8 * g, [[pX, 128], [2, 4]])
        ng = B.emit("vector", lambda e, mean_g=mean_g, t0=t0: e.tensor_scalar(
            out=NEGM[:, t0:t0 + 4], in0=mean_g, scalar1=-1.0, scalar2=None,
            op0=OP.mult), inc=("VE", 1))
        vg = B.emit("vector", lambda e, var_g=var_g, t0=t0: e.tensor_scalar(
            out=VARP[:, t0:t0 + 4], in0=var_g, scalar1=1.0, scalar2=LN_EPS,
            op0=OP.mult, op1=OP.add), inc=("VE", 1))
        B.wait("scalar", vg)
        sq1 = B.emit("scalar", lambda e, t0=t0: e.activation(
            out=RSTD[:, t0:t0 + 4], in_=VARP[:, t0:t0 + 4], func=AF.Sqrt),
            inc=("SC", 1))
        B.wait("vector", sq1)
        v2 = B.emit("vector", lambda e, t0=t0: e.reciprocal(
            out=RSTD[:, t0:t0 + 4], in_=RSTD[:, t0:t0 + 4]), inc=("VE", 1))
        B.wait("vector", v2)
        B.wait("vector", ng)
        v3 = B.emit("vector", lambda e, t0=t0: e.tensor_mul(
            NMRS[:, t0:t0 + 4], NEGM[:, t0:t0 + 4], RSTD[:, t0:t0 + 4]),
            inc=("VE", 1))
        B.wait("scalar", v3)
        for j in range(4):
            tc = t0 + j
            s = B.emit("scalar", lambda e, tc=tc: e.activation(
                out=RT[:, tc * 1024:(tc + 1) * 1024],
                in_=RT[:, tc * 1024:(tc + 1) * 1024],
                func=AF.Identity, scale=RSTD[:, tc:tc + 1],
                bias=NMRS[:, tc:tc + 1]), inc=("SC", 1))
            ln1_sigs[tc] = s
            B.wait("gpsimd", s)
            sp = B.emit("gpsimd", lambda e, tc=tc: e.dma_start(
                out=x1sp.ap()[tc * 128:(tc + 1) * 128, :],
                in_=RT[:, tc * 1024:(tc + 1) * 1024]), inc=("GQ", 16))
            spill_sigs.append(sp)
        for k in range(KD):
            bi, pb = pool.acquire()
            B.wait("tensor", ln1_sigs[t0 + 3])
            for j in range(4):
                tc = t0 + j
                fn = lambda e, tc=tc, k=k, pb=pb, j=j: e.transpose(
                    pb[:, j * 128:(j + 1) * 128],
                    RT[:, tc * 1024 + k * 128: tc * 1024 + k * 128 + 128],
                    EYE[:, :])
                if j == 3:
                    tsig = B.emit("tensor", fn, inc=("TE", 1))
                else:
                    B.emit("tensor", fn)
            B.wait("scalar", tsig)
            esig = B.emit("scalar", lambda e, k=k, g=g, pb=pb: e.copy(
                out=XB[:, k * T_CORE + g * 512: k * T_CORE + (g + 1) * 512],
                in_=pb[:, :]), inc=("SC", 1))
            pool.release(bi, esig)
            x1t_done = esig
    ln1_sigs = [ln1_sigs[tc] for tc in range(TC)]
    v3 = x1t_done

    stackAB.close()

    W1ST = en(nc.sbuf_tensor("W1ST", [128, 2 * 1024], F32))
    W1B = en(nc.sbuf_tensor("W1B", [128, 2 * KD * 1024], BF16))
    W2ST = en(nc.sbuf_tensor("W2ST", [128, 2 * 1024], F32))
    W2B = en(nc.sbuf_tensor("W2B", [128, ICN * 1024], BF16))
    HT1 = en(nc.sbuf_tensor("HT1", [128, 2 * T_CORE], BF16))
    HRD = en(nc.sbuf_tensor("HRD", [128, 3 * ICN * 128], BF16))
    X1RD = en(nc.sbuf_tensor("X1RD", [128, 3 * 1024], F32))
    R2 = en(nc.sbuf_tensor("R2", [128, 2 * 1024], F32))
    SQ2 = en(nc.sbuf_tensor("SQ2", [128, 1024], F32))
    B2B = en(nc.sbuf_tensor("B2B", [1, D], BF16))
    ONES1 = en(nc.sbuf_tensor("ONES1", [1, 128], BF16))
    S2H = en(nc.sbuf_tensor("S2H", [128, 2 * TC], F32))
    NEGM2 = en(nc.sbuf_tensor("NEGM2", [128, TC], F32))
    VARP2 = en(nc.sbuf_tensor("VARP2", [128, TC], F32))
    RSTD2 = en(nc.sbuf_tensor("RSTD2", [128, TC], F32))
    NMRS2 = en(nc.sbuf_tensor("NMRS2", [128, TC], F32))

    gate_c1 = head_state[("cmm", NU - 1)]
    gate_sigs = [x1t_done, spill_sigs[-1], head_state[("resid", NU - 1)],
                 ln1_sigs[-1], v3]
    B.wait("sync", gate_c1)
    b2sig = B.emit("sync", lambda e: e.dma_start(out=W2ST[0:1, 0:1024],
                                                 in_=b2_d[:, :]), inc=("SQ", 16))
    fence("sync", "SQ")
    B.wait("scalar", b2sig)
    b2b = B.emit("scalar", lambda e: e.copy(out=B2B[:, :], in_=W2ST[0:1, 0:1024]),
                 inc=("SC", 1))
    for g in gate_sigs:
        B.wait("vector", g)
    ones_sig = B.emit("vector", lambda e: e.memset(ONES1[:, :], 1.0), inc=("VE", 1))
    for g in gate_sigs:
        B.wait("scalar", g)

    w1_dma = {}
    w2_dma = {}
    w1b_sig = {}
    w2b_sig = {}
    w1b_reader = {}
    h_spill = {}
    ffn_fence = {0: None, 1: None, 2: None, 3: None}
    w2_fence = {0: None, 1: None}
    TBF = T_CORE // 512
    NQ = ICN // 8

    def w2_group(j):
        slot = j % 2
        sem = ("W2Q", "W2QB")[slot]
        if w2_fence[slot] is not None:
            B.wait("scalar", w2_fence[slot])
        sig = B.emit("scalar", lambda e, j=j, slot=slot: e.dma_start(
            out=W2ST[:, slot * 1024:slot * 1024 + 1024],
            in_=w2_d[j * 128:(j + 1) * 128, :]), inc=(sem, 16))
        w2_fence[slot] = sig
        return sig

    w1q_sig = {}
    w1q_reader = {}
    qconv = {}

    w1conv = {}

    def w1_quarter(q):
        qsl = q % 2
        sigs = []
        for k in range(KD):
            g = q * KD + k
            slot = g % 2
            sem = ("FQ", "FQB", "FQC", "FQD")[g % 4]
            if ffn_fence[g % 4] is not None:
                B.wait("sync", ffn_fence[g % 4])
            if g >= 2:
                B.wait("sync", w1conv.get(g - 2))
            dsig = B.emit("sync", lambda e, k=k, q=q, slot=slot: e.dma_start(
                out=W1ST[:, slot * 1024:(slot + 1) * 1024],
                in_=w1_d[k * 128:(k + 1) * 128, q * 1024:(q + 1) * 1024]),
                inc=(sem, 16))
            ffn_fence[g % 4] = dsig
            B.wait("vector", dsig)
            if (q, k) in qconv_war:
                B.wait("vector", qconv_war[(q, k)])
            csig = B.emit("vector", lambda e, k=k, qsl=qsl, slot=slot: e.tensor_copy(
                out=W1B[:, qsl * KD * 1024 + k * 1024: qsl * KD * 1024 + (k + 1) * 1024],
                in_=W1ST[:, slot * 1024:(slot + 1) * 1024]), inc=("VE", 1))
            w1conv[g] = csig
            sigs.append(csig)
        return sigs[-1]

    qconv_war = {}
    w2c_j = [0]
    h_spill = {}
    w1q_sig[0] = w1_quarter(0)
    for ic in range(ICN):
        q = ic // 8
        qsl = q % 2
        icr = ic % 8
        if icr == 0 and q + 1 < NQ:
            if q >= 1:
                B.wait("sync", w1q_reader[q - 1])
                for k in range(KD):
                    qconv_war[(q + 1, k)] = w1q_reader[q - 1]
            w1q_sig[q + 1] = w1_quarter(q + 1)
        if ic < ICN:
            j = ic
            if j == 0:
                B.wait("scalar", b2b)
            if j >= 2:
                B.wait("scalar", w2b_sig[j - 2])
            w2_dma[j] = w2_group(j)
        if w2c_j[0] <= ic:
            j = w2c_j[0]
            B.wait("vector", w2_dma[j])
            w2b_sig[j] = B.emit("vector", lambda e, j=j: e.tensor_copy(
                out=W2B[:, j * 1024:(j + 1) * 1024],
                in_=W2ST[:, (j % 2) * 1024:(j % 2) * 1024 + 1024]), inc=("VE", 1))
            w2c_j[0] += 1
        par = ic % 2
        fbanks = [pool.acquire() for _ in range(TBF)]
        if icr == 0:
            B.wait("tensor", w1q_sig[q])
        fsig = {}
        for k in range(KD):
            for tb in range(TBF):
                fn = lambda e, k=k, qsl=qsl, icr=icr, tb=tb, pb=fbanks[tb][1]: e.matmul(
                    pb[:, :],
                    W1B[:, qsl * KD * 1024 + k * 1024 + icr * 128:
                        qsl * KD * 1024 + k * 1024 + (icr + 1) * 128],
                    XB[:, k * T_CORE + tb * 512: k * T_CORE + (tb + 1) * 512],
                    start=(k == 0), stop=(k == KD - 1))
                if k == KD - 1:
                    fsig[tb] = B.emit("tensor", fn, inc=("TE", 1))
                else:
                    B.emit("tensor", fn)
        if icr == 7:
            w1q_reader[q] = fsig[TBF - 1]
        for tb in range(TBF):
            bi, pb = fbanks[tb]
            B.wait("scalar", fsig[tb])
            if ic >= 2 and tb == 0:
                B.wait("scalar", h_spill.get(ic - 2))
            rsig = B.emit("scalar", lambda e, pb=pb, ic=ic, par=par, tb=tb: e.activation(
                out=HT1[:, par * T_CORE + tb * 512: par * T_CORE + (tb + 1) * 512],
                in_=pb[:, :], func=AF.Relu, bias=B1T[:, ic:ic + 1], scale=1.0),
                inc=("SC", 1))
            pool.release(bi, rsig)
        B.wait("gpsimd", rsig)
        hsp_ap = bass.AP(hsp, ic * 128 * T_CORE, [[T_CORE, 128], [1, T_CORE]])
        h_spill[ic] = B.emit("gpsimd", lambda e, hsp_ap=hsp_ap, par=par: e.dma_start(
            out=hsp_ap,
            in_=HT1[:, par * T_CORE:(par + 1) * T_CORE]), inc=("GQ", 16))
        fence("gpsimd", "GQ")

    rb_sig = {}
    rb_reader = {}
    rb_fence = {0: None, 1: None}
    out_sig = {}
    p_hsp = 128 * T_CORE

    def issue_rb(tc):
        slot3 = tc % 3
        sem = "RQ" if tc % 2 == 0 else "RQB"
        if tc == 0:
            B.wait("sync", h_spill[ICN - 1])
        if rb_fence[tc % 2] is not None:
            B.wait("sync", rb_fence[tc % 2])
        if tc >= 3:
            B.wait("sync", rb_reader[tc - 3])
        src_ap = bass.AP(hsp, tc * 128, [[T_CORE, 128], [p_hsp, ICN], [1, 128]])
        dst_ap = bass.AP(HRD, slot3 * ICN * 128,
                         [[_pstep(HRD), 128], [128, ICN], [1, 128]])
        B.emit("sync", lambda e, s=src_ap, d=dst_ap: e.dma_start(out=d, in_=s),
               inc=(sem, 16))
        sig = B.emit("sync", lambda e, tc=tc, slot3=slot3: e.dma_start(
            out=X1RD[:, slot3 * 1024:(slot3 + 1) * 1024],
            in_=x1sp.ap()[tc * 128:(tc + 1) * 128, :]), inc=(sem, 16))
        rb_fence[tc % 2] = sig
        rb_sig[tc] = sig

    issue_rb(0)
    issue_rb(1)
    for tc in range(TC):
        slot3 = tc % 3
        par = tc % 2
        if tc + 2 < TC:
            issue_rb(tc + 2)
        for halfc in range(2):
            bi, pb = pool.acquire()
            B.wait("tensor", rb_sig[tc])
            if tc == 0 and halfc == 0:
                B.wait("tensor", w2b_sig[ICN - 1])
                B.wait("tensor", b2b)
                B.wait("tensor", ones_sig)
            for ic in range(ICN):
                fn = lambda e, ic=ic, slot3=slot3, halfc=halfc, pb=pb: e.matmul(
                    pb[:, :],
                    HRD[:, slot3 * ICN * 128 + ic * 128: slot3 * ICN * 128 + (ic + 1) * 128],
                    W2B[:, ic * 1024 + halfc * 512: ic * 1024 + (halfc + 1) * 512],
                    start=(ic == 0), stop=False)
                B.emit("tensor", fn)
            tsig = B.emit("tensor", lambda e, halfc=halfc, pb=pb: e.matmul(
                pb[:, :], ONES1[:, :],
                B2B[:, halfc * 512:(halfc + 1) * 512],
                start=False, stop=True), inc=("TE", 1))
            B.wait("vector", tsig)
            B.wait("vector", rb_sig[tc])
            if tc >= 2 and halfc == 0:
                B.wait("vector", out_sig.get(tc - 2))
            vsig = B.emit("vector", lambda e, pb=pb, par=par, slot3=slot3, halfc=halfc, tc=tc:
                          e.scalar_tensor_tensor(
                              out=R2[:, par * 1024 + halfc * 512:
                                     par * 1024 + (halfc + 1) * 512],
                              in0=pb[:, :], scalar=1.0,
                              in1=X1RD[:, slot3 * 1024 + halfc * 512:
                                       slot3 * 1024 + (halfc + 1) * 512],
                              op0=OP.mult, op1=OP.add,
                              accum_out=S2H[:, 2 * tc + halfc: 2 * tc + halfc + 1]),
                          inc=("VE", 1))
            pool.release(bi, vsig)
        rb_reader[tc] = vsig
        B.wait("vector", vsig)
        n1 = B.emit("vector", lambda e, tc=tc: e.tensor_add(
            NEGM2[:, tc:tc + 1], S2H[:, 2 * tc:2 * tc + 1],
            S2H[:, 2 * tc + 1:2 * tc + 2]), inc=("VE", 1))
        B.wait("vector", n1)
        n2 = B.emit("vector", lambda e, tc=tc: e.tensor_scalar(
            out=NEGM2[:, tc:tc + 1], in0=NEGM2[:, tc:tc + 1],
            scalar1=-1.0 / D, scalar2=None, op0=OP.mult), inc=("VE", 1))
        B.wait("scalar", n2)
        sq = B.emit("scalar", lambda e, tc=tc, par=par: e.activation(
            out=SQ2[:, :], in_=R2[:, par * 1024:(par + 1) * 1024],
            func=AF.Square, bias=NEGM2[:, tc:tc + 1], scale=1.0,
            accum_out=VARP2[:, tc:tc + 1]), inc=("SC", 1))
        B.wait("vector", sq)
        n3 = B.emit("vector", lambda e, tc=tc: e.tensor_scalar(
            out=VARP2[:, tc:tc + 1], in0=VARP2[:, tc:tc + 1],
            scalar1=1.0 / D, scalar2=LN_EPS, op0=OP.mult, op1=OP.add), inc=("VE", 1))
        B.wait("scalar", n3)
        n4 = B.emit("scalar", lambda e, tc=tc: e.activation(
            out=RSTD2[:, tc:tc + 1], in_=VARP2[:, tc:tc + 1],
            func=AF.Sqrt), inc=("SC", 1))
        B.wait("vector", n4)
        n5 = B.emit("vector", lambda e, tc=tc: e.reciprocal(
            out=RSTD2[:, tc:tc + 1], in_=RSTD2[:, tc:tc + 1]), inc=("VE", 1))
        B.wait("vector", n5)
        n6 = B.emit("vector", lambda e, tc=tc: e.tensor_mul(
            NMRS2[:, tc:tc + 1], NEGM2[:, tc:tc + 1], RSTD2[:, tc:tc + 1]),
            inc=("VE", 1))
        B.wait("scalar", n6)
        so = B.emit("scalar", lambda e, tc=tc, par=par: e.activation(
            out=R2[:, par * 1024:(par + 1) * 1024],
            in_=R2[:, par * 1024:(par + 1) * 1024],
            func=AF.Identity, scale=RSTD2[:, tc:tc + 1],
            bias=NMRS2[:, tc:tc + 1]), inc=("SC", 1))
        B.wait("gpsimd", so)
        out_sig[tc] = B.emit("gpsimd", lambda e, tc=tc, par=par: e.dma_start(
            out=out_d[tc * 128:(tc + 1) * 128, :],
            in_=R2[:, par * 1024:(par + 1) * 1024]), inc=("GQ", 16))
        fence("gpsimd", "GQ")

    B.wait("gpsimd", out_sig[TC - 1])

    B.run()
    stack.close()
    return nc


_NC_CACHE = {}


def _get_nc(T_CORE=2048):
    if T_CORE not in _NC_CACHE:
        _NC_CACHE[T_CORE] = build_nc(T_CORE)
    return _NC_CACHE[T_CORE]


def make_in_maps(inputs, T_CORE=2048):
    import jax

    x = np.ascontiguousarray(np.asarray(inputs["L0_attn_x"], dtype=np.float32)
                             .reshape(-1, D))
    n_tok = x.shape[0]
    assert n_tok == NCORES * T_CORE

    cpu = jax.devices("cpu")[0]
    with jax.default_device(cpu):
        key = jax.random.key(42, impl="threefry2x32")
        kA, kB, kC, kdt = jax.random.split(key, 4)
        eps = {}
        for name, kk in (("A", kA), ("Bp", kB), ("C", kC), ("dt", kdt)):
            kw, kb = jax.random.split(kk)
            eps[(name, "w")] = np.asarray(jax.random.normal(kw, (D, D), "float32"))
            eps[(name, "b")] = np.asarray(jax.random.normal(kb, (D,), "float32"))

    import ml_dtypes
    common = {}
    percore = {c: {} for c in range(NCORES)}
    for P in ("A", "dt", "Bp", "C"):
        wmu = np.asarray(inputs[f"L0_attn_{P}_wmu"], dtype=np.float32)
        wlv = np.asarray(inputs[f"L0_attn_{P}_wlv"], dtype=np.float32)
        wmuT = np.ascontiguousarray(wmu.T)
        wlvT = np.ascontiguousarray(wlv.T)
        epsT = np.ascontiguousarray(eps[(P, "w")].T.astype(ml_dtypes.bfloat16))
        for c in range(NCORES):
            percore[c][f"{P}_wmuT_my"] = np.ascontiguousarray(
                wmuT[c * 128:(c + 1) * 128])
            percore[c][f"{P}_wlvT_my"] = np.ascontiguousarray(
                wlvT[c * 128:(c + 1) * 128])
            percore[c][f"{P}_epsT_my"] = np.ascontiguousarray(
                epsT[c * 128:(c + 1) * 128])
        bmu = np.asarray(inputs[f"L0_attn_{P}_bmu"], dtype=np.float32)
        blv = np.asarray(inputs[f"L0_attn_{P}_blv"], dtype=np.float32)
        common[f"{P}_bmu8"] = np.ascontiguousarray(bmu.reshape(8, 128).T)
        common[f"{P}_blv8"] = np.ascontiguousarray(blv.reshape(8, 128).T)
        common[f"{P}_epsb8"] = np.ascontiguousarray(eps[(P, "b")].reshape(8, 128).T)
    common["W1"] = np.ascontiguousarray(np.asarray(inputs["L0_ffn_W1"], np.float32))
    common["b1_32"] = np.ascontiguousarray(
        np.asarray(inputs["L0_ffn_b1"], np.float32).reshape(ICN, 128).T)
    common["W2"] = np.ascontiguousarray(np.asarray(inputs["L0_ffn_W2"], np.float32))
    common["b2row"] = np.ascontiguousarray(
        np.asarray(inputs["L0_ffn_b2"], np.float32).reshape(1, D))
    common["eye"] = np.eye(128, dtype=np.float32)

    in_maps = []
    for c in range(NCORES):
        ms = np.zeros((128, 8), dtype=np.float32)
        ms[:, (c + 1) % 8] = 1.0
        rr = np.zeros((128, 8), dtype=np.float32)
        if c % 2 == 1:
            rr[:, c] = 1.0
        m = dict(common)
        m.update(percore[c])
        m["x"] = np.ascontiguousarray(x[c * T_CORE:(c + 1) * T_CORE])
        m["msend"] = ms
        m["rread"] = rr
        in_maps.append(m)
    return in_maps


def kernel(**inputs):
    T_CORE = 2048
    nc = _get_nc(T_CORE)
    in_maps = make_in_maps(inputs, T_CORE)
    res = run_bass_kernel_spmd(nc, in_maps, core_ids=list(range(NCORES)))
    out = np.concatenate([np.asarray(res.results[c]["out"]) for c in range(NCORES)],
                         axis=0)
    return out.reshape(4, 4096, 1024).astype(np.float32)


# version 24
# speedup vs baseline: 1.0834x; 1.0097x over previous
import numpy as np

import concourse.bass as bass
import concourse.mybir as mybir
from concourse.bass_utils import run_bass_kernel_spmd

F32 = mybir.dt.float32
BF16 = mybir.dt.bfloat16
AF = mybir.ActivationFunctionType
OP = mybir.AluOpType
AX = mybir.AxisListType.X

NCORES = 8
D = 1024
KD = 8
MC = 8
INTER = 4096
ICN = 32
LN_EPS = 1e-5


def _pstep(t):
    return t.ap().ap[0][0]


class Builder:

    ENGINES = ("sync", "tensor", "scalar", "vector", "gpsimd")

    def __init__(self, nc):
        self.nc = nc
        self.thunks = {e: [] for e in self.ENGINES}
        self.sems = {}
        self.counts = {}

    def sem(self, name):
        handle = self.nc.semaphore(name).__enter__()
        self.sems[name] = handle
        self.counts[name] = 0
        return handle

    def emit(self, eng, fn, inc=None):
        sig = None
        if inc is not None:
            semname, k = inc
            self.counts[semname] += k
            sig = (semname, self.counts[semname])
            handle = self.sems[semname]

            def wrapped(e, fn=fn, handle=handle, k=k):
                fn(e).then_inc(handle, k)
            self.thunks[eng].append(wrapped)
        else:
            self.thunks[eng].append(fn)
        return sig

    def wait(self, eng, sig):
        if sig is None:
            return
        semname, val = sig
        handle = self.sems[semname]
        self.thunks[eng].append(lambda e, h=handle, v=val: e.wait_ge(h, v))

    def run(self):
        nc = self.nc
        with nc.Block() as block:
            @block.sync
            def _(eng):
                for t in self.thunks["sync"]:
                    t(eng)

            @block.tensor
            def _(eng):
                for t in self.thunks["tensor"]:
                    t(eng)

            @block.scalar
            def _(eng):
                for t in self.thunks["scalar"]:
                    t(eng)

            @block.vector
            def _(eng):
                for t in self.thunks["vector"]:
                    t(eng)

            @block.gpsimd
            def _(eng):
                for t in self.thunks["gpsimd"]:
                    t(eng)


class BankPool:

    def __init__(self, B, banks):
        self.B = B
        self.banks = banks
        self.free_at = [None] * len(banks)
        self.rr = 0

    def acquire(self):
        i = self.rr % len(self.banks)
        self.rr += 1
        if self.free_at[i] is not None:
            self.B.wait("tensor", self.free_at[i])
            self.free_at[i] = None
        return i, self.banks[i]

    def release(self, i, sig):
        self.free_at[i] = sig


def build_nc(T_CORE=2048):
    TC = T_CORE // 128
    HALF = T_CORE // 2
    TBH = HALF // 512
    NU = 2 * MC

    nc = bass.Bass("TRN2", target_bir_lowering=False)

    x_d = nc.dram_tensor("x", [T_CORE, D], F32, kind="ExternalInput").ap()
    proj_d = {}
    for P in ("A", "dt", "Bp", "C"):
        proj_d[(P, "wmuT")] = nc.dram_tensor(f"{P}_wmuT_my", [128, D], F32,
                                             kind="ExternalInput").ap()
        proj_d[(P, "wlvT")] = nc.dram_tensor(f"{P}_wlvT_my", [128, D], F32,
                                             kind="ExternalInput").ap()
        proj_d[(P, "epsT")] = nc.dram_tensor(f"{P}_epsT_my", [128, D], BF16,
                                             kind="ExternalInput").ap()
        for part in ("bmu8", "blv8", "epsb8"):
            proj_d[(P, part)] = nc.dram_tensor(f"{P}_{part}", [128, 8], F32,
                                               kind="ExternalInput").ap()
    wag_in = nc.dram_tensor("wag_in", [128, 3072], BF16)
    wag_out = nc.dram_tensor("wag_out", [8, 128, 3072], BF16, addr_space="Shared")
    dag_in = nc.dram_tensor("dag_in", [1, 8], F32)
    dag_out = nc.dram_tensor("dag_out", [8, 1, 8], F32, addr_space="Shared")
    w1_d = nc.dram_tensor("W1", [D, INTER], F32, kind="ExternalInput").ap()
    b1_d = nc.dram_tensor("b1_32", [128, ICN], F32, kind="ExternalInput").ap()
    w2_d = nc.dram_tensor("W2", [INTER, D], F32, kind="ExternalInput").ap()
    b2_d = nc.dram_tensor("b2row", [1, D], F32, kind="ExternalInput").ap()
    eye_d = nc.dram_tensor("eye", [128, 128], F32, kind="ExternalInput").ap()
    msend_d = nc.dram_tensor("msend", [128, 8], F32, kind="ExternalInput").ap()
    rread_d = nc.dram_tensor("rread", [128, 8], F32, kind="ExternalInput").ap()
    out_d = nc.dram_tensor("out", [T_CORE, D], F32, kind="ExternalOutput").ap()

    hsp = nc.dram_tensor("hsp", [ICN, 128, T_CORE], BF16)
    x1sp = nc.dram_tensor("x1sp", [T_CORE, D], F32)
    cc_in = [nc.dram_tensor(f"cc_in{m}", [128, 8], F32) for m in range(MC)]
    cc_out = [nc.dram_tensor(f"cc_out{m}", [128, 8], F32) for m in range(MC)]

    from contextlib import ExitStack
    stack = ExitStack()
    en = stack.enter_context

    B = Builder(nc)
    for s in ("SQ", "SQB", "SQC", "GQ", "GQA", "SCQ", "TE", "SC", "VE", "CC",
              "FQ", "FQB", "FQC", "FQD", "W2Q", "W2QB", "RQ", "RQB"):
        B.sem(s)

    def fence(eng, semname):
        B.wait(eng, (semname, B.counts[semname]))

    XB = en(nc.sbuf_tensor("XB", [128, KD * T_CORE], BF16))
    EYE = en(nc.sbuf_tensor("EYE", [128, 128], F32))
    SM = en(nc.sbuf_tensor("SM", [128, 8], F32))
    SR = en(nc.sbuf_tensor("SR", [128, 8], F32))
    BRAW = en(nc.sbuf_tensor("BRAW", [128, 8 * 12], F32))
    BS = en(nc.sbuf_tensor("BS", [128, 8], F32))
    BB = en(nc.sbuf_tensor("BB", [128, 8], F32))
    BC4 = en(nc.sbuf_tensor("BC4", [128, 8], F32))
    BT = en(nc.sbuf_tensor("BT", [128, 8 * 3], F32))
    B1T = en(nc.sbuf_tensor("B1T", [128, ICN], F32))
    CTB = en(nc.sbuf_tensor("CTB", [128, 8 * MC], F32))
    ARB = en(nc.sbuf_tensor("ARB", [128, 8 * MC], F32))
    INI = en(nc.sbuf_tensor("INI", [128, MC], F32))
    ZZ = en(nc.sbuf_tensor("ZZ", [128, 8], F32))
    HCC = en(nc.sbuf_tensor("HCC", [128, 1], F32))
    LN4C = en(nc.sbuf_tensor("LN4C", [128, 1], F32))
    SUMS = en(nc.sbuf_tensor("SUMS", [128, TC], F32))
    SQS = en(nc.sbuf_tensor("SQS", [128, TC], F32))
    NEGM = en(nc.sbuf_tensor("NEGM", [128, TC], F32))
    VARP = en(nc.sbuf_tensor("VARP", [128, TC], F32))
    RSTD = en(nc.sbuf_tensor("RSTD", [128, TC], F32))
    NMRS = en(nc.sbuf_tensor("NMRS", [128, TC], F32))

    banks = [en(nc.psum_tensor(f"pb{i}", [128, 512], F32)) for i in range(8)]
    pool = BankPool(B, banks)

    stackAB = ExitStack()
    enB = stackAB.enter_context
    WS = enB(nc.sbuf_tensor("WS", [128, KD * 1024], BF16))
    WB = enB(nc.sbuf_tensor("WB", [128, KD * 1024], BF16))
    WC = enB(nc.sbuf_tensor("WC", [128, KD * 1024], BF16))
    RT = enB(nc.sbuf_tensor("RT", [128, TC * 1024], F32))
    EBUF = enB(nc.sbuf_tensor("EBUF", [128, 3 * HALF], F32))
    BXBUF = enB(nc.sbuf_tensor("BXBUF", [128, 4 * HALF], F32))
    HBUF = enB(nc.sbuf_tensor("HBUF", [128, 3 * HALF], F32))
    XCOL = enB(nc.sbuf_tensor("XCOL", [128, 1024], F32))
    WST = enB(nc.sbuf_tensor("WST", [128, 2 * 1024], F32))
    WSTE = enB(nc.sbuf_tensor("WSTE", [128, 1024], BF16))
    UEX = enB(nc.sbuf_tensor("UEX", [128, 1024], F32))

    pR = _pstep(RT)
    pH = _pstep(HBUF)

    x_sigs = []

    def emit_x_loads():
        for tc in range(TC):
            sig = B.emit("gpsimd", lambda e, tc=tc: e.dma_start(
                out=RT[:, tc * 1024:(tc + 1) * 1024],
                in_=x_d[tc * 128:(tc + 1) * 128, :]), inc=("GQ", 16))
            x_sigs.append(sig)
            if tc % 4 == 3:
                fence("gpsimd", "GQ")

    B.emit("gpsimd", lambda e: e.dma_start(out=dag_in.ap()[:, :],
                                           in_=msend_d[0:1, :]), inc=("GQ", 16))
    fence("gpsimd", "GQ")
    B.emit("gpsimd", lambda e: e.collective_compute(
        "AllGather", OP.bypass, replica_groups=[list(range(NCORES))],
        ins=[dag_in.ap().opt()], outs=[dag_out.ap().opt()]), inc=("CC", 1))

    tiny = [(EYE[:, :], eye_d), (SM[:, :], msend_d), (SR[:, :], rread_d),
            (B1T[:, :], b1_d)]
    for i, (P, part) in enumerate([(P, p) for P in ("A", "dt", "Bp", "C")
                                   for p in ("bmu8", "blv8", "epsb8")]):
        tiny.append((BRAW[:, i * 8:(i + 1) * 8], proj_d[(P, part)]))
    tiny_sig = None
    for dst, src in tiny:
        tiny_sig = B.emit("gpsimd", lambda e, dst=dst, src=src: e.dma_start(
            out=dst, in_=src[:, :]), inc=("GQ", 16))
    fence("gpsimd", "GQ")


    B.wait("tensor", tiny_sig)
    for w in range(70):
        bi, pb = pool.acquire()
        for j in range(4):
            fn = lambda e, pb=pb, j=j: e.transpose(
                pb[:, j * 128:(j + 1) * 128], EYE[:, :], EYE[:, :])
            if j == 3:
                wsig = B.emit("tensor", fn, inc=("TE", 1))
            else:
                B.emit("tensor", fn)
        pool.release(bi, wsig)

    ln4_sig = B.emit("vector", lambda e: e.memset(LN4C[:, :], float(np.log(4.0))),
                     inc=("VE", 1))

    prep_dma = {}
    last_prep = [None]
    for P in ("A", "dt", "Bp", "C"):
        B.wait("sync", last_prep[0])
        for j, part in enumerate(("wmuT", "wlvT")):
            B.emit("sync", lambda e, P=P, part=part, j=j: e.dma_start(
                out=WST[:, j * 1024:(j + 1) * 1024], in_=proj_d[(P, part)][:, :]),
                inc=("SQ", 16))
        sig = B.emit("sync", lambda e, P=P: e.dma_start(
            out=WSTE[:, :], in_=proj_d[(P, "epsT")][:, :]), inc=("SQ", 16))
        prep_dma[P] = sig
        fence("sync", "SQ")

        mode = {"A": "hold", "dt": "ws", "Bp": "wb", "C": "wc4"}[P]
        B.wait("scalar", prep_dma[P])
        B.wait("scalar", last_prep[0])
        if P == "C":
            B.wait("scalar", ln4_sig)
        for half in range(2):
            fn = lambda e, half=half, P=P: e.activation(
                out=UEX[:, half * 512:(half + 1) * 512],
                in_=WST[:, 1024 + half * 512: 1024 + (half + 1) * 512],
                func=AF.Exp, scale=0.5,
                **({"bias": LN4C[:, 0:1]} if P == "C" else {}))
            s1 = B.emit("scalar", fn, inc=("SC", 1))
        B.wait("vector", s1)
        for half in range(2):
            m1 = B.emit("vector", lambda e, half=half: e.tensor_mul(
                UEX[:, half * 512:(half + 1) * 512],
                UEX[:, half * 512:(half + 1) * 512],
                WSTE[:, half * 512:(half + 1) * 512]), inc=("VE", 1))
        B.wait("vector", m1)
        for half in range(2):
            hs = half * 512
            if mode == "hold":
                sig = B.emit("vector", lambda e, hs=hs: e.tensor_add(
                    XCOL[:, hs:hs + 512], UEX[:, hs:hs + 512],
                    WST[:, hs:hs + 512]), inc=("VE", 1))
            elif mode == "ws":
                a1 = B.emit("vector", lambda e, hs=hs: e.tensor_add(
                    UEX[:, hs:hs + 512], UEX[:, hs:hs + 512],
                    WST[:, hs:hs + 512]), inc=("VE", 1))
                B.wait("vector", a1)
                sig = B.emit("vector", lambda e, hs=hs: e.tensor_add(
                    WS[:, hs:hs + 512], XCOL[:, hs:hs + 512],
                    UEX[:, hs:hs + 512]), inc=("VE", 1))
            elif mode == "wb":
                sig = B.emit("vector", lambda e, hs=hs: e.tensor_add(
                    WB[:, hs:hs + 512], UEX[:, hs:hs + 512],
                    WST[:, hs:hs + 512]), inc=("VE", 1))
            else:
                sig = B.emit("vector", lambda e, hs=hs: e.scalar_tensor_tensor(
                    out=WC[:, hs:hs + 512], in0=WST[:, hs:hs + 512],
                    scalar=4.0, in1=UEX[:, hs:hs + 512], op0=OP.mult, op1=OP.add),
                    inc=("VE", 1))
        last_prep[0] = sig
        if P == "C":
            wbc_con = sig

    B.wait("gpsimd", wbc_con)
    B.emit("gpsimd", lambda e: e.dma_start(
        out=wag_in.ap()[:, 0:1024], in_=WS[:, 0:1024]), inc=("GQ", 16))
    B.emit("gpsimd", lambda e: e.dma_start(
        out=wag_in.ap()[:, 1024:2048], in_=WB[:, 0:1024]), inc=("GQ", 16))
    agd = B.emit("gpsimd", lambda e: e.dma_start(
        out=wag_in.ap()[:, 2048:3072], in_=WC[:, 0:1024]), inc=("GQ", 16))
    B.wait("gpsimd", agd)
    ag1 = B.emit("gpsimd", lambda e: e.collective_compute(
        "AllGather", OP.bypass, replica_groups=[list(range(NCORES))],
        ins=[wag_in.ap().opt()], outs=[wag_out.ap().opt()]), inc=("CC", 1))

    emit_x_loads()

    for q in range(TC // 4):
        for k in range(KD):
            bi, pb = pool.acquire()
            B.wait("tensor", x_sigs[4 * q + 3])
            if q == 0 and k == 0:
                B.wait("tensor", tiny_sig)
            for j in range(4):
                tc = 4 * q + j
                fn = lambda e, tc=tc, k=k, pb=pb, j=j: e.transpose(
                    pb[:, j * 128:(j + 1) * 128],
                    RT[:, tc * 1024 + k * 128: tc * 1024 + k * 128 + 128],
                    EYE[:, :])
                if j == 3:
                    tsig = B.emit("tensor", fn, inc=("TE", 1))
                else:
                    B.emit("tensor", fn)
            B.wait("scalar", tsig)
            esig = B.emit("scalar", lambda e, k=k, q=q, pb=pb: e.copy(
                out=XB[:, k * T_CORE + q * 512: k * T_CORE + (q + 1) * 512],
                in_=pb[:, :]), inc=("SC", 1))
            pool.release(bi, esig)
    xb_done = esig

    B.wait("sync", ag1)
    pWS, pWB, pWC = _pstep(WS), _pstep(WB), _pstep(WC)
    ws_sig_all = B.emit("sync", lambda e: e.dma_start(
        out=bass.AP(WS, 0, [[pWS, 128], [1024, 8], [1, 1024]]),
        in_=bass.AP(wag_out, 0, [[3072, 128], [128 * 3072, 8], [1, 1024]])),
        inc=("SQ", 16))
    fence("sync", "SQ")
    wb_sig_all = B.emit("sync", lambda e: e.dma_start(
        out=bass.AP(WB, 0, [[pWB, 128], [1024, 8], [1, 1024]]),
        in_=bass.AP(wag_out, 1024, [[3072, 128], [128 * 3072, 8], [1, 1024]])),
        inc=("SQ", 16))
    wc_sig_all = B.emit("sync", lambda e: e.dma_start(
        out=bass.AP(WC, 0, [[pWC, 128], [1024, 8], [1, 1024]]),
        in_=bass.AP(wag_out, 2048, [[3072, 128], [128 * 3072, 8], [1, 1024]])),
        inc=("SQB", 16))
    fence("sync", "SQ")
    fence("sync", "SQB")
    B.wait("scalar", tiny_sig)
    prev_v = None
    for i, dst in enumerate((None, None, BB, BC4)):
        B.wait("scalar", prev_v)
        s1 = B.emit("scalar", lambda e, i=i: e.activation(
            out=BT[:, 0:8], in_=BRAW[:, (3 * i + 1) * 8:(3 * i + 2) * 8],
            func=AF.Exp, scale=0.5), inc=("SC", 1))
        B.wait("vector", s1)
        tgt = BT[:, 8:16] if i == 0 else (BT[:, 16:24] if i == 1 else dst[:, :])
        v1 = B.emit("vector", lambda e, i=i, tgt=tgt: e.tensor_mul(
            tgt, BT[:, 0:8], BRAW[:, (3 * i + 2) * 8:(3 * i + 3) * 8]),
            inc=("VE", 1))
        B.wait("vector", v1)
        v2 = B.emit("vector", lambda e, i=i, tgt=tgt: e.tensor_add(
            tgt, tgt, BRAW[:, (3 * i) * 8:(3 * i + 1) * 8]), inc=("VE", 1))
        B.wait("vector", v2)
        prev_v = v2
    v = B.emit("vector", lambda e: e.tensor_add(
        BS[:, :], BT[:, 8:16], BT[:, 16:24]), inc=("VE", 1))
    B.wait("vector", v)
    bias_done = B.emit("vector", lambda e: e.tensor_scalar(
        out=BC4[:, :], in0=BC4[:, :], scalar1=4.0, scalar2=None,
        op0=OP.mult), inc=("VE", 1))

    ws_sig = {0: ws_sig_all, 1: ws_sig_all}
    wb_sig_h = {0: wb_sig_all, 1: wb_sig_all}
    wc_sig_h = {0: wc_sig_all, 1: wc_sig_all}
    prev_prep = [sig]

    head_state = {}

    def emit_head(u):
        m, h = u // 2, u % 2
        sl = u % 3
        xsl = u % 2
        bsl = u % 4
        st = {}
        for q2 in range(TBH):
            bi, pb = pool.acquire()
            for j in range(4):
                tc = (TC // 2) * h + 4 * q2 + j
                fn = lambda e, tc=tc, m=m, pb=pb, j=j: e.transpose(
                    pb[:, j * 128:(j + 1) * 128],
                    RT[:, tc * 1024 + m * 128: tc * 1024 + m * 128 + 128],
                    EYE[:, :])
                if j == 3:
                    tsig = B.emit("tensor", fn, inc=("TE", 1))
                else:
                    B.emit("tensor", fn)
            B.wait("scalar", tsig)
            if u == 0 and q2 == 0:
                B.wait("scalar", wbc_con)
            if u >= 1:
                B.wait("scalar", head_state.get(("xcol_reader", u - 1)))
            xsig = B.emit("scalar", lambda e, q2=q2, pb=pb: e.copy(
                out=XCOL[:, q2 * 512:(q2 + 1) * 512],
                in_=pb[:, :]), inc=("SC", 1))
            pool.release(bi, xsig)
            st[("xcol", q2)] = xsig
        if u == 0:
            B.wait("tensor", xb_done)
        if m % 4 == 0 and h == 0:
            B.wait("tensor", ws_sig[m // 4])
        sbanks = [pool.acquire() for _ in range(TBH)]
        for k in range(KD):
            for tb in range(TBH):
                fn = lambda e, k=k, m=m, h=h, tb=tb, pb=sbanks[tb][1]: e.matmul(
                    pb[:, :],
                    WS[:, k * 1024 + m * 128: k * 1024 + m * 128 + 128],
                    XB[:, k * T_CORE + h * HALF + tb * 512:
                       k * T_CORE + h * HALF + (tb + 1) * 512],
                    start=(k == 0), stop=(k == KD - 1))
                if k == KD - 1:
                    st[("smm", tb)] = B.emit("tensor", fn, inc=("TE", 1))
                else:
                    B.emit("tensor", fn)
        for tb in range(TBH):
            bi, pb = sbanks[tb]
            B.wait("vector", st[("smm", tb)])
            if u == 0 and tb == 0:
                B.wait("vector", bias_done)
            if u >= 3:
                B.wait("vector", head_state.get(("ebuf_reader", u - 3)))
            vc = B.emit("vector", lambda e, pb=pb, m=m, sl=sl, tb=tb: e.tensor_scalar(
                out=EBUF[:, sl * HALF + tb * 512: sl * HALF + (tb + 1) * 512],
                in0=pb[:, :], scalar1=BS[:, m:m + 1], scalar2=85.0,
                op0=OP.add, op1=OP.min), inc=("VE", 1))
            pool.release(bi, vc)
            st[("e1", tb)] = vc
        for tb in range(TBH):
            B.wait("scalar", st[("e1", tb)])
            s1 = B.emit("scalar", lambda e, sl=sl, tb=tb: e.activation(
                out=EBUF[:, sl * HALF + tb * 512: sl * HALF + (tb + 1) * 512],
                in_=EBUF[:, sl * HALF + tb * 512: sl * HALF + (tb + 1) * 512],
                func=AF.Exp), inc=("SC", 1))
            st[("e1b", tb)] = s1
        B.wait("scalar", st[("e1b", TBH - 1)])
        for tb in range(TBH):
            s2 = B.emit("scalar", lambda e, sl=sl, tb=tb: e.activation(
                out=EBUF[:, sl * HALF + tb * 512: sl * HALF + (tb + 1) * 512],
                in_=EBUF[:, sl * HALF + tb * 512: sl * HALF + (tb + 1) * 512],
                func=AF.Exp, scale=-1.0), inc=("SC", 1))
        st["E"] = s2
        if m % 4 == 0 and h == 0:
            B.wait("tensor", wb_sig_h[m // 4])
        bbanks = [pool.acquire() for _ in range(TBH)]
        bsig = {}
        for k in range(KD):
            for tb in range(TBH):
                fn = lambda e, k=k, m=m, h=h, tb=tb, pb=bbanks[tb][1]: e.matmul(
                    pb[:, :],
                    WB[:, k * 1024 + m * 128: k * 1024 + m * 128 + 128],
                    XB[:, k * T_CORE + h * HALF + tb * 512:
                       k * T_CORE + h * HALF + (tb + 1) * 512],
                    start=(k == 0), stop=(k == KD - 1))
                if k == KD - 1:
                    bsig[tb] = B.emit("tensor", fn, inc=("TE", 1))
                else:
                    B.emit("tensor", fn)
        for tb in range(TBH):
            bi, pb = bbanks[tb]
            B.wait("vector", bsig[tb])
            B.wait("vector", st[("xcol", tb)])
            if u >= 4:
                B.wait("vector", head_state.get(("bx_reader", u - 4)))
            vsig = B.emit("vector", lambda e, pb=pb, m=m, bsl=bsl, tb=tb: e.scalar_tensor_tensor(
                out=BXBUF[:, bsl * HALF + tb * 512: bsl * HALF + (tb + 1) * 512],
                in0=pb[:, :], scalar=BB[:, m:m + 1],
                in1=XCOL[:, tb * 512:(tb + 1) * 512],
                op0=OP.add, op1=OP.mult), inc=("VE", 1))
            pool.release(bi, vsig)
        head_state[("xcol_reader", u)] = vsig
        B.wait("vector", st["E"])
        B.wait("vector", vsig)
        if h == 0:
            init = 0.0
        else:
            B.wait("vector", head_state[u - 1]["scan1"])
            init = bass.AP(HBUF, ((u - 1) % 3) * HALF + HALF - 1, [[pH, 128], [1, 1]])
        s1sig = B.emit("vector", lambda e, sl=sl, bsl=bsl, init=init: e.tensor_tensor_scan(
            out=HBUF[:, sl * HALF:(sl + 1) * HALF],
            data0=EBUF[:, sl * HALF:(sl + 1) * HALF],
            data1=BXBUF[:, bsl * HALF:(bsl + 1) * HALF],
            initial=init, op0=OP.mult, op1=OP.add), inc=("VE", 1))
        st["scan1"] = s1sig
        if h == 1:
            B.wait("vector", s1sig)
            bc = bass.AP(HBUF, sl * HALF + HALF - 1, [[pH, 128], [0, 8]])
            ctsig = B.emit("vector", lambda e, m=m, bc=bc: e.tensor_tensor(
                out=CTB[:, m * 8:(m + 1) * 8], in0=SM[:, :], in1=bc,
                op=OP.mult), inc=("VE", 1))
            B.wait("gpsimd", ctsig)
            d1 = B.emit("gpsimd", lambda e, m=m: e.dma_start(
                out=cc_in[m].ap(), in_=CTB[:, m * 8:(m + 1) * 8]), inc=("GQ", 16))
            B.wait("gpsimd", d1)
            ccsig = B.emit("gpsimd", lambda e, m=m: e.collective_compute(
                "AllReduce", OP.add,
                replica_groups=[list(range(NCORES))],
                ins=[cc_in[m].ap().opt()],
                outs=[cc_out[m].ap().opt()]), inc=("CC", 1))
            B.wait("gpsimd", ccsig)
            arsig = B.emit("gpsimd", lambda e, m=m: e.dma_start(
                out=ARB[:, m * 8:(m + 1) * 8], in_=cc_out[m].ap()), inc=("GQA", 16))
            st["ar"] = arsig
        head_state[u] = st
        return st

    def emit_tail_a(u):
        m, h = u // 2, u % 2
        sl = u % 3
        bsl = u % 4
        if h == 0:
            arsig = head_state[2 * m + 1]["ar"]
            B.wait("vector", arsig)
            z1 = B.emit("vector", lambda e, m=m: e.tensor_mul(
                ZZ[:, :], ARB[:, m * 8:(m + 1) * 8], SR[:, :]), inc=("VE", 1))
            B.wait("vector", z1)
            z2 = B.emit("vector", lambda e, m=m: e.reduce_sum(
                out=INI[:, m:m + 1], in_=ZZ[:, :], axis=AX), inc=("VE", 1))
            B.wait("vector", z2)
            init = bass.AP(INI, m, [[_pstep(INI), 128], [1, 1]])
        else:
            B.wait("vector", head_state[("hcc", m)])
            init = bass.AP(HCC, 0, [[_pstep(HCC), 128], [1, 1]])
        s2sig = B.emit("vector", lambda e, sl=sl, bsl=bsl, init=init: e.tensor_tensor_scan(
            out=HBUF[:, sl * HALF:(sl + 1) * HALF],
            data0=EBUF[:, sl * HALF:(sl + 1) * HALF],
            data1=BXBUF[:, bsl * HALF:(bsl + 1) * HALF],
            initial=init, op0=OP.mult, op1=OP.add), inc=("VE", 1))
        head_state[("ebuf_reader", u)] = s2sig
        if h == 0:
            B.wait("vector", s2sig)
            hcsig = B.emit("vector", lambda e, sl=sl: e.tensor_copy(
                out=HCC[:, 0:1],
                in_=HBUF[:, sl * HALF + HALF - 1: sl * HALF + HALF]), inc=("VE", 1))
            head_state[("hcc", m)] = hcsig
        if m % 4 == 0 and h == 0:
            B.wait("tensor", wc_sig_h[m // 4])
        cbanks = [pool.acquire() for _ in range(TBH)]
        csig = {}
        for k in range(KD):
            for tb in range(TBH):
                fn = lambda e, k=k, m=m, h=h, tb=tb, pb=cbanks[tb][1]: e.matmul(
                    pb[:, :],
                    WC[:, k * 1024 + m * 128: k * 1024 + m * 128 + 128],
                    XB[:, k * T_CORE + h * HALF + tb * 512:
                       k * T_CORE + h * HALF + (tb + 1) * 512],
                    start=(k == 0), stop=(k == KD - 1))
                if k == KD - 1:
                    csig[tb] = B.emit("tensor", fn, inc=("TE", 1))
                else:
                    B.emit("tensor", fn)
        head_state[("cmm", u)] = csig[TBH - 1]
        ysigs = []
        for tb in range(TBH):
            bi, pb = cbanks[tb]
            B.wait("vector", csig[tb])
            B.wait("vector", s2sig)
            ysig = B.emit("vector", lambda e, pb=pb, m=m, sl=sl, bsl=bsl, tb=tb: e.scalar_tensor_tensor(
                out=BXBUF[:, bsl * HALF + tb * 512: bsl * HALF + (tb + 1) * 512],
                in0=pb[:, :], scalar=BC4[:, m:m + 1],
                in1=HBUF[:, sl * HALF + tb * 512: sl * HALF + (tb + 1) * 512],
                op0=OP.add, op1=OP.mult), inc=("VE", 1))
            pool.release(bi, ysig)
            ysigs.append(ysig)
        head_state[("ysigs", u)] = ysigs

    def emit_tail_b(u):
        m, h = u // 2, u % 2
        bsl = u % 4
        ysigs = head_state[("ysigs", u)]
        for q2 in range(TBH):
            B.wait("tensor", ysigs[q2])
            bi, pb = pool.acquire()
            for j in range(4):
                fn = lambda e, bsl=bsl, q2=q2, j=j, pb=pb: e.transpose(
                    pb[:, j * 128:(j + 1) * 128],
                    BXBUF[:, bsl * HALF + q2 * 512 + j * 128:
                          bsl * HALF + q2 * 512 + (j + 1) * 128],
                    EYE[:, :])
                if j == 3:
                    tsig = B.emit("tensor", fn, inc=("TE", 1))
                else:
                    B.emit("tensor", fn)
            B.wait("vector", tsig)
            tc0 = (TC // 2) * h + 4 * q2
            rt_ap = bass.AP(RT, tc0 * 1024 + m * 128, [[pR, 128], [1024, 4], [1, 128]])
            pb_ap = bass.AP(pb, 0, [[_pstep(pb), 128], [128, 4], [1, 128]])
            rsig = B.emit("vector", lambda e, rt_ap=rt_ap, pb_ap=pb_ap: e.scalar_tensor_tensor(
                out=rt_ap, in0=pb_ap, scalar=1.0, in1=rt_ap,
                op0=OP.mult, op1=OP.add), inc=("VE", 1))
            pool.release(bi, rsig)
        head_state[("bx_reader", u)] = rsig
        head_state[("resid", u)] = rsig

    for s in range(NU + 3):
        if s < NU:
            emit_head(s)
        if 2 <= s < NU + 2:
            emit_tail_a(s - 2)
        if s >= 3:
            emit_tail_b(s - 3)

    B.wait("vector", head_state[("resid", NU - 1)])
    B.wait("vector", head_state[("xcol_reader", NU - 1)])
    pX = _pstep(XCOL)
    ln1_sigs = {}
    spill_sigs = []
    x1t_done = None
    for g in range(TC // 4):
        t0 = 4 * g
        for j in range(4):
            tc = t0 + j
            B.emit("vector", lambda e, tc=tc: e.bn_stats(
                out=XCOL[:, tc * 12: tc * 12 + 6],
                in_=RT[:, tc * 1024: tc * 1024 + 512]))
            s2 = B.emit("vector", lambda e, tc=tc: e.bn_stats(
                out=XCOL[:, tc * 12 + 6: tc * 12 + 12],
                in_=RT[:, tc * 1024 + 512: tc * 1024 + 1024]),
                inc=("VE", 1) if j == 3 else None)
        B.wait("vector", s2)
        for j in range(4):
            tc = t0 + j
            bnsig = B.emit("vector", lambda e, tc=tc: e.bn_aggr(
                out=XCOL[:, 192 + 2 * tc: 192 + 2 * tc + 2],
                in_=XCOL[:, tc * 12: tc * 12 + 12]),
                inc=("VE", 1) if j == 3 else None)
        B.wait("vector", bnsig)
        mean_g = bass.AP(XCOL, 192 + 8 * g, [[pX, 128], [2, 4]])
        var_g = bass.AP(XCOL, 193 + 8 * g, [[pX, 128], [2, 4]])
        ng = B.emit("vector", lambda e, mean_g=mean_g, t0=t0: e.tensor_scalar(
            out=NEGM[:, t0:t0 + 4], in0=mean_g, scalar1=-1.0, scalar2=None,
            op0=OP.mult), inc=("VE", 1))
        vg = B.emit("vector", lambda e, var_g=var_g, t0=t0: e.tensor_scalar(
            out=VARP[:, t0:t0 + 4], in0=var_g, scalar1=1.0, scalar2=LN_EPS,
            op0=OP.mult, op1=OP.add), inc=("VE", 1))
        B.wait("scalar", vg)
        sq1 = B.emit("scalar", lambda e, t0=t0: e.activation(
            out=RSTD[:, t0:t0 + 4], in_=VARP[:, t0:t0 + 4], func=AF.Sqrt),
            inc=("SC", 1))
        B.wait("vector", sq1)
        v2 = B.emit("vector", lambda e, t0=t0: e.reciprocal(
            out=RSTD[:, t0:t0 + 4], in_=RSTD[:, t0:t0 + 4]), inc=("VE", 1))
        B.wait("vector", v2)
        B.wait("vector", ng)
        v3 = B.emit("vector", lambda e, t0=t0: e.tensor_mul(
            NMRS[:, t0:t0 + 4], NEGM[:, t0:t0 + 4], RSTD[:, t0:t0 + 4]),
            inc=("VE", 1))
        B.wait("scalar", v3)
        for j in range(4):
            tc = t0 + j
            s = B.emit("scalar", lambda e, tc=tc: e.activation(
                out=RT[:, tc * 1024:(tc + 1) * 1024],
                in_=RT[:, tc * 1024:(tc + 1) * 1024],
                func=AF.Identity, scale=RSTD[:, tc:tc + 1],
                bias=NMRS[:, tc:tc + 1]), inc=("SC", 1))
            ln1_sigs[tc] = s
            B.wait("gpsimd", s)
            sp = B.emit("gpsimd", lambda e, tc=tc: e.dma_start(
                out=x1sp.ap()[tc * 128:(tc + 1) * 128, :],
                in_=RT[:, tc * 1024:(tc + 1) * 1024]), inc=("GQ", 16))
            spill_sigs.append(sp)
        for k in range(KD):
            bi, pb = pool.acquire()
            B.wait("tensor", ln1_sigs[t0 + 3])
            for j in range(4):
                tc = t0 + j
                fn = lambda e, tc=tc, k=k, pb=pb, j=j: e.transpose(
                    pb[:, j * 128:(j + 1) * 128],
                    RT[:, tc * 1024 + k * 128: tc * 1024 + k * 128 + 128],
                    EYE[:, :])
                if j == 3:
                    tsig = B.emit("tensor", fn, inc=("TE", 1))
                else:
                    B.emit("tensor", fn)
            B.wait("scalar", tsig)
            esig = B.emit("scalar", lambda e, k=k, g=g, pb=pb: e.copy(
                out=XB[:, k * T_CORE + g * 512: k * T_CORE + (g + 1) * 512],
                in_=pb[:, :]), inc=("SC", 1))
            pool.release(bi, esig)
            x1t_done = esig
    ln1_sigs = [ln1_sigs[tc] for tc in range(TC)]
    v3 = x1t_done

    stackAB.close()

    W1ST = en(nc.sbuf_tensor("W1ST", [128, 2 * 1024], F32))
    W1B = en(nc.sbuf_tensor("W1B", [128, 2 * KD * 1024], BF16))
    W2ST = en(nc.sbuf_tensor("W2ST", [128, 2 * 1024], F32))
    W2B = en(nc.sbuf_tensor("W2B", [128, ICN * 1024], BF16))
    HT1 = en(nc.sbuf_tensor("HT1", [128, 2 * T_CORE], BF16))
    HRD = en(nc.sbuf_tensor("HRD", [128, 3 * ICN * 128], BF16))
    X1RD = en(nc.sbuf_tensor("X1RD", [128, 3 * 1024], F32))
    R2 = en(nc.sbuf_tensor("R2", [128, 2 * 1024], F32))
    SQ2 = en(nc.sbuf_tensor("SQ2", [128, 1024], F32))
    B2B = en(nc.sbuf_tensor("B2B", [1, D], BF16))
    ONES1 = en(nc.sbuf_tensor("ONES1", [1, 128], BF16))
    S2H = en(nc.sbuf_tensor("S2H", [128, 2 * TC], F32))
    NEGM2 = en(nc.sbuf_tensor("NEGM2", [128, TC], F32))
    VARP2 = en(nc.sbuf_tensor("VARP2", [128, TC], F32))
    RSTD2 = en(nc.sbuf_tensor("RSTD2", [128, TC], F32))
    NMRS2 = en(nc.sbuf_tensor("NMRS2", [128, TC], F32))

    gate_c1 = head_state[("cmm", NU - 1)]
    gate_sigs = [x1t_done, spill_sigs[-1], head_state[("resid", NU - 1)],
                 ln1_sigs[-1], v3]
    B.wait("sync", gate_c1)
    b2sig = B.emit("sync", lambda e: e.dma_start(out=W2ST[0:1, 0:1024],
                                                 in_=b2_d[:, :]), inc=("SQ", 16))
    fence("sync", "SQ")
    B.wait("scalar", b2sig)
    b2b = B.emit("scalar", lambda e: e.copy(out=B2B[:, :], in_=W2ST[0:1, 0:1024]),
                 inc=("SC", 1))
    for g in gate_sigs:
        B.wait("vector", g)
    ones_sig = B.emit("vector", lambda e: e.memset(ONES1[:, :], 1.0), inc=("VE", 1))
    for g in gate_sigs:
        B.wait("scalar", g)

    w1_dma = {}
    w2_dma = {}
    w1b_sig = {}
    w2b_sig = {}
    w1b_reader = {}
    h_spill = {}
    ffn_fence = {0: None, 1: None, 2: None, 3: None}
    w2_fence = {0: None, 1: None}
    TBF = T_CORE // 512
    NQ = ICN // 8

    def w2_group(j):
        slot = j % 2
        sem = ("W2Q", "W2QB")[slot]
        if w2_fence[slot] is not None:
            B.wait("scalar", w2_fence[slot])
        sig = B.emit("scalar", lambda e, j=j, slot=slot: e.dma_start(
            out=W2ST[:, slot * 1024:slot * 1024 + 1024],
            in_=w2_d[j * 128:(j + 1) * 128, :]), inc=(sem, 16))
        w2_fence[slot] = sig
        return sig

    w1q_sig = {}
    w1q_reader = {}
    qconv = {}

    w1conv = {}

    def w1_quarter(q):
        qsl = q % 2
        sigs = []
        for k in range(KD):
            g = q * KD + k
            slot = g % 2
            sem = ("FQ", "FQB", "FQC", "FQD")[g % 4]
            if ffn_fence[g % 4] is not None:
                B.wait("sync", ffn_fence[g % 4])
            if g >= 2:
                B.wait("sync", w1conv.get(g - 2))
            dsig = B.emit("sync", lambda e, k=k, q=q, slot=slot: e.dma_start(
                out=W1ST[:, slot * 1024:(slot + 1) * 1024],
                in_=w1_d[k * 128:(k + 1) * 128, q * 1024:(q + 1) * 1024]),
                inc=(sem, 16))
            ffn_fence[g % 4] = dsig
            B.wait("vector", dsig)
            if (q, k) in qconv_war:
                B.wait("vector", qconv_war[(q, k)])
            csig = B.emit("vector", lambda e, k=k, qsl=qsl, slot=slot: e.tensor_copy(
                out=W1B[:, qsl * KD * 1024 + k * 1024: qsl * KD * 1024 + (k + 1) * 1024],
                in_=W1ST[:, slot * 1024:(slot + 1) * 1024]), inc=("VE", 1))
            w1conv[g] = csig
            sigs.append(csig)
        return sigs[-1]

    qconv_war = {}
    w2c_j = [0]
    h_spill = {}
    w1q_sig[0] = w1_quarter(0)
    for ic in range(ICN):
        q = ic // 8
        qsl = q % 2
        icr = ic % 8
        if icr == 0 and q + 1 < NQ:
            if q >= 1:
                B.wait("sync", w1q_reader[q - 1])
                for k in range(KD):
                    qconv_war[(q + 1, k)] = w1q_reader[q - 1]
            w1q_sig[q + 1] = w1_quarter(q + 1)
        if ic < ICN:
            j = ic
            if j == 0:
                B.wait("scalar", b2b)
            if j >= 2:
                B.wait("scalar", w2b_sig[j - 2])
            w2_dma[j] = w2_group(j)
        if w2c_j[0] <= ic:
            j = w2c_j[0]
            B.wait("vector", w2_dma[j])
            w2b_sig[j] = B.emit("vector", lambda e, j=j: e.tensor_copy(
                out=W2B[:, j * 1024:(j + 1) * 1024],
                in_=W2ST[:, (j % 2) * 1024:(j % 2) * 1024 + 1024]), inc=("VE", 1))
            w2c_j[0] += 1
        par = ic % 2
        fbanks = [pool.acquire() for _ in range(TBF)]
        if icr == 0:
            B.wait("tensor", w1q_sig[q])
        fsig = {}
        for k in range(KD):
            for tb in range(TBF):
                fn = lambda e, k=k, qsl=qsl, icr=icr, tb=tb, pb=fbanks[tb][1]: e.matmul(
                    pb[:, :],
                    W1B[:, qsl * KD * 1024 + k * 1024 + icr * 128:
                        qsl * KD * 1024 + k * 1024 + (icr + 1) * 128],
                    XB[:, k * T_CORE + tb * 512: k * T_CORE + (tb + 1) * 512],
                    start=(k == 0), stop=(k == KD - 1))
                if k == KD - 1:
                    fsig[tb] = B.emit("tensor", fn, inc=("TE", 1))
                else:
                    B.emit("tensor", fn)
        if icr == 7:
            w1q_reader[q] = fsig[TBF - 1]
        for tb in range(TBF):
            bi, pb = fbanks[tb]
            B.wait("scalar", fsig[tb])
            if ic >= 2 and tb == 0:
                B.wait("scalar", h_spill.get(ic - 2))
            rsig = B.emit("scalar", lambda e, pb=pb, ic=ic, par=par, tb=tb: e.activation(
                out=HT1[:, par * T_CORE + tb * 512: par * T_CORE + (tb + 1) * 512],
                in_=pb[:, :], func=AF.Relu, bias=B1T[:, ic:ic + 1], scale=1.0),
                inc=("SC", 1))
            pool.release(bi, rsig)
        B.wait("gpsimd", rsig)
        hsp_ap = bass.AP(hsp, ic * 128 * T_CORE, [[T_CORE, 128], [1, T_CORE]])
        h_spill[ic] = B.emit("gpsimd", lambda e, hsp_ap=hsp_ap, par=par: e.dma_start(
            out=hsp_ap,
            in_=HT1[:, par * T_CORE:(par + 1) * T_CORE]), inc=("GQ", 16))
        fence("gpsimd", "GQ")

    rb_sig = {}
    rb_reader = {}
    rb_fence = {0: None, 1: None}
    out_sig = {}
    p_hsp = 128 * T_CORE

    def issue_rb(tc):
        slot3 = tc % 3
        sem = "RQ" if tc % 2 == 0 else "RQB"
        if tc == 0:
            B.wait("sync", h_spill[ICN - 1])
        if rb_fence[tc % 2] is not None:
            B.wait("sync", rb_fence[tc % 2])
        if tc >= 3:
            B.wait("sync", rb_reader[tc - 3])
        src_ap = bass.AP(hsp, tc * 128, [[T_CORE, 128], [p_hsp, ICN], [1, 128]])
        dst_ap = bass.AP(HRD, slot3 * ICN * 128,
                         [[_pstep(HRD), 128], [128, ICN], [1, 128]])
        B.emit("sync", lambda e, s=src_ap, d=dst_ap: e.dma_start(out=d, in_=s),
               inc=(sem, 16))
        sig = B.emit("sync", lambda e, tc=tc, slot3=slot3: e.dma_start(
            out=X1RD[:, slot3 * 1024:(slot3 + 1) * 1024],
            in_=x1sp.ap()[tc * 128:(tc + 1) * 128, :]), inc=(sem, 16))
        rb_fence[tc % 2] = sig
        rb_sig[tc] = sig

    issue_rb(0)
    issue_rb(1)
    for tc in range(TC):
        slot3 = tc % 3
        par = tc % 2
        if tc + 2 < TC:
            issue_rb(tc + 2)
        for halfc in range(2):
            bi, pb = pool.acquire()
            B.wait("tensor", rb_sig[tc])
            if tc == 0 and halfc == 0:
                B.wait("tensor", w2b_sig[ICN - 1])
                B.wait("tensor", b2b)
                B.wait("tensor", ones_sig)
            for ic in range(ICN):
                fn = lambda e, ic=ic, slot3=slot3, halfc=halfc, pb=pb: e.matmul(
                    pb[:, :],
                    HRD[:, slot3 * ICN * 128 + ic * 128: slot3 * ICN * 128 + (ic + 1) * 128],
                    W2B[:, ic * 1024 + halfc * 512: ic * 1024 + (halfc + 1) * 512],
                    start=(ic == 0), stop=False)
                B.emit("tensor", fn)
            tsig = B.emit("tensor", lambda e, halfc=halfc, pb=pb: e.matmul(
                pb[:, :], ONES1[:, :],
                B2B[:, halfc * 512:(halfc + 1) * 512],
                start=False, stop=True), inc=("TE", 1))
            B.wait("vector", tsig)
            B.wait("vector", rb_sig[tc])
            if tc >= 2 and halfc == 0:
                B.wait("vector", out_sig.get(tc - 2))
            vsig = B.emit("vector", lambda e, pb=pb, par=par, slot3=slot3, halfc=halfc, tc=tc:
                          e.scalar_tensor_tensor(
                              out=R2[:, par * 1024 + halfc * 512:
                                     par * 1024 + (halfc + 1) * 512],
                              in0=pb[:, :], scalar=1.0,
                              in1=X1RD[:, slot3 * 1024 + halfc * 512:
                                       slot3 * 1024 + (halfc + 1) * 512],
                              op0=OP.mult, op1=OP.add,
                              accum_out=S2H[:, 2 * tc + halfc: 2 * tc + halfc + 1]),
                          inc=("VE", 1))
            pool.release(bi, vsig)
        rb_reader[tc] = vsig
        B.wait("vector", vsig)
        n1 = B.emit("vector", lambda e, tc=tc: e.tensor_add(
            NEGM2[:, tc:tc + 1], S2H[:, 2 * tc:2 * tc + 1],
            S2H[:, 2 * tc + 1:2 * tc + 2]), inc=("VE", 1))
        B.wait("vector", n1)
        n2 = B.emit("vector", lambda e, tc=tc: e.tensor_scalar(
            out=NEGM2[:, tc:tc + 1], in0=NEGM2[:, tc:tc + 1],
            scalar1=-1.0 / D, scalar2=None, op0=OP.mult), inc=("VE", 1))
        B.wait("scalar", n2)
        sq = B.emit("scalar", lambda e, tc=tc, par=par: e.activation(
            out=SQ2[:, :], in_=R2[:, par * 1024:(par + 1) * 1024],
            func=AF.Square, bias=NEGM2[:, tc:tc + 1], scale=1.0,
            accum_out=VARP2[:, tc:tc + 1]), inc=("SC", 1))
        B.wait("vector", sq)
        n3 = B.emit("vector", lambda e, tc=tc: e.tensor_scalar(
            out=VARP2[:, tc:tc + 1], in0=VARP2[:, tc:tc + 1],
            scalar1=1.0 / D, scalar2=LN_EPS, op0=OP.mult, op1=OP.add), inc=("VE", 1))
        B.wait("scalar", n3)
        n4 = B.emit("scalar", lambda e, tc=tc: e.activation(
            out=RSTD2[:, tc:tc + 1], in_=VARP2[:, tc:tc + 1],
            func=AF.Sqrt), inc=("SC", 1))
        B.wait("vector", n4)
        n5 = B.emit("vector", lambda e, tc=tc: e.reciprocal(
            out=RSTD2[:, tc:tc + 1], in_=RSTD2[:, tc:tc + 1]), inc=("VE", 1))
        B.wait("vector", n5)
        n6 = B.emit("vector", lambda e, tc=tc: e.tensor_mul(
            NMRS2[:, tc:tc + 1], NEGM2[:, tc:tc + 1], RSTD2[:, tc:tc + 1]),
            inc=("VE", 1))
        B.wait("scalar", n6)
        so = B.emit("scalar", lambda e, tc=tc, par=par: e.activation(
            out=R2[:, par * 1024:(par + 1) * 1024],
            in_=R2[:, par * 1024:(par + 1) * 1024],
            func=AF.Identity, scale=RSTD2[:, tc:tc + 1],
            bias=NMRS2[:, tc:tc + 1]), inc=("SC", 1))
        B.wait("gpsimd", so)
        out_sig[tc] = B.emit("gpsimd", lambda e, tc=tc, par=par: e.dma_start(
            out=out_d[tc * 128:(tc + 1) * 128, :],
            in_=R2[:, par * 1024:(par + 1) * 1024]), inc=("GQ", 16))
        fence("gpsimd", "GQ")

    B.wait("gpsimd", out_sig[TC - 1])

    B.run()
    stack.close()
    return nc


_NC_CACHE = {}


def _get_nc(T_CORE=2048):
    if T_CORE not in _NC_CACHE:
        _NC_CACHE[T_CORE] = build_nc(T_CORE)
    return _NC_CACHE[T_CORE]


def make_in_maps(inputs, T_CORE=2048):
    import jax

    x = np.ascontiguousarray(np.asarray(inputs["L0_attn_x"], dtype=np.float32)
                             .reshape(-1, D))
    n_tok = x.shape[0]
    assert n_tok == NCORES * T_CORE

    cpu = jax.devices("cpu")[0]
    with jax.default_device(cpu):
        key = jax.random.key(42, impl="threefry2x32")
        kA, kB, kC, kdt = jax.random.split(key, 4)
        eps = {}
        for name, kk in (("A", kA), ("Bp", kB), ("C", kC), ("dt", kdt)):
            kw, kb = jax.random.split(kk)
            eps[(name, "w")] = np.asarray(jax.random.normal(kw, (D, D), "float32"))
            eps[(name, "b")] = np.asarray(jax.random.normal(kb, (D,), "float32"))

    import ml_dtypes
    common = {}
    percore = {c: {} for c in range(NCORES)}
    for P in ("A", "dt", "Bp", "C"):
        wmu = np.asarray(inputs[f"L0_attn_{P}_wmu"], dtype=np.float32)
        wlv = np.asarray(inputs[f"L0_attn_{P}_wlv"], dtype=np.float32)
        wmuT = np.ascontiguousarray(wmu.T)
        wlvT = np.ascontiguousarray(wlv.T)
        epsT = np.ascontiguousarray(eps[(P, "w")].T.astype(ml_dtypes.bfloat16))
        for c in range(NCORES):
            percore[c][f"{P}_wmuT_my"] = np.ascontiguousarray(
                wmuT[c * 128:(c + 1) * 128])
            percore[c][f"{P}_wlvT_my"] = np.ascontiguousarray(
                wlvT[c * 128:(c + 1) * 128])
            percore[c][f"{P}_epsT_my"] = np.ascontiguousarray(
                epsT[c * 128:(c + 1) * 128])
        bmu = np.asarray(inputs[f"L0_attn_{P}_bmu"], dtype=np.float32)
        blv = np.asarray(inputs[f"L0_attn_{P}_blv"], dtype=np.float32)
        common[f"{P}_bmu8"] = np.ascontiguousarray(bmu.reshape(8, 128).T)
        common[f"{P}_blv8"] = np.ascontiguousarray(blv.reshape(8, 128).T)
        common[f"{P}_epsb8"] = np.ascontiguousarray(eps[(P, "b")].reshape(8, 128).T)
    common["W1"] = np.ascontiguousarray(np.asarray(inputs["L0_ffn_W1"], np.float32))
    common["b1_32"] = np.ascontiguousarray(
        np.asarray(inputs["L0_ffn_b1"], np.float32).reshape(ICN, 128).T)
    common["W2"] = np.ascontiguousarray(np.asarray(inputs["L0_ffn_W2"], np.float32))
    common["b2row"] = np.ascontiguousarray(
        np.asarray(inputs["L0_ffn_b2"], np.float32).reshape(1, D))
    common["eye"] = np.eye(128, dtype=np.float32)

    in_maps = []
    for c in range(NCORES):
        ms = np.zeros((128, 8), dtype=np.float32)
        ms[:, (c + 1) % 8] = 1.0
        rr = np.zeros((128, 8), dtype=np.float32)
        if c % 2 == 1:
            rr[:, c] = 1.0
        m = dict(common)
        m.update(percore[c])
        m["x"] = np.ascontiguousarray(x[c * T_CORE:(c + 1) * T_CORE])
        m["msend"] = ms
        m["rread"] = rr
        in_maps.append(m)
    return in_maps


def kernel(**inputs):
    T_CORE = 2048
    nc = _get_nc(T_CORE)
    in_maps = make_in_maps(inputs, T_CORE)
    res = run_bass_kernel_spmd(nc, in_maps, core_ids=list(range(NCORES)))
    out = np.concatenate([np.asarray(res.results[c]["out"]) for c in range(NCORES)],
                         axis=0)
    return out.reshape(4, 4096, 1024).astype(np.float32)
